# version 06
# baseline (speedup 1.0000x reference)
import math
import numpy as np
import ml_dtypes
import concourse.bass as bass
import concourse.mybir as mybir
from concourse.bass_utils import run_bass_kernel_spmd
from contextlib import ExitStack

F32 = mybir.dt.float32
BF16 = mybir.dt.bfloat16
U8 = mybir.dt.uint8
AF = mybir.ActivationFunctionType
ALU = mybir.AluOpType
AX = mybir.AxisListType

S = 2048
D = 2048
NH = 16
NT = 16
NG = 4
LAT = 512
NE = 7
MOE_I = 1024
EPS = 1e-5
SCALE = 1.0 / math.sqrt(128.0)
NEGINF = -1e30
ENGINES = ("tensor", "vector", "scalar", "sync", "gpsimd")
KB = 1024


NPOOL = 12


class Op:
    __slots__ = ("eng", "fn", "reads", "writes", "dma", "needs_inc", "sem_val",
                 "deps", "tag", "slot", "use")
    def __init__(self, eng, fn, reads, writes, dma, tag):
        self.eng = eng
        self.fn = fn
        self.reads = reads
        self.writes = writes
        self.dma = dma
        self.needs_inc = False
        self.sem_val = 0
        self.deps = ()
        self.tag = tag
        self.slot = -1
        self.use = 0


class Prog:
    def __init__(self):
        self.ops = []
        self.dma_n = {"sync": 0, "gpsimd": 0}

    def emit(self, eng, fn, reads=(), writes=(), dma=False, tag=""):
        op = Op(eng, fn, tuple(reads), tuple(writes), dma, tag)
        if dma:
            idx = self.dma_n[eng]
            self.dma_n[eng] = idx + 1
            op.slot = idx % NPOOL
            op.use = idx // NPOOL
        self.ops.append(op)
        return op

    def barrier(self):
        for e in ENGINES:
            self.emit(e, None, tag="BARRIER")

    def link(self):
        last_writer = {}
        readers = {}
        last_op = {e: None for e in ENGINES}
        last_dma = {}
        for op in self.ops:
            deps = set()
            if op.tag == "BARRIER":
                for e, o in last_op.items():
                    if o is not None and e != op.eng:
                        deps.add(o)
                for (q, s), o in last_dma.items():
                    deps.add(o)
            else:
                for k in op.reads:
                    w = last_writer.get(k)
                    if w is not None:
                        same = w.eng == op.eng and not w.dma and not op.dma
                        if not (same and op.eng == "tensor"):
                            deps.add(w)
                for k in op.writes:
                    w = last_writer.get(k)
                    if w is not None and not (w.eng == op.eng and not w.dma and not op.dma):
                        deps.add(w)
                    for r in readers.get(k, ()):
                        if not (r.eng == op.eng and not r.dma and not op.dma and op.eng == "tensor"):
                            deps.add(r)
                for k in op.reads:
                    readers.setdefault(k, []).append(op)
                for k in op.writes:
                    last_writer[k] = op
                    readers[k] = []
            op.deps = tuple(deps)
            for d in deps:
                d.needs_inc = True
            if op.dma:
                last_dma[(op.eng, op.slot)] = op
            else:
                last_op[op.eng] = op

    def assign(self):
        counts = {e: 0 for e in ENGINES}
        for op in self.ops:
            if op.dma:
                op.sem_val = 16 * (op.use + 1)
            elif op.needs_inc:
                counts[op.eng] += 1
                op.sem_val = counts[op.eng]
        return counts

    def prepare(self):
        self.by_eng = {e: [] for e in ENGINES}
        for op in self.ops:
            self.by_eng[op.eng].append(op)
        self.marks = {}

    def _wait(self, eng, engname, sem_key, sem_handle, val):
        mk = self.marks.setdefault((engname, sem_key), [0])
        if val > mk[0]:
            eng.wait_ge(sem_handle, val)
            mk[0] = val

    def mat_engine(self, engname, eng, sems, dsems):
        for op in self.by_eng[engname]:
            for d in op.deps:
                if d.dma:
                    self._wait(eng, engname, (d.eng, d.slot), dsems[d.eng][d.slot], d.sem_val)
                else:
                    self._wait(eng, engname, d.eng, sems[d.eng], d.sem_val)
            if op.dma and op.use > 0:
                self._wait(eng, engname, (engname, op.slot), dsems[engname][op.slot],
                           16 * op.use)
            if op.fn is not None:
                inst = op.fn(eng)
                if op.dma:
                    inst.then_inc(dsems[engname][op.slot], 16)
                elif op.needs_inc:
                    assert inst is not None, f"op {op.tag} needs inc, returned None"
                    inst.then_inc(sems[engname], 1)
            elif op.needs_inc:
                eng.nop().then_inc(sems[engname], 1)


def host_consts():
    c = {}
    c["IDN"] = np.eye(128, dtype=ml_dtypes.bfloat16)
    R1 = np.zeros((128, 128), dtype=np.float32)
    for m in range(64):
        R1[m + 64, m] = -1.0
        R1[m, m + 64] = 1.0
    c["R1M"] = R1.astype(ml_dtypes.bfloat16)
    RM = np.zeros((128, 128), dtype=np.float32)
    for h in range(2):
        o = h * 64
        for i in range(32):
            RM[o + i + 32, o + i] = -1.0
            RM[o + i, o + i + 32] = 1.0
    c["RMM"] = RM.astype(ml_dtypes.bfloat16)
    mk = np.zeros((128, 128), dtype=np.float32)
    mk[np.triu_indices(128, 1)] = NEGINF
    c["MASKC"] = mk
    t = np.arange(S, dtype=np.float64)
    w1 = 10000.0 ** (-(np.arange(64, dtype=np.float64)) / 64.0)
    ang1 = np.outer(np.concatenate([w1, w1]), t)
    c["COS1"] = np.cos(ang1).astype(np.float32)
    c["SIN1"] = np.sin(ang1).astype(np.float32)
    wM = 10000.0 ** (-(np.arange(32, dtype=np.float64)) / 32.0)
    angM = np.outer(np.concatenate([wM] * 4), t)
    c["COSM"] = np.cos(angM).astype(np.float32)
    c["SINM"] = np.sin(angM).astype(np.float32)
    sel = np.zeros((8, 8 * 128), dtype=np.float32)
    for e in range(8):
        sel[e, e * 128:(e + 1) * 128] = 0.125
    c["SEL8"] = sel
    c["IDNF"] = np.eye(128, dtype=np.float32)
    return c


def build_nc(taps=(), phases="ABCDEFG"):
    nc = bass.Bass("TRN2", target_bir_lowering=False)
    dram = {}

    def din(name, shape, dt=F32):
        dram[name] = nc.dram_tensor(name, shape, dt, kind="ExternalInput").ap()
        return dram[name]

    def dint(name, shape, dt):
        dram[name] = nc.dram_tensor(name, shape, dt, kind="Internal").ap()
        return dram[name]

    din("x0", [S, D])
    din("w_qkv", [D, 3 * D])
    din("w_o1", [D, D])
    din("ln1_w", [1, D])
    din("ln2_w", [1, D])
    din("kv_d_w", [D, LAT])
    din("q_d_w", [D, LAT])
    din("k_u_w", [LAT, 1024])
    din("v_u_w", [LAT, D])
    din("q_u_w", [LAT, 1024])
    din("rope_k_w", [D, 1024])
    din("rope_q_w", [LAT, 1024])
    din("o_w", [D, D])
    din("sh_gate", [D, MOE_I])
    din("sh_up", [D, MOE_I])
    din("sh_down", [MOE_I, D])
    din("r_gate", [NE, D, MOE_I])
    din("r_up", [NE, D, MOE_I])
    din("r_down", [NE, MOE_I, D])
    din("router_w", [D, NE])
    din("rb_pad", [1, 8])
    din("IDN", [128, 128], BF16)
    din("R1M", [128, 128], BF16)
    din("RMM", [128, 128], BF16)
    din("MASKC", [128, 128])
    din("COS1", [128, S])
    din("SIN1", [128, S])
    din("COSM", [128, S])
    din("SINM", [128, S])
    din("SEL8", [8, 8 * 128])
    din("IDNF", [128, 128])
    dint("sQ1", [NH, 128, S], BF16)
    dint("sK1", [NH, 128, S], BF16)
    dint("sV1", [NH, 128, NT, 128], BF16)
    dint("sQM", [NH, 128, S], BF16)
    dint("sKM", [NH, 128, S], BF16)
    dint("sVM", [NH, 128, NT, 128], BF16)
    dint("sX1", [NT, 128, D], F32)
    dint("sY", [NT, 128, D], F32)
    dint("sH", [NT, 128, D], F32)
    dint("sAT1", [128, NT, S], BF16)
    dint("sMF", [NG, 128, NT, 512], BF16)
    dint("sR1F", [128, NT, S], BF16)
    dint("sKVF", [128, 4, S], BF16)
    out = nc.dram_tensor("out", [S, D], F32, kind="ExternalOutput").ap()
    dram["out"] = out
    dram["chk"] = nc.dram_tensor("zzchk", [128, D], F32, kind="ExternalOutput").ap()

    tap_aps = {}
    for tp in taps:
        if tp in ("R1F", "KVF"):
            srcd = dram["s" + tp]
            tap_aps[tp] = nc.dram_tensor("tap_" + tp, list(srcd.shape), srcd.dtype,
                                         kind="ExternalOutput").ap()
            continue
        if tp == "XB":
            tap_aps[tp] = nc.dram_tensor("tap_XB", [128, NT, S], BF16,
                                         kind="ExternalOutput").ap()
        else:
            srcd = dram["s" + tp]
            tap_aps[tp] = nc.dram_tensor("tap_" + tp, list(srcd.shape), srcd.dtype,
                                         kind="ExternalOutput").ap()

    prog = Prog()
    P = prog.emit

    with ExitStack() as ctx:
        def sbuf(name, shape, dt):
            return ctx.enter_context(nc.sbuf_tensor(name, shape, dt))
        def psum(name, shape, dt):
            return ctx.enter_context(nc.psum_tensor(name, shape, dt))

        idn = sbuf("idn", [128, 128], BF16)
        idnf = sbuf("idnf", [128, 128], F32)
        r1m = sbuf("r1m", [128, 128], BF16)
        rmm = sbuf("rmm", [128, 128], BF16)
        maskc = sbuf("maskc", [128, 128], F32)
        sel8 = sbuf("sel8", [8, 8 * 128], F32)
        rbp = sbuf("rbp", [1, 8], F32)
        stt = sbuf("stt", [128, 64], F32)
        sumsM = sbuf("sumsM", [128, NT], F32)
        recM = sbuf("recM", [128, NT], F32)
        wT = sbuf("wT", [8, S], F32)
        probs8 = sbuf("probs8", [128, 8], F32)
        scr8 = sbuf("scr8", [128, 8], F32)
        scr8b = sbuf("scr8b", [128, 8], F32)
        cosb = sbuf("cosb", [128, S], F32)
        sinb = sbuf("sinb", [128, S], F32)

        ARENA_BYTES = 176 * KB
        arena = sbuf("arena", [128, ARENA_BYTES], U8)

        def carve(off_kb, shape, dt):
            nbytes = int(np.prod(shape[1:])) * (2 if dt == BF16 else 4)
            assert off_kb * KB + nbytes <= ARENA_BYTES, (off_kb, nbytes)
            ap = arena[:, off_kb * KB: off_kb * KB + nbytes].bitcast(dt)
            if len(shape) == 3:
                ap = ap.rearrange("p (a b) -> p a b", a=shape[1])
            elif len(shape) == 4:
                ap = ap.rearrange("p (a b c) -> p a b c", a=shape[1], b=shape[2])
            return ap

        ps_mm = psum("ps_mm", [128, 2048], F32)
        ps_mm2 = psum("ps_mm2", [128, 2048], F32)
        ptb = ps_mm2[:, 1024:2048].bitcast(BF16)

        sems = {e: ctx.enter_context(nc.semaphore("sem_" + e)) for e in ENGINES}
        dsems = {q: [ctx.enter_context(nc.semaphore(f"dsem_{q}{i}")) for i in range(NPOOL)]
                 for q in ("sync", "gpsimd")}

        def finish():
            prog.link()
            counts = prog.assign()
            prog.prepare()
            with nc.Block() as block:
                @block.tensor
                def _(t):
                    prog.mat_engine("tensor", t, sems, dsems)
                @block.vector
                def _(v):
                    prog.mat_engine("vector", v, sems, dsems)
                @block.scalar
                def _(s):
                    prog.mat_engine("scalar", s, sems, dsems)
                @block.sync
                def _(sy):
                    prog.mat_engine("sync", sy, sems, dsems)
                @block.gpsimd
                def _(g):
                    prog.mat_engine("gpsimd", g, sems, dsems)
            return counts

        def dma(queue, dst, src, reads, writes, tag=""):
            P(queue, lambda q, dst=dst, src=src: q.dma_start(out=dst, in_=src),
              reads=reads, writes=writes, dma=True, tag=tag)

        def load_weight_cast(dst_ap2, w_dram, row0, nrows, col0, ncols, key, tag, queue="gpsimd"):
            nk = nrows // 128
            for k in range(nk):
                src = w_dram[row0 + k * 128: row0 + (k + 1) * 128, col0: col0 + ncols]
                dst = dst_ap2[:, k * ncols: (k + 1) * ncols]
                dma(queue, dst, src, (), (key + (k,),), tag + f"k{k}")

        def ln_stats(src_ap, src_keys, sq_dump, sq_key, col, n, tag=""):
            sq_keys = sq_key if isinstance(sq_key, tuple) and sq_key and isinstance(sq_key[0], tuple) else (sq_key,)
            ssum = stt[:, col + 2: col + 3]
            ssq = stt[:, col + 3: col + 4]
            P("vector", lambda v: v.reduce_sum(ssum, src_ap, axis=AX.X),
              reads=src_keys, writes=(("stt", col, "sum"),), tag=tag + "sum")
            P("scalar", lambda s: s.activation(sq_dump, src_ap, AF.Square, bias=0.0,
                                               scale=1.0, accum_out=ssq),
              reads=src_keys + (("stt", col, "sum"),),
              writes=sq_keys + (("stt", col, "sq"),), tag=tag + "sq")
            P("vector", lambda v: v.tensor_scalar(stt[:, col:col + 1], ssum, 1.0 / n, None,
                                                  op0=ALU.mult),
              reads=(("stt", col, "sum"),), writes=(("stt", col, "mean"),), tag=tag + "mean")
            P("vector", lambda v: v.tensor_scalar(ssum, stt[:, col:col + 1],
                                                  stt[:, col:col + 1], None, op0=ALU.mult),
              reads=(("stt", col, "mean"),), writes=(("stt", col, "m2"),), tag=tag + "m2")
            P("vector", lambda v: v.tensor_scalar(ssq, ssq, 1.0 / n, None, op0=ALU.mult),
              reads=(("stt", col, "sq"),), writes=(("stt", col, "sqn"),), tag=tag + "sqn")
            P("vector", lambda v: v.tensor_tensor(ssq, ssq, ssum, op=ALU.subtract),
              reads=(("stt", col, "sqn"), ("stt", col, "m2")), writes=(("stt", col, "var"),),
              tag=tag + "var")
            P("vector", lambda v: v.tensor_scalar(ssq, ssq, EPS, None, op0=ALU.add),
              reads=(("stt", col, "var"),), writes=(("stt", col, "ve"),), tag=tag + "ve")
            P("scalar", lambda s: s.activation(ssum, ssq, AF.Sqrt, bias=0.0, scale=1.0),
              reads=(("stt", col, "ve"),), writes=(("stt", col, "sd"),), tag=tag + "sd")
            P("vector", lambda v: v.reciprocal(stt[:, col + 1:col + 2], ssum),
              reads=(("stt", col, "sd"),), writes=(("stt", col, "rstd"),), tag=tag + "rstd")

        def rms_stats(src_ap, src_keys, sq_dump, sq_key, col, n, tag=""):
            sq_keys = sq_key if isinstance(sq_key, tuple) and sq_key and isinstance(sq_key[0], tuple) else (sq_key,)
            ssum = stt[:, col + 2: col + 3]
            ssq = stt[:, col + 3: col + 4]
            P("scalar", lambda s: s.activation(sq_dump, src_ap, AF.Square, bias=0.0,
                                               scale=1.0, accum_out=ssq),
              reads=src_keys, writes=sq_keys + (("stt", col, "sq"),), tag=tag + "sq")
            P("vector", lambda v: v.tensor_scalar(ssq, ssq, 1.0 / n, EPS, op0=ALU.mult,
                                                  op1=ALU.add),
              reads=(("stt", col, "sq"),), writes=(("stt", col, "ve"),), tag=tag + "ve")
            P("scalar", lambda s: s.activation(ssum, ssq, AF.Sqrt, bias=0.0, scale=1.0),
              reads=(("stt", col, "ve"),), writes=(("stt", col, "sd"),), tag=tag + "sd")
            P("vector", lambda v: v.reciprocal(stt[:, col + 1:col + 2], ssum),
              reads=(("stt", col, "sd"),), writes=(("stt", col, "rstd"),), tag=tag + "rstd")

        class Chain(list):
            def add(self, eng, fn, reads=(), writes=(), tag="", dma=False):
                self.append((eng, fn, reads, writes, tag, dma))

        def emit_zip(*chains):
            idx = [0] * len(chains)
            while True:
                alive = False
                for ci, c in enumerate(chains):
                    if idx[ci] < len(c):
                        eng, fn, reads, writes, tag, dma_ = c[idx[ci]]
                        P(eng, fn, reads=reads, writes=writes, tag=tag, dma=dma_)
                        idx[ci] += 1
                        alive = True
                if not alive:
                    break

        dma("sync", idn[:], dram["IDN"], (), (("idn",),), "c")
        dma("sync", idnf[:], dram["IDNF"], (), (("idnf",),), "c")
        dma("sync", r1m[:], dram["R1M"], (), (("rot1",),), "c")
        dma("sync", rmm[:], dram["RMM"], (), (("rotM",),), "c")
        dma("sync", maskc[:], dram["MASKC"], (), (("maskc",),), "c")
        dma("sync", sel8[:], dram["SEL8"], (), (("sel8",),), "c")
        dma("sync", rbp[:], dram["rb_pad"], (), (("rbp",),), "c")
        dma("sync", cosb[:], dram["COS1"], (), (("cosb",),), "c")
        dma("sync", sinb[:], dram["SIN1"], (), (("sinb",),), "c")

        x0r = dram["x0"].rearrange("(t p) d -> t p d", p=128)

        ln0_fm = carve(0, [128, NT, S], BF16)
        sqsc = carve(64, [128, D], F32)
        x0t_a = [carve(72 + 8 * i, [128, D], F32) for i in range(2)]
        ln0bf = [carve(88 + 4 * i, [128, D], BF16) for i in range(4)]
        wblkQ = [carve(72 + 32 * i, [128, 16, 1024], BF16) for i in range(2)]
        qraw = [carve(136 + 1 * i, [128, 512], BF16) for i in range(2)]
        tmp1 = [carve(138 + 2 * i, [128, 512], F32) for i in range(2)]
        tmp2 = [carve(142 + 2 * i, [128, 512], F32) for i in range(2)]
        qout = [carve(146 + 1 * i, [128, 512], BF16) for i in range(2)]
        vout = [carve(148 + 1 * i, [128, 512], BF16) for i in range(2)]

        for g in range(NG):
            for ti in range(4):
                t = g * 4 + ti
                sl = t % 2
                xk = ("x0t", sl)
                dma("sync", x0t_a[sl], x0r[t], (), (xk,), f"A0x{t}")
                col = sl * 8
                ln_stats(x0t_a[sl], (xk,), sqsc, ("sqsc",), col, D, tag=f"A0s{t}")
                P("vector", lambda v, sl=sl, col=col, ti=ti: v.tensor_scalar(
                    ln0bf[ti], x0t_a[sl], stt[:, col:col + 1], stt[:, col + 1:col + 2],
                    op0=ALU.subtract, op1=ALU.mult),
                  reads=(xk, ("stt", col, "mean"), ("stt", col, "rstd")),
                  writes=(("ln0bf", ti),), tag=f"A0a{t}")
            for c in range(NT):
                bank = c % 2
                pk = ("ptA", bank)
                for ti in range(4):
                    P("tensor", lambda tn, c=c, ti=ti, bank=bank: tn.transpose(
                        ptb[:, bank * 1024 + ti * 128: bank * 1024 + (ti + 1) * 128],
                        ln0bf[ti][:, c * 128:(c + 1) * 128], idn[:]),
                      reads=(("ln0bf", ti), ("idn",)), writes=(pk,), tag=f"A0t{g}c{c}")
                P("vector", lambda v, c=c, g=g, bank=bank: v.tensor_copy(
                    ln0_fm[:, c, g * 512:(g + 1) * 512],
                    ptb[:, bank * 1024: bank * 1024 + 512]),
                  reads=(pk,), writes=(("ln0_fm", c, g),), tag=f"A0e{g}c{c}")

        ln0_all = tuple(("ln0_fm", c, g) for c in range(NT) for g in range(NG))
        prog.barrier()

        def rope_evac(uid, g, bank, rot_ap, rot_key, scr_name, head, tag):
            sl = uid % 2
            pk = ("psA", bank)
            qk = ("qraw", sl)
            P("scalar", lambda s, sl=sl, bank=bank: s.activation(
                qraw[sl], ps_mm[:, bank * 512:(bank + 1) * 512], AF.Identity,
                bias=0.0, scale=1.0),
              reads=(pk,), writes=(qk,), tag=tag + "ev")
            rk = ("psrot", sl)
            P("tensor", lambda tn, sl=sl, rot_ap=rot_ap: tn.matmul(
                ps_mm2[:, sl * 512:(sl + 1) * 512], rot_ap, qraw[sl], start=True, stop=True),
              reads=(qk, (rot_key,)), writes=(rk,), tag=tag + "rot")
            P("vector", lambda v, sl=sl, g=g: v.tensor_tensor(
                tmp1[sl], qraw[sl], cosb[:, g * 512:(g + 1) * 512], op=ALU.mult),
              reads=(qk, ("cosb",)), writes=(("tmp1", sl),), tag=tag + "m1")
            P("vector", lambda v, sl=sl, g=g: v.tensor_tensor(
                tmp2[sl], ps_mm2[:, sl * 512:(sl + 1) * 512],
                sinb[:, g * 512:(g + 1) * 512], op=ALU.mult),
              reads=(rk, ("sinb",)), writes=(("tmp2", sl),), tag=tag + "m2")
            P("vector", lambda v, sl=sl: v.tensor_tensor(
                qout[sl], tmp1[sl], tmp2[sl], op=ALU.add),
              reads=(("tmp1", sl), ("tmp2", sl)), writes=(("qout", sl),), tag=tag + "ad")
            dma("sync", dram[scr_name][head][:, g * 512:(g + 1) * 512], qout[sl],
                (("qout", sl),), ((scr_name, head, g),), tag + "st")

        uid = 0
        for j in range(4):
            sl = j % 2
            wk = ("wblkQ", sl)
            load_weight_cast(wblkQ[sl].rearrange("p a b -> p (a b)"), dram["w_qkv"],
                             0, D, j * 1024, 1024, wk, f"A1w{j}")
            for oc in range(8):
                col_glob = j * 1024 + oc * 128
                is_k = col_glob >= 2048
                head = (col_glob % 2048) // 128
                scr_name = "sK1" if is_k else "sQ1"
                for g in range(NG):
                    bank = uid % 4
                    pk = ("psA", bank)
                    for k in range(NT):
                        P("tensor", lambda tn, sl=sl, k=k, oc=oc, g=g, bank=bank,
                          first=(k == 0), last=(k == NT - 1): tn.matmul(
                            ps_mm[:, bank * 512:(bank + 1) * 512],
                            wblkQ[sl][:, k, oc * 128:(oc + 1) * 128],
                            ln0_fm[:, k, g * 512:(g + 1) * 512],
                            start=first, stop=last),
                          reads=(wk + (k,),) + (ln0_all if k == 0 else ()),
                          writes=(pk,), tag=f"A1m{j}o{oc}g{g}k{k}")
                    rope_evac(uid, g, bank, r1m[:], "rot1", scr_name, head,
                              f"A1r{j}o{oc}g{g}")
                    uid += 1

        for j in range(2):
            sl = j % 2
            wk = ("wblkQ", sl)
            load_weight_cast(wblkQ[sl].rearrange("p a b -> p (a b)"), dram["w_qkv"],
                             0, D, 4096 + j * 1024, 1024, wk, f"A2w{j}")
            for cg in range(2):
                for t in range(NT):
                    bank = (cg * NT + t) % 4
                    pk = ("psA", bank)
                    for k in range(NT):
                        P("tensor", lambda tn, sl=sl, k=k, t=t, cg=cg, bank=bank,
                          first=(k == 0), last=(k == NT - 1): tn.matmul(
                            ps_mm[:, bank * 512:(bank + 1) * 512],
                            ln0_fm[:, k, t * 128:(t + 1) * 128],
                            wblkQ[sl][:, k, cg * 512:(cg + 1) * 512],
                            start=first, stop=last),
                          reads=(wk + (k,),) + (ln0_all if k == 0 else ()),
                          writes=(pk,), tag=f"A2m{j}c{cg}t{t}k{k}")
                    vsl = (t + cg * NT) % 2
                    P("scalar", lambda s, bank=bank, vsl=vsl: s.activation(
                        vout[vsl], ps_mm[:, bank * 512:(bank + 1) * 512],
                        AF.Identity, bias=0.0, scale=1.0),
                      reads=(pk,), writes=(("vout", vsl),), tag=f"A2e{j}c{cg}t{t}")
                    h0 = (j * 1024 + cg * 512) // 128
                    dst = dram["sV1"][h0:h0 + 4, :, t, :].rearrange("h p f -> p h f")
                    src = vout[vsl].rearrange("p (h f) -> p h f", h=4)
                    dma("sync", dst, src, (("vout", vsl),), (("sV1", t, h0),),
                        f"A2s{j}c{cg}t{t}")

        prog.barrier()

        def attention_phase(scrQ, scrK, scrV, attn_fm, akey, with_sums, ph):
            Qh = [carve(64 + 4 * i, [128, S], BF16) for i in range(4)]
            Kh = [carve(80 + 4 * i, [128, S], BF16) for i in range(4)]
            Vh = [carve(96 + 4 * i, [128, NT, 128], BF16) for i in range(4)]
            Psb = [carve(112 + 4 * i, [128, S], BF16) for i in range(4)]
            PTg = [carve(128 + 16 * i, [128, 16, 512], BF16) for i in range(2)]
            tc = [0]

            def load_head(h):
                sl = h % 4
                dma("sync", Qh[sl][:, :], dram[scrQ][h], (), (("Qh", sl),), f"{ph}lq{h}")
                dma("sync", Kh[sl][:, :], dram[scrK][h], (), (("Kh", sl),), f"{ph}lk{h}")
                dma("sync", Vh[sl], dram[scrV][h], (), (("Vh", sl),), f"{ph}lv{h}")

            def sm_chain(h, qt):
                c = Chain()
                sl = h % 4
                hp2 = h % 2
                ctx = hp2 * 2 + (qt % 2)
                base = ctx * 16
                qsl = ctx
                ext = (qt + 1) * 128
                np_ = (ext + 511) // 512
                for j in range(np_):
                    w = min(512, ext - j * 512)
                    b = hp2 * 2 + (j % 2)
                    is_diag = (j == np_ - 1)
                    c.add("tensor", lambda tn, sl=sl, qt=qt, j=j, w=w, b=b: tn.matmul(
                        ps_mm[:, b * 512: b * 512 + w],
                        Qh[sl][:, qt * 128:(qt + 1) * 128],
                        Kh[sl][:, j * 512: j * 512 + w],
                        start=True, stop=True),
                      reads=(("Qh", sl), ("Kh", sl)), writes=(("psS", b),),
                      tag=f"{ph}s{h}q{qt}p{j}")
                    if is_diag:
                        c.add("vector", lambda v, b=b, w=w: v.tensor_tensor(
                            ps_mm[:, b * 512 + w - 128: b * 512 + w],
                            ps_mm[:, b * 512 + w - 128: b * 512 + w], maskc[:], op=ALU.add),
                          reads=(("psS", b), ("maskc",)), writes=(("psS", b),),
                          tag=f"{ph}mk{h}q{qt}")
                    c.add("vector", lambda v, b=b, w=w, base=base, j=j: v.reduce_max(
                        stt[:, base + j: base + j + 1], ps_mm[:, b * 512: b * 512 + w],
                        axis=AX.X, negate=True),
                      reads=(("psS", b),), writes=(("att", base, "pm", j),),
                      tag=f"{ph}mx{h}q{qt}p{j}")
                    c.add("scalar", lambda s, base=base, j=j: s.activation(
                        stt[:, base + 4 + j: base + 5 + j], stt[:, base + j: base + j + 1],
                        AF.Identity, bias=0.0, scale=SCALE),
                      reads=(("att", base, "pm", j),), writes=(("att", base, "bi", j),),
                      tag=f"{ph}bi{h}q{qt}p{j}")
                    c.add("scalar", lambda s, b=b, j=j, w=w, base=base, qsl=qsl:
                      s.activation(Psb[qsl][:, j * 512: j * 512 + w],
                                   ps_mm[:, b * 512: b * 512 + w], AF.Exp,
                                   bias=stt[:, base + 4 + j: base + 5 + j], scale=SCALE,
                                   accum_out=stt[:, base + 8 + j: base + 9 + j]),
                      reads=(("psS", b), ("att", base, "bi", j)),
                      writes=(("Psb", qsl), ("att", base, "ps", j)),
                      tag=f"{ph}ex{h}q{qt}p{j}")
                pmk = tuple(("att", base, "pm", j) for j in range(np_))
                psk = tuple(("att", base, "ps", j) for j in range(np_))
                bik = tuple(("att", base, "bi", j) for j in range(np_))
                if np_ == 1:
                    c.add("vector", lambda v, base=base: v.reciprocal(
                        stt[:, base + 12: base + 13], stt[:, base + 8: base + 9]),
                      reads=psk, writes=(("att", base, "rc"),), tag=f"{ph}rc{h}q{qt}")
                    c.add("vector", lambda v, qsl=qsl, ext=ext, base=base: v.tensor_scalar(
                        Psb[qsl][:, 0:ext], Psb[qsl][:, 0:ext],
                        stt[:, base + 12: base + 13], None, op0=ALU.mult),
                      reads=(("Psb", qsl), ("att", base, "rc")), writes=(("Psb", qsl),),
                      tag=f"{ph}pn{h}q{qt}")
                    return c
                c.add("vector", lambda v, base=base, np_=np_: v.tensor_reduce(
                    stt[:, base + 12: base + 13], stt[:, base: base + np_],
                    axis=AX.X, op=ALU.min),
                  reads=pmk, writes=(("att", base, "nm"),), tag=f"{ph}cm{h}q{qt}")
                c.add("vector", lambda v, base=base, np_=np_: v.tensor_scalar(
                    stt[:, base + 4: base + 4 + np_], stt[:, base: base + np_],
                    stt[:, base + 12: base + 13], None, op0=ALU.subtract),
                  reads=pmk + (("att", base, "nm"),), writes=bik, tag=f"{ph}df{h}q{qt}")
                c.add("scalar", lambda s, base=base, np_=np_: s.activation(
                    stt[:, base + 4: base + 4 + np_], stt[:, base + 4: base + 4 + np_],
                    AF.Exp, bias=0.0, scale=-SCALE),
                  reads=bik, writes=bik, tag=f"{ph}ef{h}q{qt}")
                c.add("vector", lambda v, base=base, np_=np_: v.tensor_tensor(
                    stt[:, base + 8: base + 8 + np_], stt[:, base + 8: base + 8 + np_],
                    stt[:, base + 4: base + 4 + np_], op=ALU.mult),
                  reads=psk + bik, writes=psk, tag=f"{ph}wt{h}q{qt}")
                c.add("vector", lambda v, base=base, np_=np_: v.reduce_sum(
                    stt[:, base + 13: base + 14], stt[:, base + 8: base + 8 + np_],
                    axis=AX.X),
                  reads=psk, writes=(("att", base, "tt"),), tag=f"{ph}ts{h}q{qt}")
                c.add("vector", lambda v, base=base: v.reciprocal(
                    stt[:, base + 14: base + 15], stt[:, base + 13: base + 14]),
                  reads=(("att", base, "tt"),), writes=(("att", base, "rc"),),
                  tag=f"{ph}rc{h}q{qt}")
                c.add("vector", lambda v, base=base, np_=np_: v.tensor_scalar(
                    stt[:, base + 4: base + 4 + np_], stt[:, base + 4: base + 4 + np_],
                    stt[:, base + 14: base + 15], None, op0=ALU.mult),
                  reads=bik + (("att", base, "rc"),), writes=bik, tag=f"{ph}fs{h}q{qt}")
                for j in range(np_):
                    w = min(512, ext - j * 512)
                    if j % 2 == 0:
                        c.add("vector", lambda v, qsl=qsl, j=j, w=w, base=base: v.tensor_scalar(
                            Psb[qsl][:, j * 512: j * 512 + w], Psb[qsl][:, j * 512: j * 512 + w],
                            stt[:, base + 4 + j: base + 5 + j], None, op0=ALU.mult),
                          reads=(("Psb", qsl),) + bik, writes=(("Psb", qsl),),
                          tag=f"{ph}pn{h}q{qt}p{j}")
                    else:
                        c.add("scalar", lambda s, qsl=qsl, j=j, w=w, base=base: s.activation(
                            Psb[qsl][:, j * 512: j * 512 + w], Psb[qsl][:, j * 512: j * 512 + w],
                            AF.Identity, bias=0.0, scale=stt[:, base + 4 + j: base + 5 + j]),
                          reads=(("Psb", qsl),) + bik, writes=(("Psb", qsl),),
                          tag=f"{ph}pn{h}q{qt}p{j}")
                return c

            def transposes(h, qt):
                qsl = (h % 2) * 2 + (qt % 2)
                hp2 = h % 2
                for j0 in range(0, qt + 1, 8):
                    nb = min(8, qt + 1 - j0)
                    tb = tc[0] % 2
                    tc[0] += 1
                    tkey = ("ptT", tb)
                    for jj in range(nb):
                        kt = j0 + jj
                        P("tensor", lambda tn, qsl=qsl, kt=kt, jj=jj, tb=tb: tn.transpose(
                            ptb[:, tb * 1024 + jj * 128: tb * 1024 + (jj + 1) * 128],
                            Psb[qsl][:, kt * 128:(kt + 1) * 128], idn[:]),
                          reads=(("Psb", qsl), ("idn",)), writes=(tkey,),
                          tag=f"{ph}t{h}q{qt}j{kt}")
                    P("scalar", lambda s, hp2=hp2, qt=qt, j0=j0, nb=nb, tb=tb:
                      s.copy(
                        PTg[hp2][:, j0:j0 + nb, (qt % 4) * 128:(qt % 4) * 128 + 128],
                        ptb[:, tb * 1024: tb * 1024 + nb * 128].rearrange(
                            "p (a b) -> p a b", a=nb)),
                      reads=(tkey, ("PTgz", hp2)), writes=(("PTg", hp2, qt, j0),),
                      tag=f"{ph}pe{h}q{qt}j{j0}")

            for h in range(4):
                if h < NH:
                    load_head(h)
            for hp in range(0, NH, 2):
                if hp + 3 < NH:
                    load_head(hp + 2)
                    load_head(hp + 3)
                pair = (hp, hp + 1)
                for g in range(NG):
                    nkt = 4 * g + 4
                    for h in pair:
                        hp2 = h % 2
                        P("vector", lambda v, hp2=hp2, g=g: v.memset(
                            PTg[hp2].rearrange("p a b -> p (a b)")[:, 4 * g * 512:(4 * g + 4) * 512], 0.0),
                          reads=(), writes=(("PTgz", hp2),), tag=f"{ph}z{h}g{g}")
                    for qt in range(4 * g, 4 * g + 4):
                        emit_zip(sm_chain(pair[0], qt), sm_chain(pair[1], qt))
                        if qt > 4 * g:
                            for h in pair:
                                transposes(h, qt - 1)
                    for h in pair:
                        transposes(h, 4 * g + 3)
                    for h in pair:
                        hp2 = h % 2
                        sl = h % 4
                        okey = ("psO", hp2)
                        ptreads = tuple(("PTg", hp2, qt, j0) for qt in range(4 * g, 4 * g + 4)
                                        for j0 in range(0, qt + 1, 8)) + (("PTgz", hp2),)
                        for kt in range(nkt):
                            P("tensor", lambda tn, sl=sl, kt=kt, hp2=hp2,
                              first=(kt == 0), last=(kt == nkt - 1): tn.matmul(
                                ps_mm2[:, hp2 * 512:(hp2 + 1) * 512],
                                Vh[sl][:, kt, :], PTg[hp2][:, kt, :],
                                start=first, stop=last),
                              reads=(("Vh", sl),) + (ptreads if (kt == 0 or kt == nkt - 1) else ()),
                              writes=(okey,), tag=f"{ph}pv{h}g{g}k{kt}")
                        P("scalar", lambda s, h=h, g=g, hp2=hp2: s.activation(
                            attn_fm[:, h, g * 512:(g + 1) * 512],
                            ps_mm2[:, hp2 * 512:(hp2 + 1) * 512],
                            AF.Identity, bias=0.0, scale=1.0),
                          reads=(okey,), writes=((akey, h, g),), tag=f"{ph}oe{h}g{g}")

        attn1_fm = carve(0, [128, NT, S], BF16)
        if "B" in phases:
            attention_phase("sQ1", "sK1", "sV1", attn1_fm, "at1", False, "B")

        if "C" in phases:
            wo_hi = carve(160, [128, 4, D], BF16)
            load_weight_cast(wo_hi.rearrange("p a b -> p (a b)"), dram["w_o1"], 12 * 128, 512, 0, D,
                             ("wo", 1), "Cw1")

        prog.barrier()

        if "B" in phases and "XB" in taps:
            dma("sync", dram["sAT1"], attn1_fm, tuple(("at1", h, g) for h in range(NH) for g in range(NG)),
                (("sAT1w",),), "tapB")
            dma("sync", tap_aps["XB"], dram["sAT1"], (("sAT1w",),), (("XB_tap",),), "tapB2")

        if "C" in phases:
            wo_lo = carve(64, [128, 12, D], BF16)
            load_weight_cast(wo_lo.rearrange("p a b -> p (a b)"), dram["w_o1"], 0, 12 * 128, 0, D,
                             ("wo", 0), "Cw0")

        if "C" not in phases:
            prog.barrier()
            for tp in taps:
                if tp == "XB":
                    continue
                dma("sync", tap_aps[tp], dram["s" + tp], (), ((tp + "_tap",),), "tap" + tp)
            prog.barrier()
            return nc, finish(), len(prog.ops)
        x0t_c = [carve(112 + 8 * i, [128, D], F32) for i in range(2)]
        x1t = [carve(128 + 8 * i, [128, D], F32) for i in range(2)]
        yt = carve(144, [128, D], F32)
        sqc = carve(152, [128, D], F32)
        at1_all = tuple(("at1", h, g) for h in range(NH) for g in range(NG))

        for t in range(NT):
            par = t % 2
            xk = ("x0c", par)
            dma("sync", x0t_c[par], x0r[t], (), (xk,), f"Cx{t}")
            pk = ("psC", par)
            pst = ps_mm if par == 0 else ps_mm2
            korder = [12, 13, 14, 15] + list(range(12))
            for cg in range(4):
                for ki, k in enumerate(korder):
                    half = 1 if k >= 12 else 0
                    wbuf = wo_hi if half == 1 else wo_lo
                    P("tensor", lambda tn, t=t, cg=cg, k=k, pst=pst, wbuf=wbuf,
                      first=(ki == 0), last=(ki == NT - 1), ki=ki: tn.matmul(
                        pst[:, cg * 512:(cg + 1) * 512],
                        attn1_fm[:, k, t * 128:(t + 1) * 128],
                        wbuf[:, k % 12 if k < 12 else k - 12, cg * 512:(cg + 1) * 512],
                        start=first, stop=last),
                      reads=(("wo", half, (k % 12) if k < 12 else k - 12),) + (at1_all if (cg == 0 and ki == 0) else ()),
                      writes=(pk,), tag=f"Cm{t}c{cg}k{k}")
            col = 40 + par * 8
            ln_stats(pst[:, 0:D], (pk,), sqc, ("sqc",), col, D, tag=f"Cs{t}")
            P("vector", lambda v, pst=pst, col=col, par=par: v.tensor_scalar(
                x1t[par], pst[:, 0:D], stt[:, col:col + 1], stt[:, col + 1:col + 2],
                op0=ALU.subtract, op1=ALU.mult),
              reads=(pk, ("stt", col, "mean"), ("stt", col, "rstd")),
              writes=(("x1t", par),), tag=f"Ca{t}")
            P("vector", lambda v, par=par: v.tensor_tensor(
                x1t[par], x1t[par], x0t_c[par], op=ALU.add),
              reads=(("x1t", par), xk), writes=(("x1t", par),), tag=f"Cr{t}")
            dma("sync", dram["sX1"][t], x1t[par], (("x1t", par),), (("sX1", t),), f"Cox{t}")
            col2 = 56 + par * 4
            ln_stats(x1t[par], (("x1t", par),), sqc, ("sqc",), col2, D, tag=f"Cy{t}")
            P("vector", lambda v, par=par, col2=col2: v.tensor_scalar(
                yt, x1t[par], stt[:, col2:col2 + 1], stt[:, col2 + 1:col2 + 2],
                op0=ALU.subtract, op1=ALU.mult),
              reads=(("x1t", par), ("stt", col2, "mean"), ("stt", col2, "rstd")),
              writes=(("yt",),), tag=f"Cyap{t}")
            dma("sync", dram["sY"][t], yt, (("yt",),), (("sY", t),), f"Coy{t}")

        prog.barrier()

        if "D" in phases:
            dma("sync", cosb[:], dram["COSM"], (("cosb",),), (("cosb",),), "Dc")
            dma("sync", sinb[:], dram["SINM"], (("sinb",),), (("sinb",),), "Ds")
            r1_fm = carve(0, [128, NT, S], BF16)
            lnw1_bc = carve(64, [128, D], F32)
            ytD = [carve(72 + 8 * i, [128, D], F32) for i in range(2)]
            r1bf = [carve(88 + 4 * i, [128, D], BF16) for i in range(4)]
            sqD = carve(104, [128, D], F32)
            wkvd = carve(112, [128, NT, 512], BF16)
            wqd = carve(128, [128, NT, 512], BF16)
            kv_fm = carve(144, [128, 4, S], BF16)
            qd_fm = carve(160, [128, 4, S], BF16)

            dma("sync", lnw1_bc, dram["ln1_w"][0:1, :].partition_broadcast(128).squeeze(1),
                (), (("lnw1",),), "Dlnw")
            load_weight_cast(wkvd.rearrange("p a b -> p (a b)"), dram["kv_d_w"], 0, D, 0, 512,
                             ("wkvd",), "Dwk")
            load_weight_cast(wqd.rearrange("p a b -> p (a b)"), dram["q_d_w"], 0, D, 0, 512,
                             ("wqd",), "Dwq")

            for g in range(NG):
                for ti in range(4):
                    t = g * 4 + ti
                    sl = t % 2
                    yk = ("ytD", sl)
                    dma("sync", ytD[sl], dram["sY"][t], (), (yk,), f"Dy{t}")
                    col = sl * 8
                    rms_stats(ytD[sl], (yk,), sqD, ("sqD",), col, D, tag=f"Dr{t}")
                    P("vector", lambda v, sl=sl, col=col: v.tensor_scalar(
                        sqD, ytD[sl], stt[:, col + 1:col + 2], None, op0=ALU.mult),
                      reads=(yk, ("stt", col, "rstd")), writes=(("sqD",),), tag=f"Dm1{t}")
                    P("vector", lambda v, ti=ti: v.tensor_tensor(
                        r1bf[ti], sqD, lnw1_bc, op=ALU.mult),
                      reads=(("sqD",), ("lnw1",)), writes=(("r1bf", ti),), tag=f"Dm2{t}")
                for c in range(NT):
                    bank = c % 2
                    pk = ("ptA", bank)
                    for ti in range(4):
                        P("tensor", lambda tn, c=c, ti=ti, bank=bank: tn.transpose(
                            ptb[:, bank * 1024 + ti * 128: bank * 1024 + (ti + 1) * 128],
                            r1bf[ti][:, c * 128:(c + 1) * 128], idn[:]),
                          reads=(("r1bf", ti), ("idn",)), writes=(pk,), tag=f"Dt{g}c{c}")
                    P("vector", lambda v, c=c, g=g, bank=bank: v.tensor_copy(
                        r1_fm[:, c, g * 512:(g + 1) * 512],
                        ptb[:, bank * 1024: bank * 1024 + 512]),
                      reads=(pk,), writes=(("r1_fm", c, g),), tag=f"De{g}c{c}")

            r1_all = tuple(("r1_fm", c, g) for c in range(NT) for g in range(NG))

            uidD = 0
            for which, wbuf, wkey, dstb, dkey in (
                    (0, wkvd, ("wkvd",), kv_fm, "kv_fm"), (1, wqd, ("wqd",), qd_fm, "qd_fm")):
                for oc in range(4):
                    for g in range(NG):
                        bank = uidD % 4
                        pk = ("psA", bank)
                        for k in range(NT):
                            P("tensor", lambda tn, wbuf=wbuf, k=k, oc=oc, g=g, bank=bank,
                              first=(k == 0), last=(k == NT - 1): tn.matmul(
                                ps_mm[:, bank * 512:(bank + 1) * 512],
                                wbuf[:, k, oc * 128:(oc + 1) * 128],
                                r1_fm[:, k, g * 512:(g + 1) * 512],
                                start=first, stop=last),
                              reads=(wkey + (k,),) + (r1_all if k == 0 else ()),
                              writes=(pk,), tag=f"Dp{which}o{oc}g{g}k{k}")
                        P("scalar", lambda s, dstb=dstb, oc=oc, g=g, bank=bank: s.activation(
                            dstb[:, oc, g * 512:(g + 1) * 512],
                            ps_mm[:, bank * 512:(bank + 1) * 512],
                            AF.Identity, bias=0.0, scale=1.0),
                          reads=(pk,), writes=((dkey, oc, g),), tag=f"Dpe{which}o{oc}g{g}")
                        uidD += 1

            if "R1F" in taps:
                dma("sync", dram["sR1F"], r1_fm, r1_all, (("sR1Fw",),), "tapR1F")
                dma("sync", tap_aps["R1F"], dram["sR1F"], (("sR1Fw",),), (("R1F_tap",),), "tapR1F2")
            if "KVF" in taps:
                dma("sync", dram["sKVF"], kv_fm, kv_all0 if False else tuple(("kv_fm", oc, g) for oc in range(4) for g in range(NG)), (("sKVFw",),), "tapKVF")
                dma("sync", tap_aps["KVF"], dram["sKVF"], (("sKVFw",),), (("KVF_tap",),), "tapKVF2")

            prog.barrier()

            wrk = carve(64, [128, NT, 1024], BF16)
            qraw2 = [carve(104 + 1 * i, [128, 512], BF16) for i in range(2)]
            tmp1b = [carve(106 + 2 * i, [128, 512], F32) for i in range(2)]
            tmp2b = [carve(110 + 2 * i, [128, 512], F32) for i in range(2)]
            qout2 = [carve(114 + 1 * i, [128, 512], BF16) for i in range(2)]
            vout2 = [carve(116 + 1 * i, [128, 512], BF16) for i in range(2)]
            load_weight_cast(wrk.rearrange("p a b -> p (a b)"), dram["rope_k_w"], 0, D, 0, 1024,
                             ("wrk",), "D2wr")

            kv_all = tuple(("kv_fm", oc, g) for oc in range(4) for g in range(NG))
            qd_all = tuple(("qd_fm", oc, g) for oc in range(4) for g in range(NG))

            def mla_unit(uid, nk, mm_fn, wkeys, rhs_all, g, oc, do_rope, scr_name, row0, tag):
                c = Chain()
                sl = uid % 2
                bank = uid % 4
                pk = ("psA", bank)
                for k in range(nk):
                    c.add("tensor", lambda tn, k=k, bank=bank, first=(k == 0),
                          last=(k == nk - 1): mm_fn(tn, k, bank, first, last),
                          reads=(wkeys[k],) + (rhs_all if k == 0 else ()),
                          writes=(pk,), tag=tag + f"k{k}")
                c.add("scalar", lambda s, sl=sl, bank=bank: s.activation(
                    qraw2[sl], ps_mm[:, bank * 512:(bank + 1) * 512], AF.Identity,
                    bias=0.0, scale=1.0),
                  reads=(pk,), writes=(("qr2", sl),), tag=tag + "ev")
                qk = ("qr2", sl)
                if do_rope:
                    rk = ("psrot", sl)
                    c.add("tensor", lambda tn, sl=sl: tn.matmul(
                        ps_mm2[:, sl * 512:(sl + 1) * 512], rmm[:], qraw2[sl],
                        start=True, stop=True),
                      reads=(qk, ("rotM",)), writes=(rk,), tag=tag + "rot")
                    c.add("vector", lambda v, sl=sl, g=g: v.tensor_tensor(
                        tmp1b[sl], qraw2[sl], cosb[:, g * 512:(g + 1) * 512], op=ALU.mult),
                      reads=(qk, ("cosb",)), writes=(("t1b", sl),), tag=tag + "m1")
                    c.add("vector", lambda v, sl=sl, g=g: v.tensor_tensor(
                        tmp2b[sl], ps_mm2[:, sl * 512:(sl + 1) * 512],
                        sinb[:, g * 512:(g + 1) * 512], op=ALU.mult),
                      reads=(rk, ("sinb",)), writes=(("t2b", sl),), tag=tag + "m2")
                    c.add("vector", lambda v, sl=sl: v.tensor_tensor(
                        qout2[sl], tmp1b[sl], tmp2b[sl], op=ALU.add),
                      reads=(("t1b", sl), ("t2b", sl)), writes=(("qo2", sl),), tag=tag + "ad")
                    srcb, srckey = qout2[sl], ("qo2", sl)
                else:
                    srcb, srckey = qraw2[sl], qk
                for half in range(2):
                    h = 2 * oc + half
                    dstp = dram[scr_name][h][row0:row0 + 64, g * 512:(g + 1) * 512]
                    c.add("sync", lambda q, dstp=dstp, sb=srcb, half=half: q.dma_start(
                        out=dstp, in_=sb[half * 64:(half + 1) * 64, :]),
                      reads=(srckey,), writes=((scr_name, h, row0, g),),
                      tag=tag + f"st{half}", dma=True)
                return c

            def zip_pairs(units):
                for i in range(0, len(units) - 1, 2):
                    emit_zip(units[i], units[i + 1])
                if len(units) % 2:
                    emit_zip(units[-1])

            uid2 = 0
            units = []
            for oc in range(8):
                for g in range(NG):
                    def mk_mm(oc=oc, g=g):
                        def f(tn, k, bank, first, last):
                            return tn.matmul(
                                ps_mm[:, bank * 512:(bank + 1) * 512],
                                wrk[:, k, oc * 128:(oc + 1) * 128],
                                r1_fm[:, k, g * 512:(g + 1) * 512],
                                start=first, stop=last)
                        return f
                    units.append(mla_unit(uid2, NT, mk_mm(), [("wrk", k) for k in range(NT)],
                                          r1_all, g, oc, True, "sKM", 64, f"D2kr{oc}g{g}"))
                    uid2 += 1
            zip_pairs(units)

            prog.barrier()

            wku = carve(64, [128, 4, 1024], BF16)
            wqu = carve(72, [128, 4, 1024], BF16)
            wrq = carve(80, [128, 4, 1024], BF16)
            wvu = carve(88, [128, 4, S], BF16)
            load_weight_cast(wku.rearrange("p a b -> p (a b)"), dram["k_u_w"], 0, LAT, 0, 1024,
                             ("wku",), "D2wku")
            load_weight_cast(wqu.rearrange("p a b -> p (a b)"), dram["q_u_w"], 0, LAT, 0, 1024,
                             ("wqu",), "D2wqu")
            load_weight_cast(wrq.rearrange("p a b -> p (a b)"), dram["rope_q_w"], 0, LAT, 0, 1024,
                             ("wrq",), "D2wrq")
            load_weight_cast(wvu.rearrange("p a b -> p (a b)"), dram["v_u_w"], 0, LAT, 0, D,
                             ("wvu",), "D2wvu")

            units = []
            for which, wbuf, wkeyn, rhsb, rhsall, rope_f, scrn, row0 in (
                    ("kp", wku, "wku", kv_fm, kv_all, False, "sKM", 0),
                    ("qp", wqu, "wqu", qd_fm, qd_all, False, "sQM", 0),
                    ("qr", wrq, "wrq", qd_fm, qd_all, True, "sQM", 64)):
                for oc in range(8):
                    for g in range(NG):
                        def mk_mm(wbuf=wbuf, rhsb=rhsb, oc=oc, g=g):
                            def f(tn, k, bank, first, last):
                                return tn.matmul(
                                    ps_mm[:, bank * 512:(bank + 1) * 512],
                                    wbuf[:, k, oc * 128:(oc + 1) * 128],
                                    rhsb[:, k, g * 512:(g + 1) * 512],
                                    start=first, stop=last)
                            return f
                        units.append(mla_unit(uid2, 4, mk_mm(),
                                              [(wkeyn, k) for k in range(4)], rhsall,
                                              g, oc, rope_f, scrn, row0,
                                              f"D2{which}{oc}g{g}"))
                        uid2 += 1
            def v_unit(uid, cg, t):
                c = Chain()
                bank = uid % 4
                pk = ("psA", bank)
                for k in range(4):
                    c.add("tensor", lambda tn, k=k, t=t, cg=cg, bank=bank,
                          first=(k == 0), last=(k == 3): tn.matmul(
                        ps_mm[:, bank * 512:(bank + 1) * 512],
                        kv_fm[:, k, t * 128:(t + 1) * 128],
                        wvu[:, k, cg * 512:(cg + 1) * 512],
                        start=first, stop=last),
                      reads=(("wvu", k),) + (kv_all if k == 0 else ()),
                      writes=(pk,), tag=f"D2v{cg}t{t}k{k}")
                vsl = uid % 2
                c.add("scalar", lambda s, bank=bank, vsl=vsl: s.activation(
                    vout2[vsl], ps_mm[:, bank * 512:(bank + 1) * 512],
                    AF.Identity, bias=0.0, scale=1.0),
                  reads=(pk,), writes=(("vo2", vsl),), tag=f"D2ve{cg}t{t}")
                h0 = cg * 4
                dstp = dram["sVM"][h0:h0 + 4, :, t, :].rearrange("h p f -> p h f")
                srcp = vout2[vsl].rearrange("p (h f) -> p h f", h=4)
                c.add("sync", lambda q, dstp=dstp, srcp=srcp: q.dma_start(out=dstp, in_=srcp),
                      reads=(("vo2", vsl),), writes=(("sVM", t, h0),),
                      tag=f"D2vs{cg}t{t}", dma=True)
                return c

            for cg in range(4):
                for t in range(NT):
                    units.append(v_unit(uid2, cg, t))
                    uid2 += 1
            zip_pairs(units)

            prog.barrier()

        if "E" in phases:
            attn2_fm = carve(0, [128, NT, S], BF16)
            attention_phase("sQM", "sKM", "sVM", attn2_fm, "at2", False, "E")
            ow_hi = carve(160, [128, 4, D], BF16)
            load_weight_cast(ow_hi.rearrange("p a b -> p (a b)"), dram["o_w"], 12 * 128, 512,
                             0, D, ("ow", 1), "Fw1")
            prog.barrier()

        if "F" in phases:
            ow_lo = carve(64, [128, 12, D], BF16)
            load_weight_cast(ow_lo.rearrange("p a b -> p (a b)"), dram["o_w"], 0, 12 * 128,
                             0, D, ("ow", 0), "Fw0")
            ytF = carve(112, [128, D], F32)
            htF = carve(120, [128, D], F32)
            m_bf = [carve(128 + 4 * i, [128, D], BF16) for i in range(2)]
            mstage = [carve(136 + 2 * i, [128, 512], BF16) for i in range(2)]
            sqF = carve(140, [128, D], F32)
            lnw2_bc = carve(148, [128, D], F32)
            rw_sb = carve(156, [128, NT, 8], BF16)
            rbbc = carve(157, [128, 8], F32)
            at2_all = tuple(("at2", h, g) for h in range(NH) for g in range(NG))

            dma("sync", lnw2_bc, dram["ln2_w"][0:1, :].partition_broadcast(128).squeeze(1),
                (), (("lnw2",),), "Flnw")
            dma("sync", rbbc, dram["rb_pad"][0:1, :].partition_broadcast(128).squeeze(1),
                (), (("rbbc",),), "Frb")
            P("vector", lambda v: v.memset(rw_sb.rearrange("p a b -> p (a b)"), 0.0),
              reads=(), writes=(("rwz",),), tag="Frwz")
            for k in range(NT):
                dma("gpsimd", rw_sb[:, k, 0:7], dram["router_w"][k * 128:(k + 1) * 128, :],
                    (("rwz",),), (("rw", k),), f"Frw{k}")

            for t in range(NT):
                g, tt4 = t // 4, t % 4
                korder = [12, 13, 14, 15] + list(range(12))
                for cg in range(4):
                    for ki, k in enumerate(korder):
                        half = 1 if k >= 12 else 0
                        wbuf = ow_hi if half == 1 else ow_lo
                        P("tensor", lambda tn, t=t, cg=cg, k=k, wbuf=wbuf,
                          first=(ki == 0), last=(ki == NT - 1), ki=ki: tn.matmul(
                            ps_mm[:, cg * 512:(cg + 1) * 512],
                            attn2_fm[:, k, t * 128:(t + 1) * 128],
                            wbuf[:, k if k < 12 else k - 12, cg * 512:(cg + 1) * 512],
                            start=first, stop=last),
                          reads=(("ow", half, k if k < 12 else k - 12),) + (at2_all if (cg == 0 and ki == 0) else ()),
                          writes=(("psF", cg),), tag=f"Fm{t}c{cg}k{k}")
                dma("sync", ytF, dram["sY"][t], (), (("ytF",),), f"Fy{t}")
                for cg in range(4):
                    P("vector", lambda v, cg=cg: v.tensor_tensor(
                        htF[:, cg * 512:(cg + 1) * 512], ps_mm[:, cg * 512:(cg + 1) * 512],
                        ytF[:, cg * 512:(cg + 1) * 512], op=ALU.add),
                      reads=(("psF", cg), ("ytF",)), writes=(("htF", cg),), tag=f"Fh{t}c{cg}")
                hkeys = tuple(("htF", cg) for cg in range(4))
                dma("sync", dram["sH"][t], htF, hkeys, (("sHw", t),), f"Fho{t}")
                col = 16 + (t % 2) * 8
                rms_stats(htF, hkeys, sqF, ("sqF",), col, D, tag=f"Fr{t}")
                P("vector", lambda v, col=col: v.tensor_scalar(
                    sqF, htF, stt[:, col + 1:col + 2], None, op0=ALU.mult),
                  reads=hkeys + (("stt", col, "rstd"),), writes=(("sqF",),), tag=f"Fm1{t}")
                mpar = t % 2
                P("vector", lambda v, mpar=mpar: v.tensor_tensor(
                    m_bf[mpar], sqF, lnw2_bc, op=ALU.mult),
                  reads=(("sqF",), ("lnw2",)), writes=(("m_bf", mpar),), tag=f"Fm2{t}")
                for c4 in range(4):
                    tb = c4 % 2
                    tkey = ("ptT", tb)
                    for cc in range(4):
                        c = c4 * 4 + cc
                        P("tensor", lambda tn, mpar=mpar, c=c, cc=cc, tb=tb: tn.transpose(
                            ptb[:, tb * 1024 + cc * 128: tb * 1024 + (cc + 1) * 128],
                            m_bf[mpar][:, c * 128:(c + 1) * 128], idn[:]),
                          reads=(("m_bf", mpar), ("idn",)), writes=(tkey,), tag=f"Ft{t}c{c}")
                    msl = c4 % 2
                    P("vector", lambda v, msl=msl, tb=tb: v.tensor_copy(
                        mstage[msl], ptb[:, tb * 1024: tb * 1024 + 512]),
                      reads=(tkey,), writes=(("mst", msl),), tag=f"Fme{t}c{c4}")
                    dst = dram["sMF"][g][:, c4 * 4:(c4 + 1) * 4, tt4 * 128:(tt4 + 1) * 128]
                    src = mstage[msl].rearrange("p (a b) -> p a b", a=4)
                    dma("sync", dst, src, (("mst", msl),), (("sMF", t, c4),), f"Fms{t}c{c4}")
                    for cc in range(4):
                        c = c4 * 4 + cc
                        P("tensor", lambda tn, msl=msl, cc=cc, c=c,
                          first=(c == 0), last=(c == NT - 1): tn.matmul(
                            ps_mm2[:, 0:8], mstage[msl][:, cc * 128:(cc + 1) * 128],
                            rw_sb[:, c, :], start=first, stop=last),
                          reads=(("mst", msl), ("rw", c)), writes=(("psR",),), tag=f"Flg{t}c{c}")
                P("vector", lambda v: v.tensor_tensor(ps_mm2[:, 0:8], ps_mm2[:, 0:8],
                                                      rbbc, op=ALU.add),
                  reads=(("psR",), ("rbbc",)), writes=(("psR",),), tag=f"Fba{t}")
                P("scalar", lambda s: s.activation(probs8[:], ps_mm2[:, 0:8], AF.Sigmoid,
                                                   bias=0.0, scale=1.0),
                  reads=(("psR",),), writes=(("pr8",),), tag=f"Fsg{t}")
                P("vector", lambda v: v.reduce_max(stt[:, 48:49], probs8[:], axis=AX.X),
                  reads=(("pr8",),), writes=(("rm1",),), tag=f"Fx1{t}")
                P("vector", lambda v: v.tensor_scalar(scr8[:], probs8[:], stt[:, 48:49], None,
                                                      op0=ALU.is_equal),
                  reads=(("pr8",), ("rm1",)), writes=(("eq1",),), tag=f"Fe1{t}")
                P("vector", lambda v: v.tensor_scalar(scr8b[:], scr8[:], -1e9, None, op0=ALU.mult),
                  reads=(("eq1",),), writes=(("eq1b",),), tag=f"Fe2{t}")
                P("vector", lambda v: v.tensor_tensor(scr8b[:], probs8[:], scr8b[:], op=ALU.add),
                  reads=(("pr8",), ("eq1b",)), writes=(("msk",),), tag=f"Fe3{t}")
                P("vector", lambda v: v.reduce_max(stt[:, 49:50], scr8b[:], axis=AX.X),
                  reads=(("msk",),), writes=(("rm2",),), tag=f"Fx2{t}")
                P("vector", lambda v: v.tensor_scalar(scr8b[:], scr8b[:], stt[:, 49:50], None,
                                                      op0=ALU.is_equal),
                  reads=(("msk",), ("rm2",)), writes=(("eq2",),), tag=f"Fe4{t}")
                P("vector", lambda v: v.tensor_tensor(scr8[:], scr8[:], scr8b[:], op=ALU.add),
                  reads=(("eq1",), ("eq2",)), writes=(("sel2",),), tag=f"Fe5{t}")
                P("vector", lambda v: v.tensor_tensor(scr8[:], scr8[:], probs8[:], op=ALU.mult),
                  reads=(("sel2",), ("pr8",)), writes=(("wsel",),), tag=f"Fe6{t}")
                P("vector", lambda v: v.tensor_scalar(stt[:, 50:51], stt[:, 48:49],
                                                      stt[:, 49:50], None, op0=ALU.add),
                  reads=(("rm1",), ("rm2",)), writes=(("den",),), tag=f"Fe7{t}")
                P("vector", lambda v: v.reciprocal(stt[:, 51:52], stt[:, 50:51]),
                  reads=(("den",),), writes=(("rden",),), tag=f"Fe8{t}")
                P("vector", lambda v: v.tensor_scalar(scr8[:], scr8[:], stt[:, 51:52], None,
                                                      op0=ALU.mult),
                  reads=(("wsel",), ("rden",)), writes=(("wfin",),), tag=f"Fe9{t}")
                P("tensor", lambda tn: tn.transpose(ps_mm2[0:8, 512:640], scr8[:], idnf[:]),
                  reads=(("wfin",), ("idnf",)), writes=(("psW",),), tag=f"Fwt{t}")
                P("vector", lambda v, t=t: v.tensor_copy(wT[0:8, t * 128:(t + 1) * 128],
                                                         ps_mm2[0:8, 512:640]),
                  reads=(("psW",),), writes=(("wT", t),), tag=f"Fwc{t}")

            prog.barrier()

        if "G" in phases:
            m_g = [carve(0 + 16 * i, [128, NT, 512], BF16) for i in range(2)]
            wg_b = carve(32, [128, NT, 1024], BF16)
            wu_b = carve(64, [128, NT, 1024], BF16)
            wd_b = carve(96, [128, 8, D], BF16)
            tE = carve(128, [128, 8, 512], BF16)
            tEf = carve(128, [128, D], F32)
            f_sb = carve(136, [128, 4, D], F32)
            gsb = [carve(168 + 2 * i, [128, 512], F32) for i in range(2)]
            wbc = [carve(172 + 2 * i, [128, 512], F32) for i in range(2)]

            def g_load_expert(e, tag):
                if e == 0:
                    wg_d, wu_d, wd_d = dram["sh_gate"], dram["sh_up"], dram["sh_down"]
                else:
                    wg_d, wu_d, wd_d = (dram["r_gate"][e - 1], dram["r_up"][e - 1],
                                        dram["r_down"][e - 1])
                load_weight_cast(wg_b.rearrange("p a b -> p (a b)"), wg_d, 0, D, 0, MOE_I,
                                 ("wgB",), tag + "g")
                load_weight_cast(wu_b.rearrange("p a b -> p (a b)"), wu_d, 0, D, 0, MOE_I,
                                 ("wuB",), tag + "u")
                load_weight_cast(wd_b.rearrange("p a b -> p (a b)"), wd_d, 0, MOE_I, 0, D,
                                 ("wdB",), tag + "d")

            for g in range(NG):
                gp = g % 2
                mk = ("m_g", gp)
                dma("sync", m_g[gp].rearrange("p a b -> p (a b)"),
                    dram["sMF"][g].rearrange("p a b -> p (a b)"), (), (mk,), f"Gm{g}")
                for ti in range(4):
                    t = g * 4 + ti
                    dma("sync", f_sb[:, ti, :], dram["sH"][t], (("fS", ti),),
                        (("fS", ti),), f"Gh{g}t{ti}")
                for e in range(8):
                    tg = f"Ge{g}e{e}"
                    g_load_expert(e, tg)
                    if e > 0:
                        P("tensor", lambda tn, e=e, g=g: tn.matmul(
                            ps_mm[:, 0:512], sel8[:, (e - 1) * 128: e * 128],
                            wT[:, g * 512:(g + 1) * 512], start=True, stop=True),
                          reads=(("sel8",), ("wT", 4 * g), ("wT", 4 * g + 1),
                                 ("wT", 4 * g + 2), ("wT", 4 * g + 3)),
                          writes=(("psGg", 0),), tag=tg + "wb")
                        P("vector", lambda v, gp=gp: v.tensor_copy(wbc[gp], ps_mm[:, 0:512]),
                          reads=(("psGg", 0),), writes=(("wbc", gp),), tag=tg + "wc")
                    for ic in range(8):
                        bg = ic % 2
                        bu = 2 + ic % 2
                        for k in range(NT):
                            P("tensor", lambda tn, k=k, ic=ic, gp=gp, bg=bg,
                              first=(k == 0), last=(k == NT - 1): tn.matmul(
                                ps_mm[:, bg * 512:(bg + 1) * 512],
                                wg_b[:, k, ic * 128:(ic + 1) * 128],
                                m_g[gp][:, k, :], start=first, stop=last),
                              reads=(("wgB", k), mk), writes=(("psGg", bg),), tag=tg + f"g{ic}k{k}")
                        for k in range(NT):
                            P("tensor", lambda tn, k=k, ic=ic, gp=gp, bu=bu,
                              first=(k == 0), last=(k == NT - 1): tn.matmul(
                                ps_mm[:, bu * 512:(bu + 1) * 512],
                                wu_b[:, k, ic * 128:(ic + 1) * 128],
                                m_g[gp][:, k, :], start=first, stop=last),
                              reads=(("wuB", k), mk), writes=(("psGu", bu),), tag=tg + f"u{ic}k{k}")
                        gp2 = ic % 2
                        P("scalar", lambda s, gp2=gp2, bg=bg: s.activation(
                            gsb[gp2], ps_mm[:, bg * 512:(bg + 1) * 512], AF.Silu,
                            bias=0.0, scale=1.0),
                          reads=(("psGg", bg),), writes=(("gsb", gp2),), tag=tg + f"si{ic}")
                        P("vector", lambda v, gp2=gp2, bu=bu: v.tensor_tensor(
                            gsb[gp2], gsb[gp2], ps_mm[:, bu * 512:(bu + 1) * 512],
                            op=ALU.mult),
                          reads=(("gsb", gp2), ("psGu", bu)), writes=(("gsb", gp2),),
                          tag=tg + f"gu{ic}")
                        if e > 0:
                            P("vector", lambda v, gp2=gp2, ic=ic, gp=gp: v.tensor_tensor(
                                tE[:, ic, :], gsb[gp2], wbc[gp], op=ALU.mult),
                              reads=(("gsb", gp2), ("wbc", gp)), writes=(("tE", ic),),
                              tag=tg + f"tw{ic}")
                        else:
                            P("vector", lambda v, gp2=gp2, ic=ic: v.tensor_scalar(
                                tE[:, ic, :], gsb[gp2], 0.125, None, op0=ALU.mult),
                              reads=(("gsb", gp2),), writes=(("tE", ic),), tag=tg + f"ts{ic}")
                    tE_all = tuple(("tE", ic) for ic in range(8))
                    for ti in range(4):
                        for cg in range(4):
                            for ic in range(8):
                                P("tensor", lambda tn, ti=ti, cg=cg, ic=ic,
                                  first=(ic == 0), last=(ic == 7): tn.matmul(
                                    ps_mm2[:, cg * 512:(cg + 1) * 512],
                                    tE[:, ic, ti * 128:(ti + 1) * 128],
                                    wd_b[:, ic, cg * 512:(cg + 1) * 512],
                                    start=first, stop=last),
                                  reads=(("wdB", ic), ("tE", ic)),
                                  writes=(("psGd", cg),), tag=tg + f"d{ti}c{cg}i{ic}")
                            P("vector", lambda v, ti=ti, cg=cg: v.tensor_tensor(
                                f_sb[:, ti, cg * 512:(cg + 1) * 512],
                                f_sb[:, ti, cg * 512:(cg + 1) * 512],
                                ps_mm2[:, cg * 512:(cg + 1) * 512], op=ALU.add),
                              reads=(("psGd", cg), ("fS", ti)), writes=(("fS", ti),),
                              tag=tg + f"fa{ti}c{cg}")
                for ti in range(4):
                    t = g * 4 + ti
                    col = 32 + (ti % 2) * 8
                    ln_stats(f_sb[:, ti, :], (("fS", ti),), tEf,
                             tuple(("tE", ic) for ic in range(8)), col, D, tag=f"Gl{t}")
                    P("vector", lambda v, ti=ti, col=col: v.tensor_scalar(
                        tEf, f_sb[:, ti, :], stt[:, col:col + 1], stt[:, col + 1:col + 2],
                        op0=ALU.subtract, op1=ALU.mult),
                      reads=(("fS", ti), ("stt", col, "mean"), ("stt", col, "rstd")),
                      writes=tuple(("tE", ic) for ic in range(8)), tag=f"Ga{t}")
                    dma("sync", f_sb[:, ti, :], dram["sX1"][t],
                        (("fS", ti),), (("fS", ti),), f"Gx1{t}")
                    P("vector", lambda v, ti=ti: v.tensor_tensor(
                        tEf, tEf, f_sb[:, ti, :], op=ALU.add),
                      reads=(("tE", 0), ("fS", ti)), writes=tuple(("tE", ic) for ic in range(8)),
                      tag=f"Gx2{t}")
                    dma("sync", dram["out"][t * 128:(t + 1) * 128, :], tEf,
                        (("tE", 0),), (("outw", t),), f"Go{t}")

        prog.barrier()

        dma("sync", dram["chk"], dram["sX1"][0], (), (("chk",),), "chk")

        for tp in taps:
            if tp == "XB":
                continue
            dma("sync", tap_aps[tp], dram["s" + tp], (), ((tp + "_tap",),), "tap" + tp)

        prog.barrier()

        counts = finish()

    return nc, counts, len(prog.ops)


def make_inmaps(inputs):
    c = host_consts()
    x = np.ascontiguousarray(np.asarray(inputs["L0_attn_x"]), dtype=np.float32)
    base = {
        "w_qkv": np.ascontiguousarray(np.asarray(inputs["L0_attn_w_qkv"], np.float32)),
        "w_o1": np.ascontiguousarray(np.asarray(inputs["L0_attn_w_o"], np.float32)),
        "ln1_w": np.ascontiguousarray(np.asarray(inputs["L0_ffn_ln1_w"], np.float32)).reshape(1, D),
        "ln2_w": np.ascontiguousarray(np.asarray(inputs["L0_ffn_ln2_w"], np.float32)).reshape(1, D),
        "kv_d_w": np.ascontiguousarray(np.asarray(inputs["L0_ffn_kv_d_w"], np.float32)),
        "q_d_w": np.ascontiguousarray(np.asarray(inputs["L0_ffn_q_d_w"], np.float32)),
        "k_u_w": np.ascontiguousarray(np.asarray(inputs["L0_ffn_k_u_w"], np.float32)),
        "v_u_w": np.ascontiguousarray(np.asarray(inputs["L0_ffn_v_u_w"], np.float32)),
        "q_u_w": np.ascontiguousarray(np.asarray(inputs["L0_ffn_q_u_w"], np.float32)),
        "rope_k_w": np.ascontiguousarray(np.asarray(inputs["L0_ffn_rope_k_w"], np.float32)),
        "rope_q_w": np.ascontiguousarray(np.asarray(inputs["L0_ffn_rope_q_w"], np.float32)),
        "o_w": np.ascontiguousarray(np.asarray(inputs["L0_ffn_o_w"], np.float32)),
        "sh_gate": np.ascontiguousarray(np.asarray(inputs["L0_ffn_sh_gate"], np.float32)),
        "sh_up": np.ascontiguousarray(np.asarray(inputs["L0_ffn_sh_up"], np.float32)),
        "sh_down": np.ascontiguousarray(np.asarray(inputs["L0_ffn_sh_down"], np.float32)),
        "r_gate": np.ascontiguousarray(np.asarray(inputs["L0_ffn_r_gate"], np.float32)),
        "r_up": np.ascontiguousarray(np.asarray(inputs["L0_ffn_r_up"], np.float32)),
        "r_down": np.ascontiguousarray(np.asarray(inputs["L0_ffn_r_down"], np.float32)),
        "router_w": np.ascontiguousarray(np.asarray(inputs["L0_ffn_router_w"], np.float32)),
        "rb_pad": np.concatenate([np.asarray(inputs["L0_ffn_router_b"], np.float32).ravel(),
                                  np.float32([-1e9])]).reshape(1, 8),
    }
    base.update(c)
    in_maps = []
    for core in range(8):
        m = dict(base)
        m["x0"] = np.ascontiguousarray(x[core])
        in_maps.append(m)
    return in_maps


def kernel(**inputs):
    nc, _, _ = build_nc(taps=())
    in_maps = make_inmaps(inputs)
    res = run_bass_kernel_spmd(nc, in_maps, core_ids=list(range(8)))
    out = np.stack([np.asarray(res.results[i]["out"], np.float32) for i in range(8)], axis=0)
    return out


# version 08
# speedup vs baseline: 1.1674x; 1.1674x over previous
import math
import numpy as np
import ml_dtypes
import concourse.bass as bass
import concourse.mybir as mybir
from concourse.bass_utils import run_bass_kernel_spmd
from contextlib import ExitStack

F32 = mybir.dt.float32
BF16 = mybir.dt.bfloat16
U8 = mybir.dt.uint8
AF = mybir.ActivationFunctionType
ALU = mybir.AluOpType
AX = mybir.AxisListType

S = 2048
D = 2048
NH = 16
NT = 16
NG = 4
LAT = 512
NE = 7
MOE_I = 1024
EPS = 1e-5
SCALE = 1.0 / math.sqrt(128.0)
NEGINF = -1e30
ENGINES = ("tensor", "vector", "scalar", "sync", "gpsimd")
KB = 1024


NPOOL = 12


class Op:
    __slots__ = ("eng", "fn", "reads", "writes", "dma", "needs_inc", "sem_val",
                 "deps", "tag", "slot", "use")
    def __init__(self, eng, fn, reads, writes, dma, tag):
        self.eng = eng
        self.fn = fn
        self.reads = reads
        self.writes = writes
        self.dma = dma
        self.needs_inc = False
        self.sem_val = 0
        self.deps = ()
        self.tag = tag
        self.slot = -1
        self.use = 0


class Prog:
    def __init__(self):
        self.ops = []
        self.dma_n = {"sync": 0, "gpsimd": 0}

    def emit(self, eng, fn, reads=(), writes=(), dma=False, tag=""):
        op = Op(eng, fn, tuple(reads), tuple(writes), dma, tag)
        if dma:
            idx = self.dma_n[eng]
            self.dma_n[eng] = idx + 1
            op.slot = idx % NPOOL
            op.use = idx // NPOOL
        self.ops.append(op)
        return op

    def barrier(self):
        for e in ENGINES:
            self.emit(e, None, tag="BARRIER")

    def link(self):
        last_writer = {}
        readers = {}
        last_op = {e: None for e in ENGINES}
        last_dma = {}
        for op in self.ops:
            deps = set()
            if op.tag == "BARRIER":
                for e, o in last_op.items():
                    if o is not None and e != op.eng:
                        deps.add(o)
                for (q, s), o in last_dma.items():
                    deps.add(o)
            else:
                for k in op.reads:
                    w = last_writer.get(k)
                    if w is not None:
                        same = w.eng == op.eng and not w.dma and not op.dma
                        if not (same and op.eng == "tensor"):
                            deps.add(w)
                for k in op.writes:
                    w = last_writer.get(k)
                    if w is not None and not (w.eng == op.eng and not w.dma and not op.dma):
                        deps.add(w)
                    for r in readers.get(k, ()):
                        if not (r.eng == op.eng and not r.dma and not op.dma and op.eng == "tensor"):
                            deps.add(r)
                for k in op.reads:
                    readers.setdefault(k, []).append(op)
                for k in op.writes:
                    last_writer[k] = op
                    readers[k] = []
            op.deps = tuple(deps)
            for d in deps:
                d.needs_inc = True
            if op.dma:
                last_dma[(op.eng, op.slot)] = op
            else:
                last_op[op.eng] = op

    def assign(self):
        counts = {e: 0 for e in ENGINES}
        for op in self.ops:
            if op.dma:
                op.sem_val = 16 * (op.use + 1)
            elif op.needs_inc:
                counts[op.eng] += 1
                op.sem_val = counts[op.eng]
        return counts

    def prepare(self):
        self.by_eng = {e: [] for e in ENGINES}
        for op in self.ops:
            self.by_eng[op.eng].append(op)
        self.marks = {}

    def _wait(self, eng, engname, sem_key, sem_handle, val):
        mk = self.marks.setdefault((engname, sem_key), [0])
        if val > mk[0]:
            eng.wait_ge(sem_handle, val)
            mk[0] = val

    def mat_engine(self, engname, eng, sems, dsems):
        for op in self.by_eng[engname]:
            for d in op.deps:
                if d.dma:
                    self._wait(eng, engname, (d.eng, d.slot), dsems[d.eng][d.slot], d.sem_val)
                else:
                    self._wait(eng, engname, d.eng, sems[d.eng], d.sem_val)
            if op.dma and op.use > 0:
                self._wait(eng, engname, (engname, op.slot), dsems[engname][op.slot],
                           16 * op.use)
            if op.fn is not None:
                inst = op.fn(eng)
                if op.dma:
                    inst.then_inc(dsems[engname][op.slot], 16)
                elif op.needs_inc:
                    assert inst is not None, f"op {op.tag} needs inc, returned None"
                    inst.then_inc(sems[engname], 1)
            elif op.needs_inc:
                eng.nop().then_inc(sems[engname], 1)


def host_consts():
    c = {}
    c["IDN"] = np.eye(128, dtype=ml_dtypes.bfloat16)
    R1 = np.zeros((128, 128), dtype=np.float32)
    for m in range(64):
        R1[m + 64, m] = -1.0
        R1[m, m + 64] = 1.0
    c["R1M"] = R1.astype(ml_dtypes.bfloat16)
    RM = np.zeros((128, 128), dtype=np.float32)
    for h in range(2):
        o = h * 64
        for i in range(32):
            RM[o + i + 32, o + i] = -1.0
            RM[o + i, o + i + 32] = 1.0
    c["RMM"] = RM.astype(ml_dtypes.bfloat16)
    mk = np.zeros((128, 128), dtype=np.float32)
    mk[np.triu_indices(128, 1)] = NEGINF
    c["MASKC"] = mk
    t = np.arange(S, dtype=np.float64)
    w1 = 10000.0 ** (-(np.arange(64, dtype=np.float64)) / 64.0)
    ang1 = np.outer(np.concatenate([w1, w1]), t)
    c["COS1"] = np.cos(ang1).astype(np.float32)
    c["SIN1"] = np.sin(ang1).astype(np.float32)
    wM = 10000.0 ** (-(np.arange(32, dtype=np.float64)) / 32.0)
    angM = np.outer(np.concatenate([wM] * 4), t)
    c["COSM"] = np.cos(angM).astype(np.float32)
    c["SINM"] = np.sin(angM).astype(np.float32)
    sel = np.zeros((8, 8 * 128), dtype=np.float32)
    for e in range(8):
        sel[e, e * 128:(e + 1) * 128] = 0.125
    c["SEL8"] = sel
    c["IDNF"] = np.eye(128, dtype=np.float32)
    return c


def build_nc(taps=(), phases="ABCDEFG"):
    nc = bass.Bass("TRN2", target_bir_lowering=False)
    dram = {}

    def din(name, shape, dt=F32):
        dram[name] = nc.dram_tensor(name, shape, dt, kind="ExternalInput").ap()
        return dram[name]

    def dint(name, shape, dt):
        dram[name] = nc.dram_tensor(name, shape, dt, kind="Internal").ap()
        return dram[name]

    din("x0", [S, D])
    din("w_qkv", [D, 3 * D])
    din("w_o1", [D, D])
    din("ln1_w", [1, D])
    din("ln2_w", [1, D])
    din("kv_d_w", [D, LAT])
    din("q_d_w", [D, LAT])
    din("k_u_w", [LAT, 1024])
    din("v_u_w", [LAT, D])
    din("q_u_w", [LAT, 1024])
    din("rope_k_w", [D, 1024])
    din("rope_q_w", [LAT, 1024])
    din("o_w", [D, D])
    din("sh_gate", [D, MOE_I])
    din("sh_up", [D, MOE_I])
    din("sh_down", [MOE_I, D])
    din("r_gate", [NE, D, MOE_I])
    din("r_up", [NE, D, MOE_I])
    din("r_down", [NE, MOE_I, D])
    din("router_w", [D, NE])
    din("rb_pad", [1, 8])
    din("IDN", [128, 128], BF16)
    din("R1M", [128, 128], BF16)
    din("RMM", [128, 128], BF16)
    din("MASKC", [128, 128])
    din("COS1", [128, S])
    din("SIN1", [128, S])
    din("COSM", [128, S])
    din("SINM", [128, S])
    din("SEL8", [8, 8 * 128])
    din("IDNF", [128, 128])
    dint("sQ1", [NH, 128, S], BF16)
    dint("sK1", [NH, 128, S], BF16)
    dint("sV1", [NH, 128, NT, 128], BF16)
    dint("sQM", [NH, 128, S], BF16)
    dint("sKM", [NH, 128, S], BF16)
    dint("sVM", [NH, 128, NT, 128], BF16)
    dint("sX1", [NT, 128, D], F32)
    dint("sY", [NT, 128, D], F32)
    dint("sH", [NT, 128, D], F32)
    dint("sAT1", [128, NT, S], BF16)
    dint("sMF", [NG, 128, NT, 512], BF16)
    dint("sR1F", [128, NT, S], BF16)
    dint("sKVF", [128, 4, S], BF16)
    out = nc.dram_tensor("out", [S, D], F32, kind="ExternalOutput").ap()
    dram["out"] = out
    dram["chk"] = nc.dram_tensor("zzchk", [128, D], F32, kind="ExternalOutput").ap()

    tap_aps = {}
    for tp in taps:
        if tp in ("R1F", "KVF"):
            srcd = dram["s" + tp]
            tap_aps[tp] = nc.dram_tensor("tap_" + tp, list(srcd.shape), srcd.dtype,
                                         kind="ExternalOutput").ap()
            continue
        if tp == "XB":
            tap_aps[tp] = nc.dram_tensor("tap_XB", [128, NT, S], BF16,
                                         kind="ExternalOutput").ap()
        else:
            srcd = dram["s" + tp]
            tap_aps[tp] = nc.dram_tensor("tap_" + tp, list(srcd.shape), srcd.dtype,
                                         kind="ExternalOutput").ap()

    prog = Prog()
    P = prog.emit

    with ExitStack() as ctx:
        def sbuf(name, shape, dt):
            return ctx.enter_context(nc.sbuf_tensor(name, shape, dt))
        def psum(name, shape, dt):
            return ctx.enter_context(nc.psum_tensor(name, shape, dt))

        idn = sbuf("idn", [128, 128], BF16)
        idnf = sbuf("idnf", [128, 128], F32)
        r1m = sbuf("r1m", [128, 128], BF16)
        rmm = sbuf("rmm", [128, 128], BF16)
        maskc = sbuf("maskc", [128, 128], F32)
        sel8 = sbuf("sel8", [8, 8 * 128], F32)
        rbp = sbuf("rbp", [1, 8], F32)
        stt = sbuf("stt", [128, 64], F32)
        sumsM = sbuf("sumsM", [128, NT], F32)
        recM = sbuf("recM", [128, NT], F32)
        wT = sbuf("wT", [8, S], F32)
        probs8 = sbuf("probs8", [128, 8], F32)
        scr8 = sbuf("scr8", [128, 8], F32)
        scr8b = sbuf("scr8b", [128, 8], F32)
        cosb = sbuf("cosb", [128, S], F32)
        sinb = sbuf("sinb", [128, S], F32)

        ARENA_BYTES = 176 * KB
        arena = sbuf("arena", [128, ARENA_BYTES], U8)

        def carve(off_kb, shape, dt):
            nbytes = int(np.prod(shape[1:])) * (2 if dt == BF16 else 4)
            assert off_kb * KB + nbytes <= ARENA_BYTES, (off_kb, nbytes)
            ap = arena[:, off_kb * KB: off_kb * KB + nbytes].bitcast(dt)
            if len(shape) == 3:
                ap = ap.rearrange("p (a b) -> p a b", a=shape[1])
            elif len(shape) == 4:
                ap = ap.rearrange("p (a b c) -> p a b c", a=shape[1], b=shape[2])
            return ap

        ps_mm = psum("ps_mm", [128, 2048], F32)
        ps_mm2 = psum("ps_mm2", [128, 2048], F32)
        ptb = ps_mm2[:, 1024:2048].bitcast(BF16)

        sems = {e: ctx.enter_context(nc.semaphore("sem_" + e)) for e in ENGINES}
        dsems = {q: [ctx.enter_context(nc.semaphore(f"dsem_{q}{i}")) for i in range(NPOOL)]
                 for q in ("sync", "gpsimd")}

        def finish():
            prog.link()
            counts = prog.assign()
            prog.prepare()
            with nc.Block() as block:
                @block.tensor
                def _(t):
                    prog.mat_engine("tensor", t, sems, dsems)
                @block.vector
                def _(v):
                    prog.mat_engine("vector", v, sems, dsems)
                @block.scalar
                def _(s):
                    prog.mat_engine("scalar", s, sems, dsems)
                @block.sync
                def _(sy):
                    prog.mat_engine("sync", sy, sems, dsems)
                @block.gpsimd
                def _(g):
                    prog.mat_engine("gpsimd", g, sems, dsems)
            return counts

        def dma(queue, dst, src, reads, writes, tag=""):
            P(queue, lambda q, dst=dst, src=src: q.dma_start(out=dst, in_=src),
              reads=reads, writes=writes, dma=True, tag=tag)

        def load_weight_cast(dst_ap2, w_dram, row0, nrows, col0, ncols, key, tag, queue="gpsimd"):
            nk = nrows // 128
            for k in range(nk):
                src = w_dram[row0 + k * 128: row0 + (k + 1) * 128, col0: col0 + ncols]
                dst = dst_ap2[:, k * ncols: (k + 1) * ncols]
                dma(queue, dst, src, (), (key + (k,),), tag + f"k{k}")

        def ln_stats(src_ap, src_keys, sq_dump, sq_key, col, n, tag=""):
            sq_keys = sq_key if isinstance(sq_key, tuple) and sq_key and isinstance(sq_key[0], tuple) else (sq_key,)
            ssum = stt[:, col + 2: col + 3]
            ssq = stt[:, col + 3: col + 4]
            P("vector", lambda v: v.reduce_sum(ssum, src_ap, axis=AX.X),
              reads=src_keys, writes=(("stt", col, "sum"),), tag=tag + "sum")
            P("scalar", lambda s: s.activation(sq_dump, src_ap, AF.Square, bias=0.0,
                                               scale=1.0, accum_out=ssq),
              reads=src_keys + (("stt", col, "sum"),),
              writes=sq_keys + (("stt", col, "sq"),), tag=tag + "sq")
            P("vector", lambda v: v.tensor_scalar(stt[:, col:col + 1], ssum, 1.0 / n, None,
                                                  op0=ALU.mult),
              reads=(("stt", col, "sum"),), writes=(("stt", col, "mean"),), tag=tag + "mean")
            P("vector", lambda v: v.tensor_scalar(ssum, stt[:, col:col + 1],
                                                  stt[:, col:col + 1], None, op0=ALU.mult),
              reads=(("stt", col, "mean"),), writes=(("stt", col, "m2"),), tag=tag + "m2")
            P("vector", lambda v: v.tensor_scalar(ssq, ssq, 1.0 / n, None, op0=ALU.mult),
              reads=(("stt", col, "sq"),), writes=(("stt", col, "sqn"),), tag=tag + "sqn")
            P("vector", lambda v: v.tensor_tensor(ssq, ssq, ssum, op=ALU.subtract),
              reads=(("stt", col, "sqn"), ("stt", col, "m2")), writes=(("stt", col, "var"),),
              tag=tag + "var")
            P("vector", lambda v: v.tensor_scalar(ssq, ssq, EPS, None, op0=ALU.add),
              reads=(("stt", col, "var"),), writes=(("stt", col, "ve"),), tag=tag + "ve")
            P("scalar", lambda s: s.activation(ssum, ssq, AF.Sqrt, bias=0.0, scale=1.0),
              reads=(("stt", col, "ve"),), writes=(("stt", col, "sd"),), tag=tag + "sd")
            P("vector", lambda v: v.reciprocal(stt[:, col + 1:col + 2], ssum),
              reads=(("stt", col, "sd"),), writes=(("stt", col, "rstd"),), tag=tag + "rstd")

        def rms_stats(src_ap, src_keys, sq_dump, sq_key, col, n, tag=""):
            sq_keys = sq_key if isinstance(sq_key, tuple) and sq_key and isinstance(sq_key[0], tuple) else (sq_key,)
            ssum = stt[:, col + 2: col + 3]
            ssq = stt[:, col + 3: col + 4]
            P("scalar", lambda s: s.activation(sq_dump, src_ap, AF.Square, bias=0.0,
                                               scale=1.0, accum_out=ssq),
              reads=src_keys, writes=sq_keys + (("stt", col, "sq"),), tag=tag + "sq")
            P("vector", lambda v: v.tensor_scalar(ssq, ssq, 1.0 / n, EPS, op0=ALU.mult,
                                                  op1=ALU.add),
              reads=(("stt", col, "sq"),), writes=(("stt", col, "ve"),), tag=tag + "ve")
            P("scalar", lambda s: s.activation(ssum, ssq, AF.Sqrt, bias=0.0, scale=1.0),
              reads=(("stt", col, "ve"),), writes=(("stt", col, "sd"),), tag=tag + "sd")
            P("vector", lambda v: v.reciprocal(stt[:, col + 1:col + 2], ssum),
              reads=(("stt", col, "sd"),), writes=(("stt", col, "rstd"),), tag=tag + "rstd")

        class Chain(list):
            def add(self, eng, fn, reads=(), writes=(), tag="", dma=False):
                self.append((eng, fn, reads, writes, tag, dma))

        def emit_zip(*chains):
            idx = [0] * len(chains)
            while True:
                alive = False
                for ci, c in enumerate(chains):
                    if idx[ci] < len(c):
                        eng, fn, reads, writes, tag, dma_ = c[idx[ci]]
                        P(eng, fn, reads=reads, writes=writes, tag=tag, dma=dma_)
                        idx[ci] += 1
                        alive = True
                if not alive:
                    break

        dma("sync", idn[:], dram["IDN"], (), (("idn",),), "c")
        dma("sync", idnf[:], dram["IDNF"], (), (("idnf",),), "c")
        dma("sync", r1m[:], dram["R1M"], (), (("rot1",),), "c")
        dma("sync", rmm[:], dram["RMM"], (), (("rotM",),), "c")
        dma("sync", maskc[:], dram["MASKC"], (), (("maskc",),), "c")
        dma("sync", sel8[:], dram["SEL8"], (), (("sel8",),), "c")
        dma("sync", rbp[:], dram["rb_pad"], (), (("rbp",),), "c")
        dma("sync", cosb[:], dram["COS1"], (), (("cosb",),), "c")
        dma("sync", sinb[:], dram["SIN1"], (), (("sinb",),), "c")

        x0r = dram["x0"].rearrange("(t p) d -> t p d", p=128)

        ln0_fm = carve(0, [128, NT, S], BF16)
        sqsc = carve(64, [128, D], F32)
        x0t_a = [carve(72 + 8 * i, [128, D], F32) for i in range(2)]
        ln0bf = [carve(88 + 4 * i, [128, D], BF16) for i in range(4)]
        wblkQ = [carve(72 + 32 * i, [128, 16, 1024], BF16) for i in range(2)]
        qraw = [carve(136 + 1 * i, [128, 512], BF16) for i in range(2)]
        tmp1 = [carve(138 + 2 * i, [128, 512], F32) for i in range(2)]
        tmp2 = [carve(142 + 2 * i, [128, 512], F32) for i in range(2)]
        qout = [carve(146 + 1 * i, [128, 512], BF16) for i in range(2)]
        vout = [carve(148 + 1 * i, [128, 512], BF16) for i in range(2)]

        for g in range(NG):
            for ti in range(4):
                t = g * 4 + ti
                sl = t % 2
                xk = ("x0t", sl)
                dma("sync", x0t_a[sl], x0r[t], (), (xk,), f"A0x{t}")
                col = sl * 8
                ln_stats(x0t_a[sl], (xk,), sqsc, ("sqsc",), col, D, tag=f"A0s{t}")
                P("vector", lambda v, sl=sl, col=col, ti=ti: v.tensor_scalar(
                    ln0bf[ti], x0t_a[sl], stt[:, col:col + 1], stt[:, col + 1:col + 2],
                    op0=ALU.subtract, op1=ALU.mult),
                  reads=(xk, ("stt", col, "mean"), ("stt", col, "rstd")),
                  writes=(("ln0bf", ti),), tag=f"A0a{t}")
            for c in range(NT):
                bank = c % 2
                pk = ("ptA", bank)
                for ti in range(4):
                    P("tensor", lambda tn, c=c, ti=ti, bank=bank: tn.transpose(
                        ptb[:, bank * 1024 + ti * 128: bank * 1024 + (ti + 1) * 128],
                        ln0bf[ti][:, c * 128:(c + 1) * 128], idn[:]),
                      reads=(("ln0bf", ti), ("idn",)), writes=(pk,), tag=f"A0t{g}c{c}")
                P("vector", lambda v, c=c, g=g, bank=bank: v.tensor_copy(
                    ln0_fm[:, c, g * 512:(g + 1) * 512],
                    ptb[:, bank * 1024: bank * 1024 + 512]),
                  reads=(pk,), writes=(("ln0_fm", c, g),), tag=f"A0e{g}c{c}")

        ln0_all = tuple(("ln0_fm", c, g) for c in range(NT) for g in range(NG))
        prog.barrier()

        def rope_evac(uid, g, bank, rot_ap, rot_key, scr_name, head, tag):
            sl = uid % 2
            pk = ("psA", bank)
            qk = ("qraw", sl)
            P("scalar", lambda s, sl=sl, bank=bank: s.activation(
                qraw[sl], ps_mm[:, bank * 512:(bank + 1) * 512], AF.Identity,
                bias=0.0, scale=1.0),
              reads=(pk,), writes=(qk,), tag=tag + "ev")
            rk = ("psrot", sl)
            P("tensor", lambda tn, sl=sl, rot_ap=rot_ap: tn.matmul(
                ps_mm2[:, sl * 512:(sl + 1) * 512], rot_ap, qraw[sl], start=True, stop=True),
              reads=(qk, (rot_key,)), writes=(rk,), tag=tag + "rot")
            P("vector", lambda v, sl=sl, g=g: v.tensor_tensor(
                tmp1[sl], qraw[sl], cosb[:, g * 512:(g + 1) * 512], op=ALU.mult),
              reads=(qk, ("cosb",)), writes=(("tmp1", sl),), tag=tag + "m1")
            P("vector", lambda v, sl=sl, g=g: v.tensor_tensor(
                tmp2[sl], ps_mm2[:, sl * 512:(sl + 1) * 512],
                sinb[:, g * 512:(g + 1) * 512], op=ALU.mult),
              reads=(rk, ("sinb",)), writes=(("tmp2", sl),), tag=tag + "m2")
            P("vector", lambda v, sl=sl: v.tensor_tensor(
                qout[sl], tmp1[sl], tmp2[sl], op=ALU.add),
              reads=(("tmp1", sl), ("tmp2", sl)), writes=(("qout", sl),), tag=tag + "ad")
            dma("sync", dram[scr_name][head][:, g * 512:(g + 1) * 512], qout[sl],
                (("qout", sl),), ((scr_name, head, g),), tag + "st")

        uid = 0
        for j in range(4):
            sl = j % 2
            wk = ("wblkQ", sl)
            load_weight_cast(wblkQ[sl].rearrange("p a b -> p (a b)"), dram["w_qkv"],
                             0, D, j * 1024, 1024, wk, f"A1w{j}")
            for oc in range(8):
                col_glob = j * 1024 + oc * 128
                is_k = col_glob >= 2048
                head = (col_glob % 2048) // 128
                scr_name = "sK1" if is_k else "sQ1"
                for g in range(NG):
                    bank = uid % 4
                    pk = ("psA", bank)
                    for k in range(NT):
                        P("tensor", lambda tn, sl=sl, k=k, oc=oc, g=g, bank=bank,
                          first=(k == 0), last=(k == NT - 1): tn.matmul(
                            ps_mm[:, bank * 512:(bank + 1) * 512],
                            wblkQ[sl][:, k, oc * 128:(oc + 1) * 128],
                            ln0_fm[:, k, g * 512:(g + 1) * 512],
                            start=first, stop=last),
                          reads=(wk + (k,),) + (ln0_all if k == 0 else ()),
                          writes=(pk,), tag=f"A1m{j}o{oc}g{g}k{k}")
                    rope_evac(uid, g, bank, r1m[:], "rot1", scr_name, head,
                              f"A1r{j}o{oc}g{g}")
                    uid += 1

        for j in range(2):
            sl = j % 2
            wk = ("wblkQ", sl)
            load_weight_cast(wblkQ[sl].rearrange("p a b -> p (a b)"), dram["w_qkv"],
                             0, D, 4096 + j * 1024, 1024, wk, f"A2w{j}")
            for cg in range(2):
                for t in range(NT):
                    bank = (cg * NT + t) % 4
                    pk = ("psA", bank)
                    for k in range(NT):
                        P("tensor", lambda tn, sl=sl, k=k, t=t, cg=cg, bank=bank,
                          first=(k == 0), last=(k == NT - 1): tn.matmul(
                            ps_mm[:, bank * 512:(bank + 1) * 512],
                            ln0_fm[:, k, t * 128:(t + 1) * 128],
                            wblkQ[sl][:, k, cg * 512:(cg + 1) * 512],
                            start=first, stop=last),
                          reads=(wk + (k,),) + (ln0_all if k == 0 else ()),
                          writes=(pk,), tag=f"A2m{j}c{cg}t{t}k{k}")
                    vsl = (t + cg * NT) % 2
                    P("scalar", lambda s, bank=bank, vsl=vsl: s.activation(
                        vout[vsl], ps_mm[:, bank * 512:(bank + 1) * 512],
                        AF.Identity, bias=0.0, scale=1.0),
                      reads=(pk,), writes=(("vout", vsl),), tag=f"A2e{j}c{cg}t{t}")
                    h0 = (j * 1024 + cg * 512) // 128
                    dst = dram["sV1"][h0:h0 + 4, :, t, :].rearrange("h p f -> p h f")
                    src = vout[vsl].rearrange("p (h f) -> p h f", h=4)
                    dma("sync", dst, src, (("vout", vsl),), (("sV1", t, h0),),
                        f"A2s{j}c{cg}t{t}")

        prog.barrier()

        def attention_phase(scrQ, scrK, scrV, attn_fm, akey, with_sums, ph):
            Qh = [carve(64 + 4 * i, [128, S], BF16) for i in range(4)]
            Kh = [carve(80 + 4 * i, [128, S], BF16) for i in range(4)]
            Vh = [carve(96 + 4 * i, [128, NT, 128], BF16) for i in range(4)]
            Psb = [carve(112 + 4 * i, [128, S], BF16) for i in range(4)]
            PTg = [carve(128 + 16 * i, [128, 16, 512], BF16) for i in range(2)]
            tc = [0]

            def load_head(h):
                sl = h % 4
                dma("sync", Qh[sl][:, :], dram[scrQ][h], (), (("Qh", sl),), f"{ph}lq{h}")
                dma("sync", Kh[sl][:, :], dram[scrK][h], (), (("Kh", sl),), f"{ph}lk{h}")
                dma("sync", Vh[sl], dram[scrV][h], (), (("Vh", sl),), f"{ph}lv{h}")

            def sm_chain(h, qt):
                c = Chain()
                sl = h % 4
                hp2 = h % 2
                ctx = hp2 * 2 + (qt % 2)
                base = ctx * 16
                qsl = ctx
                ext = (qt + 1) * 128
                np_ = (ext + 511) // 512
                for j in range(np_):
                    w = min(512, ext - j * 512)
                    b = hp2 * 2 + (j % 2)
                    is_diag = (j == np_ - 1)
                    c.add("tensor", lambda tn, sl=sl, qt=qt, j=j, w=w, b=b: tn.matmul(
                        ps_mm[:, b * 512: b * 512 + w],
                        Qh[sl][:, qt * 128:(qt + 1) * 128],
                        Kh[sl][:, j * 512: j * 512 + w],
                        start=True, stop=True),
                      reads=(("Qh", sl), ("Kh", sl)), writes=(("psS", b),),
                      tag=f"{ph}s{h}q{qt}p{j}")
                    if is_diag:
                        c.add("vector", lambda v, b=b, w=w: v.tensor_tensor(
                            ps_mm[:, b * 512 + w - 128: b * 512 + w],
                            ps_mm[:, b * 512 + w - 128: b * 512 + w], maskc[:], op=ALU.add),
                          reads=(("psS", b), ("maskc",)), writes=(("psS", b),),
                          tag=f"{ph}mk{h}q{qt}")
                    c.add("scalar", lambda s, b=b, j=j, w=w, base=base, qsl=qsl:
                      s.activation(Psb[qsl][:, j * 512: j * 512 + w],
                                   ps_mm[:, b * 512: b * 512 + w], AF.Exp,
                                   bias=0.0, scale=SCALE,
                                   accum_out=stt[:, base + 8 + j: base + 9 + j]),
                      reads=(("psS", b),),
                      writes=(("Psb", qsl), ("att", base, "ps", j)),
                      tag=f"{ph}ex{h}q{qt}p{j}")
                psk = tuple(("att", base, "ps", j) for j in range(np_))
                if np_ == 1:
                    c.add("vector", lambda v, base=base: v.reciprocal(
                        stt[:, base + 14: base + 15], stt[:, base + 8: base + 9]),
                      reads=psk, writes=(("att", base, "rc"),), tag=f"{ph}rc{h}q{qt}")
                else:
                    c.add("vector", lambda v, base=base, np_=np_: v.reduce_sum(
                        stt[:, base + 13: base + 14], stt[:, base + 8: base + 8 + np_],
                        axis=AX.X),
                      reads=psk, writes=(("att", base, "tt"),), tag=f"{ph}ts{h}q{qt}")
                    c.add("vector", lambda v, base=base: v.reciprocal(
                        stt[:, base + 14: base + 15], stt[:, base + 13: base + 14]),
                      reads=(("att", base, "tt"),), writes=(("att", base, "rc"),),
                      tag=f"{ph}rc{h}q{qt}")
                for j in range(np_):
                    w = min(512, ext - j * 512)
                    if j % 2 == 0:
                        c.add("vector", lambda v, qsl=qsl, j=j, w=w, base=base: v.tensor_scalar(
                            Psb[qsl][:, j * 512: j * 512 + w], Psb[qsl][:, j * 512: j * 512 + w],
                            stt[:, base + 14: base + 15], None, op0=ALU.mult),
                          reads=(("Psb", qsl), ("att", base, "rc")), writes=(("Psb", qsl),),
                          tag=f"{ph}pn{h}q{qt}p{j}")
                    else:
                        c.add("scalar", lambda s, qsl=qsl, j=j, w=w, base=base: s.activation(
                            Psb[qsl][:, j * 512: j * 512 + w], Psb[qsl][:, j * 512: j * 512 + w],
                            AF.Identity, bias=0.0, scale=stt[:, base + 14: base + 15]),
                          reads=(("Psb", qsl), ("att", base, "rc")), writes=(("Psb", qsl),),
                          tag=f"{ph}pn{h}q{qt}p{j}")
                return c

            def transposes(h, qt):
                qsl = (h % 2) * 2 + (qt % 2)
                hp2 = h % 2
                for j0 in range(0, qt + 1, 8):
                    nb = min(8, qt + 1 - j0)
                    tb = tc[0] % 2
                    tc[0] += 1
                    tkey = ("ptT", tb)
                    for jj in range(nb):
                        kt = j0 + jj
                        P("tensor", lambda tn, qsl=qsl, kt=kt, jj=jj, tb=tb: tn.transpose(
                            ptb[:, tb * 1024 + jj * 128: tb * 1024 + (jj + 1) * 128],
                            Psb[qsl][:, kt * 128:(kt + 1) * 128], idn[:]),
                          reads=(("Psb", qsl), ("idn",)), writes=(tkey,),
                          tag=f"{ph}t{h}q{qt}j{kt}")
                    P("vector", lambda v, hp2=hp2, qt=qt, j0=j0, nb=nb, tb=tb:
                      v.tensor_copy(
                        PTg[hp2][:, j0:j0 + nb, (qt % 4) * 128:(qt % 4) * 128 + 128],
                        ptb[:, tb * 1024: tb * 1024 + nb * 128].rearrange(
                            "p (a b) -> p a b", a=nb)),
                      reads=(tkey, ("PTgz", hp2)), writes=(("PTg", hp2, qt, j0),),
                      tag=f"{ph}pe{h}q{qt}j{j0}")

            for h in range(4):
                if h < NH:
                    load_head(h)
            for hp in range(0, NH, 2):
                if hp + 3 < NH:
                    load_head(hp + 2)
                    load_head(hp + 3)
                pair = (hp, hp + 1)
                for g in range(NG):
                    nkt = 4 * g + 4
                    for h in pair:
                        hp2 = h % 2
                        P("vector", lambda v, hp2=hp2, g=g: v.memset(
                            PTg[hp2].rearrange("p a b -> p (a b)")[:, 4 * g * 512:(4 * g + 4) * 512], 0.0),
                          reads=(), writes=(("PTgz", hp2),), tag=f"{ph}z{h}g{g}")
                    for qt in range(4 * g, 4 * g + 4):
                        emit_zip(sm_chain(pair[0], qt), sm_chain(pair[1], qt))
                        if qt > 4 * g:
                            for h in pair:
                                transposes(h, qt - 1)
                    for h in pair:
                        transposes(h, 4 * g + 3)
                    for h in pair:
                        hp2 = h % 2
                        sl = h % 4
                        okey = ("psO", hp2)
                        ptreads = tuple(("PTg", hp2, qt, j0) for qt in range(4 * g, 4 * g + 4)
                                        for j0 in range(0, qt + 1, 8)) + (("PTgz", hp2),)
                        for kt in range(nkt):
                            P("tensor", lambda tn, sl=sl, kt=kt, hp2=hp2,
                              first=(kt == 0), last=(kt == nkt - 1): tn.matmul(
                                ps_mm2[:, hp2 * 512:(hp2 + 1) * 512],
                                Vh[sl][:, kt, :], PTg[hp2][:, kt, :],
                                start=first, stop=last),
                              reads=(("Vh", sl),) + (ptreads if (kt == 0 or kt == nkt - 1) else ()),
                              writes=(okey,), tag=f"{ph}pv{h}g{g}k{kt}")
                        P("scalar", lambda s, h=h, g=g, hp2=hp2: s.activation(
                            attn_fm[:, h, g * 512:(g + 1) * 512],
                            ps_mm2[:, hp2 * 512:(hp2 + 1) * 512],
                            AF.Identity, bias=0.0, scale=1.0),
                          reads=(okey,), writes=((akey, h, g),), tag=f"{ph}oe{h}g{g}")

        attn1_fm = carve(0, [128, NT, S], BF16)
        if "B" in phases:
            attention_phase("sQ1", "sK1", "sV1", attn1_fm, "at1", False, "B")

        if "C" in phases:
            wo_hi = carve(160, [128, 4, D], BF16)
            load_weight_cast(wo_hi.rearrange("p a b -> p (a b)"), dram["w_o1"], 12 * 128, 512, 0, D,
                             ("wo", 1), "Cw1")

        prog.barrier()

        if "B" in phases and "XB" in taps:
            dma("sync", dram["sAT1"], attn1_fm, tuple(("at1", h, g) for h in range(NH) for g in range(NG)),
                (("sAT1w",),), "tapB")
            dma("sync", tap_aps["XB"], dram["sAT1"], (("sAT1w",),), (("XB_tap",),), "tapB2")

        if "C" in phases:
            wo_lo = carve(64, [128, 12, D], BF16)
            load_weight_cast(wo_lo.rearrange("p a b -> p (a b)"), dram["w_o1"], 0, 12 * 128, 0, D,
                             ("wo", 0), "Cw0")

        if "C" not in phases:
            prog.barrier()
            for tp in taps:
                if tp == "XB":
                    continue
                dma("sync", tap_aps[tp], dram["s" + tp], (), ((tp + "_tap",),), "tap" + tp)
            prog.barrier()
            return nc, finish(), len(prog.ops)
        x0t_c = [carve(112 + 8 * i, [128, D], F32) for i in range(2)]
        x1t = [carve(128 + 8 * i, [128, D], F32) for i in range(2)]
        yt = carve(144, [128, D], F32)
        sqc = carve(152, [128, D], F32)
        at1_all = tuple(("at1", h, g) for h in range(NH) for g in range(NG))

        for t in range(NT):
            par = t % 2
            xk = ("x0c", par)
            dma("sync", x0t_c[par], x0r[t], (), (xk,), f"Cx{t}")
            pk = ("psC", par)
            pst = ps_mm if par == 0 else ps_mm2
            korder = [12, 13, 14, 15] + list(range(12))
            for cg in range(4):
                for ki, k in enumerate(korder):
                    half = 1 if k >= 12 else 0
                    wbuf = wo_hi if half == 1 else wo_lo
                    P("tensor", lambda tn, t=t, cg=cg, k=k, pst=pst, wbuf=wbuf,
                      first=(ki == 0), last=(ki == NT - 1), ki=ki: tn.matmul(
                        pst[:, cg * 512:(cg + 1) * 512],
                        attn1_fm[:, k, t * 128:(t + 1) * 128],
                        wbuf[:, k % 12 if k < 12 else k - 12, cg * 512:(cg + 1) * 512],
                        start=first, stop=last),
                      reads=(("wo", half, (k % 12) if k < 12 else k - 12),) + (at1_all if (cg == 0 and ki == 0) else ()),
                      writes=(pk,), tag=f"Cm{t}c{cg}k{k}")
            col = 40 + par * 8
            ln_stats(pst[:, 0:D], (pk,), sqc, ("sqc",), col, D, tag=f"Cs{t}")
            P("vector", lambda v, pst=pst, col=col, par=par: v.tensor_scalar(
                x1t[par], pst[:, 0:D], stt[:, col:col + 1], stt[:, col + 1:col + 2],
                op0=ALU.subtract, op1=ALU.mult),
              reads=(pk, ("stt", col, "mean"), ("stt", col, "rstd")),
              writes=(("x1t", par),), tag=f"Ca{t}")
            P("vector", lambda v, par=par: v.tensor_tensor(
                x1t[par], x1t[par], x0t_c[par], op=ALU.add),
              reads=(("x1t", par), xk), writes=(("x1t", par),), tag=f"Cr{t}")
            dma("sync", dram["sX1"][t], x1t[par], (("x1t", par),), (("sX1", t),), f"Cox{t}")
            col2 = 56 + par * 4
            ln_stats(x1t[par], (("x1t", par),), sqc, ("sqc",), col2, D, tag=f"Cy{t}")
            P("vector", lambda v, par=par, col2=col2: v.tensor_scalar(
                yt, x1t[par], stt[:, col2:col2 + 1], stt[:, col2 + 1:col2 + 2],
                op0=ALU.subtract, op1=ALU.mult),
              reads=(("x1t", par), ("stt", col2, "mean"), ("stt", col2, "rstd")),
              writes=(("yt",),), tag=f"Cyap{t}")
            dma("sync", dram["sY"][t], yt, (("yt",),), (("sY", t),), f"Coy{t}")

        prog.barrier()

        if "D" in phases:
            dma("sync", cosb[:], dram["COSM"], (("cosb",),), (("cosb",),), "Dc")
            dma("sync", sinb[:], dram["SINM"], (("sinb",),), (("sinb",),), "Ds")
            r1_fm = carve(0, [128, NT, S], BF16)
            lnw1_bc = carve(64, [128, D], F32)
            ytD = [carve(72 + 8 * i, [128, D], F32) for i in range(2)]
            r1bf = [carve(88 + 4 * i, [128, D], BF16) for i in range(4)]
            sqD = carve(104, [128, D], F32)
            wkvd = carve(112, [128, NT, 512], BF16)
            wqd = carve(128, [128, NT, 512], BF16)
            kv_fm = carve(144, [128, 4, S], BF16)
            qd_fm = carve(160, [128, 4, S], BF16)

            dma("sync", lnw1_bc, dram["ln1_w"][0:1, :].partition_broadcast(128).squeeze(1),
                (), (("lnw1",),), "Dlnw")
            load_weight_cast(wkvd.rearrange("p a b -> p (a b)"), dram["kv_d_w"], 0, D, 0, 512,
                             ("wkvd",), "Dwk")
            load_weight_cast(wqd.rearrange("p a b -> p (a b)"), dram["q_d_w"], 0, D, 0, 512,
                             ("wqd",), "Dwq")

            for g in range(NG):
                for ti in range(4):
                    t = g * 4 + ti
                    sl = t % 2
                    yk = ("ytD", sl)
                    dma("sync", ytD[sl], dram["sY"][t], (), (yk,), f"Dy{t}")
                    col = sl * 8
                    rms_stats(ytD[sl], (yk,), sqD, ("sqD",), col, D, tag=f"Dr{t}")
                    P("vector", lambda v, sl=sl, col=col: v.tensor_scalar(
                        sqD, ytD[sl], stt[:, col + 1:col + 2], None, op0=ALU.mult),
                      reads=(yk, ("stt", col, "rstd")), writes=(("sqD",),), tag=f"Dm1{t}")
                    P("vector", lambda v, ti=ti: v.tensor_tensor(
                        r1bf[ti], sqD, lnw1_bc, op=ALU.mult),
                      reads=(("sqD",), ("lnw1",)), writes=(("r1bf", ti),), tag=f"Dm2{t}")
                for c in range(NT):
                    bank = c % 2
                    pk = ("ptA", bank)
                    for ti in range(4):
                        P("tensor", lambda tn, c=c, ti=ti, bank=bank: tn.transpose(
                            ptb[:, bank * 1024 + ti * 128: bank * 1024 + (ti + 1) * 128],
                            r1bf[ti][:, c * 128:(c + 1) * 128], idn[:]),
                          reads=(("r1bf", ti), ("idn",)), writes=(pk,), tag=f"Dt{g}c{c}")
                    P("vector", lambda v, c=c, g=g, bank=bank: v.tensor_copy(
                        r1_fm[:, c, g * 512:(g + 1) * 512],
                        ptb[:, bank * 1024: bank * 1024 + 512]),
                      reads=(pk,), writes=(("r1_fm", c, g),), tag=f"De{g}c{c}")

            r1_all = tuple(("r1_fm", c, g) for c in range(NT) for g in range(NG))

            uidD = 0
            for which, wbuf, wkey, dstb, dkey in (
                    (0, wkvd, ("wkvd",), kv_fm, "kv_fm"), (1, wqd, ("wqd",), qd_fm, "qd_fm")):
                for oc in range(4):
                    for g in range(NG):
                        bank = uidD % 4
                        pk = ("psA", bank)
                        for k in range(NT):
                            P("tensor", lambda tn, wbuf=wbuf, k=k, oc=oc, g=g, bank=bank,
                              first=(k == 0), last=(k == NT - 1): tn.matmul(
                                ps_mm[:, bank * 512:(bank + 1) * 512],
                                wbuf[:, k, oc * 128:(oc + 1) * 128],
                                r1_fm[:, k, g * 512:(g + 1) * 512],
                                start=first, stop=last),
                              reads=(wkey + (k,),) + (r1_all if k == 0 else ()),
                              writes=(pk,), tag=f"Dp{which}o{oc}g{g}k{k}")
                        P("scalar", lambda s, dstb=dstb, oc=oc, g=g, bank=bank: s.activation(
                            dstb[:, oc, g * 512:(g + 1) * 512],
                            ps_mm[:, bank * 512:(bank + 1) * 512],
                            AF.Identity, bias=0.0, scale=1.0),
                          reads=(pk,), writes=((dkey, oc, g),), tag=f"Dpe{which}o{oc}g{g}")
                        uidD += 1

            if "R1F" in taps:
                dma("sync", dram["sR1F"], r1_fm, r1_all, (("sR1Fw",),), "tapR1F")
                dma("sync", tap_aps["R1F"], dram["sR1F"], (("sR1Fw",),), (("R1F_tap",),), "tapR1F2")
            if "KVF" in taps:
                dma("sync", dram["sKVF"], kv_fm, kv_all0 if False else tuple(("kv_fm", oc, g) for oc in range(4) for g in range(NG)), (("sKVFw",),), "tapKVF")
                dma("sync", tap_aps["KVF"], dram["sKVF"], (("sKVFw",),), (("KVF_tap",),), "tapKVF2")

            prog.barrier()

            wrk = carve(64, [128, NT, 1024], BF16)
            qraw2 = [carve(104 + 1 * i, [128, 512], BF16) for i in range(2)]
            tmp1b = [carve(106 + 2 * i, [128, 512], F32) for i in range(2)]
            tmp2b = [carve(110 + 2 * i, [128, 512], F32) for i in range(2)]
            qout2 = [carve(114 + 1 * i, [128, 512], BF16) for i in range(2)]
            vout2 = [carve(116 + 1 * i, [128, 512], BF16) for i in range(2)]
            load_weight_cast(wrk.rearrange("p a b -> p (a b)"), dram["rope_k_w"], 0, D, 0, 1024,
                             ("wrk",), "D2wr")

            kv_all = tuple(("kv_fm", oc, g) for oc in range(4) for g in range(NG))
            qd_all = tuple(("qd_fm", oc, g) for oc in range(4) for g in range(NG))

            def mla_unit(uid, nk, mm_fn, wkeys, rhs_all, g, oc, do_rope, scr_name, row0, tag):
                c = Chain()
                sl = uid % 2
                bank = uid % 4
                pk = ("psA", bank)
                for k in range(nk):
                    c.add("tensor", lambda tn, k=k, bank=bank, first=(k == 0),
                          last=(k == nk - 1): mm_fn(tn, k, bank, first, last),
                          reads=(wkeys[k],) + (rhs_all if k == 0 else ()),
                          writes=(pk,), tag=tag + f"k{k}")
                c.add("scalar", lambda s, sl=sl, bank=bank: s.activation(
                    qraw2[sl], ps_mm[:, bank * 512:(bank + 1) * 512], AF.Identity,
                    bias=0.0, scale=1.0),
                  reads=(pk,), writes=(("qr2", sl),), tag=tag + "ev")
                qk = ("qr2", sl)
                if do_rope:
                    rk = ("psrot", sl)
                    c.add("tensor", lambda tn, sl=sl: tn.matmul(
                        ps_mm2[:, sl * 512:(sl + 1) * 512], rmm[:], qraw2[sl],
                        start=True, stop=True),
                      reads=(qk, ("rotM",)), writes=(rk,), tag=tag + "rot")
                    c.add("vector", lambda v, sl=sl, g=g: v.tensor_tensor(
                        tmp1b[sl], qraw2[sl], cosb[:, g * 512:(g + 1) * 512], op=ALU.mult),
                      reads=(qk, ("cosb",)), writes=(("t1b", sl),), tag=tag + "m1")
                    c.add("vector", lambda v, sl=sl, g=g: v.tensor_tensor(
                        tmp2b[sl], ps_mm2[:, sl * 512:(sl + 1) * 512],
                        sinb[:, g * 512:(g + 1) * 512], op=ALU.mult),
                      reads=(rk, ("sinb",)), writes=(("t2b", sl),), tag=tag + "m2")
                    c.add("vector", lambda v, sl=sl: v.tensor_tensor(
                        qout2[sl], tmp1b[sl], tmp2b[sl], op=ALU.add),
                      reads=(("t1b", sl), ("t2b", sl)), writes=(("qo2", sl),), tag=tag + "ad")
                    srcb, srckey = qout2[sl], ("qo2", sl)
                else:
                    srcb, srckey = qraw2[sl], qk
                for half in range(2):
                    h = 2 * oc + half
                    dstp = dram[scr_name][h][row0:row0 + 64, g * 512:(g + 1) * 512]
                    c.add("sync", lambda q, dstp=dstp, sb=srcb, half=half: q.dma_start(
                        out=dstp, in_=sb[half * 64:(half + 1) * 64, :]),
                      reads=(srckey,), writes=((scr_name, h, row0, g),),
                      tag=tag + f"st{half}", dma=True)
                return c

            def zip_pairs(units):
                for i in range(0, len(units) - 1, 2):
                    emit_zip(units[i], units[i + 1])
                if len(units) % 2:
                    emit_zip(units[-1])

            uid2 = 0
            units = []
            for oc in range(8):
                for g in range(NG):
                    def mk_mm(oc=oc, g=g):
                        def f(tn, k, bank, first, last):
                            return tn.matmul(
                                ps_mm[:, bank * 512:(bank + 1) * 512],
                                wrk[:, k, oc * 128:(oc + 1) * 128],
                                r1_fm[:, k, g * 512:(g + 1) * 512],
                                start=first, stop=last)
                        return f
                    units.append(mla_unit(uid2, NT, mk_mm(), [("wrk", k) for k in range(NT)],
                                          r1_all, g, oc, True, "sKM", 64, f"D2kr{oc}g{g}"))
                    uid2 += 1
            zip_pairs(units)

            prog.barrier()

            wku = carve(64, [128, 4, 1024], BF16)
            wqu = carve(72, [128, 4, 1024], BF16)
            wrq = carve(80, [128, 4, 1024], BF16)
            wvu = carve(88, [128, 4, S], BF16)
            load_weight_cast(wku.rearrange("p a b -> p (a b)"), dram["k_u_w"], 0, LAT, 0, 1024,
                             ("wku",), "D2wku")
            load_weight_cast(wqu.rearrange("p a b -> p (a b)"), dram["q_u_w"], 0, LAT, 0, 1024,
                             ("wqu",), "D2wqu")
            load_weight_cast(wrq.rearrange("p a b -> p (a b)"), dram["rope_q_w"], 0, LAT, 0, 1024,
                             ("wrq",), "D2wrq")
            load_weight_cast(wvu.rearrange("p a b -> p (a b)"), dram["v_u_w"], 0, LAT, 0, D,
                             ("wvu",), "D2wvu")

            units = []
            for which, wbuf, wkeyn, rhsb, rhsall, rope_f, scrn, row0 in (
                    ("kp", wku, "wku", kv_fm, kv_all, False, "sKM", 0),
                    ("qp", wqu, "wqu", qd_fm, qd_all, False, "sQM", 0),
                    ("qr", wrq, "wrq", qd_fm, qd_all, True, "sQM", 64)):
                for oc in range(8):
                    for g in range(NG):
                        def mk_mm(wbuf=wbuf, rhsb=rhsb, oc=oc, g=g):
                            def f(tn, k, bank, first, last):
                                return tn.matmul(
                                    ps_mm[:, bank * 512:(bank + 1) * 512],
                                    wbuf[:, k, oc * 128:(oc + 1) * 128],
                                    rhsb[:, k, g * 512:(g + 1) * 512],
                                    start=first, stop=last)
                            return f
                        units.append(mla_unit(uid2, 4, mk_mm(),
                                              [(wkeyn, k) for k in range(4)], rhsall,
                                              g, oc, rope_f, scrn, row0,
                                              f"D2{which}{oc}g{g}"))
                        uid2 += 1
            def v_unit(uid, cg, t):
                c = Chain()
                bank = uid % 4
                pk = ("psA", bank)
                for k in range(4):
                    c.add("tensor", lambda tn, k=k, t=t, cg=cg, bank=bank,
                          first=(k == 0), last=(k == 3): tn.matmul(
                        ps_mm[:, bank * 512:(bank + 1) * 512],
                        kv_fm[:, k, t * 128:(t + 1) * 128],
                        wvu[:, k, cg * 512:(cg + 1) * 512],
                        start=first, stop=last),
                      reads=(("wvu", k),) + (kv_all if k == 0 else ()),
                      writes=(pk,), tag=f"D2v{cg}t{t}k{k}")
                vsl = uid % 2
                c.add("scalar", lambda s, bank=bank, vsl=vsl: s.activation(
                    vout2[vsl], ps_mm[:, bank * 512:(bank + 1) * 512],
                    AF.Identity, bias=0.0, scale=1.0),
                  reads=(pk,), writes=(("vo2", vsl),), tag=f"D2ve{cg}t{t}")
                h0 = cg * 4
                dstp = dram["sVM"][h0:h0 + 4, :, t, :].rearrange("h p f -> p h f")
                srcp = vout2[vsl].rearrange("p (h f) -> p h f", h=4)
                c.add("sync", lambda q, dstp=dstp, srcp=srcp: q.dma_start(out=dstp, in_=srcp),
                      reads=(("vo2", vsl),), writes=(("sVM", t, h0),),
                      tag=f"D2vs{cg}t{t}", dma=True)
                return c

            for cg in range(4):
                for t in range(NT):
                    units.append(v_unit(uid2, cg, t))
                    uid2 += 1
            zip_pairs(units)

            prog.barrier()

        if "E" in phases:
            attn2_fm = carve(0, [128, NT, S], BF16)
            attention_phase("sQM", "sKM", "sVM", attn2_fm, "at2", False, "E")
            ow_hi = carve(160, [128, 4, D], BF16)
            load_weight_cast(ow_hi.rearrange("p a b -> p (a b)"), dram["o_w"], 12 * 128, 512,
                             0, D, ("ow", 1), "Fw1")
            prog.barrier()

        if "F" in phases:
            ow_lo = carve(64, [128, 12, D], BF16)
            load_weight_cast(ow_lo.rearrange("p a b -> p (a b)"), dram["o_w"], 0, 12 * 128,
                             0, D, ("ow", 0), "Fw0")
            ytF = carve(112, [128, D], F32)
            htF = carve(120, [128, D], F32)
            m_bf = [carve(128 + 4 * i, [128, D], BF16) for i in range(2)]
            mstage = [carve(136 + 2 * i, [128, 512], BF16) for i in range(2)]
            sqF = carve(140, [128, D], F32)
            lnw2_bc = carve(148, [128, D], F32)
            rw_sb = carve(156, [128, NT, 8], BF16)
            rbbc = carve(157, [128, 8], F32)
            at2_all = tuple(("at2", h, g) for h in range(NH) for g in range(NG))

            dma("sync", lnw2_bc, dram["ln2_w"][0:1, :].partition_broadcast(128).squeeze(1),
                (), (("lnw2",),), "Flnw")
            dma("sync", rbbc, dram["rb_pad"][0:1, :].partition_broadcast(128).squeeze(1),
                (), (("rbbc",),), "Frb")
            P("vector", lambda v: v.memset(rw_sb.rearrange("p a b -> p (a b)"), 0.0),
              reads=(), writes=(("rwz",),), tag="Frwz")
            for k in range(NT):
                dma("gpsimd", rw_sb[:, k, 0:7], dram["router_w"][k * 128:(k + 1) * 128, :],
                    (("rwz",),), (("rw", k),), f"Frw{k}")

            for t in range(NT):
                g, tt4 = t // 4, t % 4
                korder = [12, 13, 14, 15] + list(range(12))
                for cg in range(4):
                    for ki, k in enumerate(korder):
                        half = 1 if k >= 12 else 0
                        wbuf = ow_hi if half == 1 else ow_lo
                        P("tensor", lambda tn, t=t, cg=cg, k=k, wbuf=wbuf,
                          first=(ki == 0), last=(ki == NT - 1), ki=ki: tn.matmul(
                            ps_mm[:, cg * 512:(cg + 1) * 512],
                            attn2_fm[:, k, t * 128:(t + 1) * 128],
                            wbuf[:, k if k < 12 else k - 12, cg * 512:(cg + 1) * 512],
                            start=first, stop=last),
                          reads=(("ow", half, k if k < 12 else k - 12),) + (at2_all if (cg == 0 and ki == 0) else ()),
                          writes=(("psF", cg),), tag=f"Fm{t}c{cg}k{k}")
                dma("sync", ytF, dram["sY"][t], (), (("ytF",),), f"Fy{t}")
                for cg in range(4):
                    P("vector", lambda v, cg=cg: v.tensor_tensor(
                        htF[:, cg * 512:(cg + 1) * 512], ps_mm[:, cg * 512:(cg + 1) * 512],
                        ytF[:, cg * 512:(cg + 1) * 512], op=ALU.add),
                      reads=(("psF", cg), ("ytF",)), writes=(("htF", cg),), tag=f"Fh{t}c{cg}")
                hkeys = tuple(("htF", cg) for cg in range(4))
                dma("sync", dram["sH"][t], htF, hkeys, (("sHw", t),), f"Fho{t}")
                col = 16 + (t % 2) * 8
                rms_stats(htF, hkeys, sqF, ("sqF",), col, D, tag=f"Fr{t}")
                P("vector", lambda v, col=col: v.tensor_scalar(
                    sqF, htF, stt[:, col + 1:col + 2], None, op0=ALU.mult),
                  reads=hkeys + (("stt", col, "rstd"),), writes=(("sqF",),), tag=f"Fm1{t}")
                mpar = t % 2
                P("vector", lambda v, mpar=mpar: v.tensor_tensor(
                    m_bf[mpar], sqF, lnw2_bc, op=ALU.mult),
                  reads=(("sqF",), ("lnw2",)), writes=(("m_bf", mpar),), tag=f"Fm2{t}")
                for c4 in range(4):
                    tb = c4 % 2
                    tkey = ("ptT", tb)
                    for cc in range(4):
                        c = c4 * 4 + cc
                        P("tensor", lambda tn, mpar=mpar, c=c, cc=cc, tb=tb: tn.transpose(
                            ptb[:, tb * 1024 + cc * 128: tb * 1024 + (cc + 1) * 128],
                            m_bf[mpar][:, c * 128:(c + 1) * 128], idn[:]),
                          reads=(("m_bf", mpar), ("idn",)), writes=(tkey,), tag=f"Ft{t}c{c}")
                    msl = c4 % 2
                    P("vector", lambda v, msl=msl, tb=tb: v.tensor_copy(
                        mstage[msl], ptb[:, tb * 1024: tb * 1024 + 512]),
                      reads=(tkey,), writes=(("mst", msl),), tag=f"Fme{t}c{c4}")
                    dst = dram["sMF"][g][:, c4 * 4:(c4 + 1) * 4, tt4 * 128:(tt4 + 1) * 128]
                    src = mstage[msl].rearrange("p (a b) -> p a b", a=4)
                    dma("sync", dst, src, (("mst", msl),), (("sMF", t, c4),), f"Fms{t}c{c4}")
                    for cc in range(4):
                        c = c4 * 4 + cc
                        P("tensor", lambda tn, msl=msl, cc=cc, c=c,
                          first=(c == 0), last=(c == NT - 1): tn.matmul(
                            ps_mm2[:, 0:8], mstage[msl][:, cc * 128:(cc + 1) * 128],
                            rw_sb[:, c, :], start=first, stop=last),
                          reads=(("mst", msl), ("rw", c)), writes=(("psR",),), tag=f"Flg{t}c{c}")
                P("vector", lambda v: v.tensor_tensor(ps_mm2[:, 0:8], ps_mm2[:, 0:8],
                                                      rbbc, op=ALU.add),
                  reads=(("psR",), ("rbbc",)), writes=(("psR",),), tag=f"Fba{t}")
                P("scalar", lambda s: s.activation(probs8[:], ps_mm2[:, 0:8], AF.Sigmoid,
                                                   bias=0.0, scale=1.0),
                  reads=(("psR",),), writes=(("pr8",),), tag=f"Fsg{t}")
                P("vector", lambda v: v.reduce_max(stt[:, 48:49], probs8[:], axis=AX.X),
                  reads=(("pr8",),), writes=(("rm1",),), tag=f"Fx1{t}")
                P("vector", lambda v: v.tensor_scalar(scr8[:], probs8[:], stt[:, 48:49], None,
                                                      op0=ALU.is_equal),
                  reads=(("pr8",), ("rm1",)), writes=(("eq1",),), tag=f"Fe1{t}")
                P("vector", lambda v: v.tensor_scalar(scr8b[:], scr8[:], -1e9, None, op0=ALU.mult),
                  reads=(("eq1",),), writes=(("eq1b",),), tag=f"Fe2{t}")
                P("vector", lambda v: v.tensor_tensor(scr8b[:], probs8[:], scr8b[:], op=ALU.add),
                  reads=(("pr8",), ("eq1b",)), writes=(("msk",),), tag=f"Fe3{t}")
                P("vector", lambda v: v.reduce_max(stt[:, 49:50], scr8b[:], axis=AX.X),
                  reads=(("msk",),), writes=(("rm2",),), tag=f"Fx2{t}")
                P("vector", lambda v: v.tensor_scalar(scr8b[:], scr8b[:], stt[:, 49:50], None,
                                                      op0=ALU.is_equal),
                  reads=(("msk",), ("rm2",)), writes=(("eq2",),), tag=f"Fe4{t}")
                P("vector", lambda v: v.tensor_tensor(scr8[:], scr8[:], scr8b[:], op=ALU.add),
                  reads=(("eq1",), ("eq2",)), writes=(("sel2",),), tag=f"Fe5{t}")
                P("vector", lambda v: v.tensor_tensor(scr8[:], scr8[:], probs8[:], op=ALU.mult),
                  reads=(("sel2",), ("pr8",)), writes=(("wsel",),), tag=f"Fe6{t}")
                P("vector", lambda v: v.tensor_scalar(stt[:, 50:51], stt[:, 48:49],
                                                      stt[:, 49:50], None, op0=ALU.add),
                  reads=(("rm1",), ("rm2",)), writes=(("den",),), tag=f"Fe7{t}")
                P("vector", lambda v: v.reciprocal(stt[:, 51:52], stt[:, 50:51]),
                  reads=(("den",),), writes=(("rden",),), tag=f"Fe8{t}")
                P("vector", lambda v: v.tensor_scalar(scr8[:], scr8[:], stt[:, 51:52], None,
                                                      op0=ALU.mult),
                  reads=(("wsel",), ("rden",)), writes=(("wfin",),), tag=f"Fe9{t}")
                P("tensor", lambda tn: tn.transpose(ps_mm2[0:8, 512:640], scr8[:], idnf[:]),
                  reads=(("wfin",), ("idnf",)), writes=(("psW",),), tag=f"Fwt{t}")
                P("vector", lambda v, t=t: v.tensor_copy(wT[0:8, t * 128:(t + 1) * 128],
                                                         ps_mm2[0:8, 512:640]),
                  reads=(("psW",),), writes=(("wT", t),), tag=f"Fwc{t}")

            prog.barrier()

        if "G" in phases:
            m_g = [carve(0 + 16 * i, [128, NT, 512], BF16) for i in range(2)]
            wg_b = carve(32, [128, NT, 1024], BF16)
            wu_b = carve(64, [128, NT, 1024], BF16)
            wd_b = carve(96, [128, 8, D], BF16)
            tE = carve(128, [128, 8, 512], BF16)
            tEf = carve(128, [128, D], F32)
            f_sb = carve(136, [128, 4, D], F32)
            gsb = [carve(168 + 2 * i, [128, 512], F32) for i in range(2)]
            wbc = [carve(172 + 2 * i, [128, 512], F32) for i in range(2)]

            def g_load_expert(e, tag):
                if e == 0:
                    wg_d, wu_d, wd_d = dram["sh_gate"], dram["sh_up"], dram["sh_down"]
                else:
                    wg_d, wu_d, wd_d = (dram["r_gate"][e - 1], dram["r_up"][e - 1],
                                        dram["r_down"][e - 1])
                load_weight_cast(wg_b.rearrange("p a b -> p (a b)"), wg_d, 0, D, 0, MOE_I,
                                 ("wgB",), tag + "g")
                load_weight_cast(wu_b.rearrange("p a b -> p (a b)"), wu_d, 0, D, 0, MOE_I,
                                 ("wuB",), tag + "u")
                load_weight_cast(wd_b.rearrange("p a b -> p (a b)"), wd_d, 0, MOE_I, 0, D,
                                 ("wdB",), tag + "d")

            dma("sync", m_g[0].rearrange("p a b -> p (a b)"),
                dram["sMF"][0].rearrange("p a b -> p (a b)"), (), (("m_g", 0),), "Gm0")
            for g in range(NG):
                gp = g % 2
                mk = ("m_g", gp)
                for ti in range(4):
                    t = g * 4 + ti
                    dma("sync", f_sb[:, ti, :], dram["sH"][t], (("fS", ti),),
                        (("fS", ti),), f"Gh{g}t{ti}")
                order = list(range(8)) if g % 2 == 0 else list(range(7, -1, -1))
                for idx, e in enumerate(order):
                    tg = f"Ge{g}e{e}"
                    if not (g > 0 and idx == 0):
                        g_load_expert(e, tg)
                    if idx == 7 and g + 1 < NG:
                        gp2 = (g + 1) % 2
                        dma("sync", m_g[gp2].rearrange("p a b -> p (a b)"),
                            dram["sMF"][g + 1].rearrange("p a b -> p (a b)"), (),
                            (("m_g", gp2),), f"Gm{g + 1}")
                    if e > 0:
                        P("tensor", lambda tn, e=e, g=g: tn.matmul(
                            ps_mm[:, 0:512], sel8[:, (e - 1) * 128: e * 128],
                            wT[:, g * 512:(g + 1) * 512], start=True, stop=True),
                          reads=(("sel8",), ("wT", 4 * g), ("wT", 4 * g + 1),
                                 ("wT", 4 * g + 2), ("wT", 4 * g + 3)),
                          writes=(("psGg", 0),), tag=tg + "wb")
                        P("vector", lambda v, gp=gp: v.tensor_copy(wbc[gp], ps_mm[:, 0:512]),
                          reads=(("psGg", 0),), writes=(("wbc", gp),), tag=tg + "wc")
                    for ic in range(8):
                        bg = ic % 2
                        bu = 2 + ic % 2
                        for k in range(NT):
                            P("tensor", lambda tn, k=k, ic=ic, gp=gp, bg=bg,
                              first=(k == 0), last=(k == NT - 1): tn.matmul(
                                ps_mm[:, bg * 512:(bg + 1) * 512],
                                wg_b[:, k, ic * 128:(ic + 1) * 128],
                                m_g[gp][:, k, :], start=first, stop=last),
                              reads=(("wgB", k), mk), writes=(("psGg", bg),), tag=tg + f"g{ic}k{k}")
                        for k in range(NT):
                            P("tensor", lambda tn, k=k, ic=ic, gp=gp, bu=bu,
                              first=(k == 0), last=(k == NT - 1): tn.matmul(
                                ps_mm[:, bu * 512:(bu + 1) * 512],
                                wu_b[:, k, ic * 128:(ic + 1) * 128],
                                m_g[gp][:, k, :], start=first, stop=last),
                              reads=(("wuB", k), mk), writes=(("psGu", bu),), tag=tg + f"u{ic}k{k}")
                        gp2 = ic % 2
                        P("scalar", lambda s, gp2=gp2, bg=bg: s.activation(
                            gsb[gp2], ps_mm[:, bg * 512:(bg + 1) * 512], AF.Silu,
                            bias=0.0, scale=1.0),
                          reads=(("psGg", bg),), writes=(("gsb", gp2),), tag=tg + f"si{ic}")
                        P("vector", lambda v, gp2=gp2, bu=bu: v.tensor_tensor(
                            gsb[gp2], gsb[gp2], ps_mm[:, bu * 512:(bu + 1) * 512],
                            op=ALU.mult),
                          reads=(("gsb", gp2), ("psGu", bu)), writes=(("gsb", gp2),),
                          tag=tg + f"gu{ic}")
                        if e > 0:
                            P("vector", lambda v, gp2=gp2, ic=ic, gp=gp: v.tensor_tensor(
                                tE[:, ic, :], gsb[gp2], wbc[gp], op=ALU.mult),
                              reads=(("gsb", gp2), ("wbc", gp)), writes=(("tE", ic),),
                              tag=tg + f"tw{ic}")
                        else:
                            P("vector", lambda v, gp2=gp2, ic=ic: v.tensor_scalar(
                                tE[:, ic, :], gsb[gp2], 0.125, None, op0=ALU.mult),
                              reads=(("gsb", gp2),), writes=(("tE", ic),), tag=tg + f"ts{ic}")
                    tE_all = tuple(("tE", ic) for ic in range(8))
                    for ti in range(4):
                        for cg in range(4):
                            for ic in range(8):
                                P("tensor", lambda tn, ti=ti, cg=cg, ic=ic,
                                  first=(ic == 0), last=(ic == 7): tn.matmul(
                                    ps_mm2[:, cg * 512:(cg + 1) * 512],
                                    tE[:, ic, ti * 128:(ti + 1) * 128],
                                    wd_b[:, ic, cg * 512:(cg + 1) * 512],
                                    start=first, stop=last),
                                  reads=(("wdB", ic), ("tE", ic)),
                                  writes=(("psGd", cg),), tag=tg + f"d{ti}c{cg}i{ic}")
                            P("vector", lambda v, ti=ti, cg=cg: v.tensor_tensor(
                                f_sb[:, ti, cg * 512:(cg + 1) * 512],
                                f_sb[:, ti, cg * 512:(cg + 1) * 512],
                                ps_mm2[:, cg * 512:(cg + 1) * 512], op=ALU.add),
                              reads=(("psGd", cg), ("fS", ti)), writes=(("fS", ti),),
                              tag=tg + f"fa{ti}c{cg}")
                for ti in range(4):
                    t = g * 4 + ti
                    col = 32 + (ti % 2) * 8
                    ln_stats(f_sb[:, ti, :], (("fS", ti),), tEf,
                             tuple(("tE", ic) for ic in range(8)), col, D, tag=f"Gl{t}")
                    P("vector", lambda v, ti=ti, col=col: v.tensor_scalar(
                        tEf, f_sb[:, ti, :], stt[:, col:col + 1], stt[:, col + 1:col + 2],
                        op0=ALU.subtract, op1=ALU.mult),
                      reads=(("fS", ti), ("stt", col, "mean"), ("stt", col, "rstd")),
                      writes=tuple(("tE", ic) for ic in range(8)), tag=f"Ga{t}")
                    dma("sync", f_sb[:, ti, :], dram["sX1"][t],
                        (("fS", ti),), (("fS", ti),), f"Gx1{t}")
                    P("vector", lambda v, ti=ti: v.tensor_tensor(
                        tEf, tEf, f_sb[:, ti, :], op=ALU.add),
                      reads=(("tE", 0), ("fS", ti)), writes=tuple(("tE", ic) for ic in range(8)),
                      tag=f"Gx2{t}")
                    dma("sync", dram["out"][t * 128:(t + 1) * 128, :], tEf,
                        (("tE", 0),), (("outw", t),), f"Go{t}")

        prog.barrier()

        dma("sync", dram["chk"], dram["sX1"][0], (), (("chk",),), "chk")

        for tp in taps:
            if tp == "XB":
                continue
            dma("sync", tap_aps[tp], dram["s" + tp], (), ((tp + "_tap",),), "tap" + tp)

        prog.barrier()

        counts = finish()

    return nc, counts, len(prog.ops)


def make_inmaps(inputs):
    c = host_consts()
    x = np.ascontiguousarray(np.asarray(inputs["L0_attn_x"]), dtype=np.float32)
    base = {
        "w_qkv": np.ascontiguousarray(np.asarray(inputs["L0_attn_w_qkv"], np.float32)),
        "w_o1": np.ascontiguousarray(np.asarray(inputs["L0_attn_w_o"], np.float32)),
        "ln1_w": np.ascontiguousarray(np.asarray(inputs["L0_ffn_ln1_w"], np.float32)).reshape(1, D),
        "ln2_w": np.ascontiguousarray(np.asarray(inputs["L0_ffn_ln2_w"], np.float32)).reshape(1, D),
        "kv_d_w": np.ascontiguousarray(np.asarray(inputs["L0_ffn_kv_d_w"], np.float32)),
        "q_d_w": np.ascontiguousarray(np.asarray(inputs["L0_ffn_q_d_w"], np.float32)),
        "k_u_w": np.ascontiguousarray(np.asarray(inputs["L0_ffn_k_u_w"], np.float32)),
        "v_u_w": np.ascontiguousarray(np.asarray(inputs["L0_ffn_v_u_w"], np.float32)),
        "q_u_w": np.ascontiguousarray(np.asarray(inputs["L0_ffn_q_u_w"], np.float32)),
        "rope_k_w": np.ascontiguousarray(np.asarray(inputs["L0_ffn_rope_k_w"], np.float32)),
        "rope_q_w": np.ascontiguousarray(np.asarray(inputs["L0_ffn_rope_q_w"], np.float32)),
        "o_w": np.ascontiguousarray(np.asarray(inputs["L0_ffn_o_w"], np.float32)),
        "sh_gate": np.ascontiguousarray(np.asarray(inputs["L0_ffn_sh_gate"], np.float32)),
        "sh_up": np.ascontiguousarray(np.asarray(inputs["L0_ffn_sh_up"], np.float32)),
        "sh_down": np.ascontiguousarray(np.asarray(inputs["L0_ffn_sh_down"], np.float32)),
        "r_gate": np.ascontiguousarray(np.asarray(inputs["L0_ffn_r_gate"], np.float32)),
        "r_up": np.ascontiguousarray(np.asarray(inputs["L0_ffn_r_up"], np.float32)),
        "r_down": np.ascontiguousarray(np.asarray(inputs["L0_ffn_r_down"], np.float32)),
        "router_w": np.ascontiguousarray(np.asarray(inputs["L0_ffn_router_w"], np.float32)),
        "rb_pad": np.concatenate([np.asarray(inputs["L0_ffn_router_b"], np.float32).ravel(),
                                  np.float32([-1e9])]).reshape(1, 8),
    }
    base.update(c)
    in_maps = []
    for core in range(8):
        m = dict(base)
        m["x0"] = np.ascontiguousarray(x[core])
        in_maps.append(m)
    return in_maps


def kernel(**inputs):
    nc, _, _ = build_nc(taps=())
    in_maps = make_inmaps(inputs)
    res = run_bass_kernel_spmd(nc, in_maps, core_ids=list(range(8)))
    out = np.stack([np.asarray(res.results[i]["out"], np.float32) for i in range(8)], axis=0)
    return out


# version 11
# speedup vs baseline: 1.2035x; 1.0309x over previous
import math
import numpy as np
import ml_dtypes
import concourse.bass as bass
import concourse.mybir as mybir
from concourse.bass_utils import run_bass_kernel_spmd
from contextlib import ExitStack

F32 = mybir.dt.float32
BF16 = mybir.dt.bfloat16
U8 = mybir.dt.uint8
AF = mybir.ActivationFunctionType
ALU = mybir.AluOpType
AX = mybir.AxisListType

S = 2048
D = 2048
NH = 16
NT = 16
NG = 4
LAT = 512
NE = 7
MOE_I = 1024
EPS = 1e-5
SCALE = 1.0 / math.sqrt(128.0)
NEGINF = -1e30
ENGINES = ("tensor", "vector", "scalar", "sync", "gpsimd")
KB = 1024


NPOOL = 12


class Op:
    __slots__ = ("eng", "fn", "reads", "writes", "dma", "needs_inc", "sem_val",
                 "deps", "tag", "slot", "use")
    def __init__(self, eng, fn, reads, writes, dma, tag):
        self.eng = eng
        self.fn = fn
        self.reads = reads
        self.writes = writes
        self.dma = dma
        self.needs_inc = False
        self.sem_val = 0
        self.deps = ()
        self.tag = tag
        self.slot = -1
        self.use = 0


class Prog:
    def __init__(self):
        self.ops = []
        self.dma_n = {"sync": 0, "gpsimd": 0}

    def emit(self, eng, fn, reads=(), writes=(), dma=False, tag=""):
        op = Op(eng, fn, tuple(reads), tuple(writes), dma, tag)
        if dma:
            idx = self.dma_n[eng]
            self.dma_n[eng] = idx + 1
            op.slot = idx % NPOOL
            op.use = idx // NPOOL
        self.ops.append(op)
        return op

    def barrier(self):
        for e in ENGINES:
            self.emit(e, None, tag="BARRIER")

    def link(self):
        last_writer = {}
        readers = {}
        last_op = {e: None for e in ENGINES}
        last_dma = {}
        for op in self.ops:
            deps = set()
            if op.tag == "BARRIER":
                for e, o in last_op.items():
                    if o is not None and e != op.eng:
                        deps.add(o)
                for (q, s), o in last_dma.items():
                    deps.add(o)
            else:
                for k in op.reads:
                    w = last_writer.get(k)
                    if w is not None:
                        same = w.eng == op.eng and not w.dma and not op.dma
                        if not (same and op.eng == "tensor"):
                            deps.add(w)
                for k in op.writes:
                    w = last_writer.get(k)
                    if w is not None and not (w.eng == op.eng and not w.dma and not op.dma):
                        deps.add(w)
                    for r in readers.get(k, ()):
                        if not (r.eng == op.eng and not r.dma and not op.dma and op.eng == "tensor"):
                            deps.add(r)
                for k in op.reads:
                    readers.setdefault(k, []).append(op)
                for k in op.writes:
                    last_writer[k] = op
                    readers[k] = []
            op.deps = tuple(deps)
            for d in deps:
                d.needs_inc = True
            if op.dma:
                last_dma[(op.eng, op.slot)] = op
            else:
                last_op[op.eng] = op

    def assign(self):
        counts = {e: 0 for e in ENGINES}
        for op in self.ops:
            if op.dma:
                op.sem_val = 16 * (op.use + 1)
            elif op.needs_inc:
                counts[op.eng] += 1
                op.sem_val = counts[op.eng]
        return counts

    def prepare(self):
        self.by_eng = {e: [] for e in ENGINES}
        for op in self.ops:
            self.by_eng[op.eng].append(op)
        self.marks = {}

    def _wait(self, eng, engname, sem_key, sem_handle, val):
        mk = self.marks.setdefault((engname, sem_key), [0])
        if val > mk[0]:
            eng.wait_ge(sem_handle, val)
            mk[0] = val

    def mat_engine(self, engname, eng, sems, dsems):
        for op in self.by_eng[engname]:
            for d in op.deps:
                if d.dma:
                    self._wait(eng, engname, (d.eng, d.slot), dsems[d.eng][d.slot], d.sem_val)
                else:
                    self._wait(eng, engname, d.eng, sems[d.eng], d.sem_val)
            if op.dma and op.use > 0:
                self._wait(eng, engname, (engname, op.slot), dsems[engname][op.slot],
                           16 * op.use)
            if op.fn is not None:
                inst = op.fn(eng)
                if op.dma:
                    inst.then_inc(dsems[engname][op.slot], 16)
                elif op.needs_inc:
                    assert inst is not None, f"op {op.tag} needs inc, returned None"
                    inst.then_inc(sems[engname], 1)
            elif op.needs_inc:
                eng.nop().then_inc(sems[engname], 1)


def host_consts():
    c = {}
    c["IDN"] = np.eye(128, dtype=ml_dtypes.bfloat16)
    R1 = np.zeros((128, 128), dtype=np.float32)
    for m in range(64):
        R1[m + 64, m] = -1.0
        R1[m, m + 64] = 1.0
    c["R1M"] = R1.astype(ml_dtypes.bfloat16)
    RM = np.zeros((128, 128), dtype=np.float32)
    for h in range(2):
        o = h * 64
        for i in range(32):
            RM[o + i + 32, o + i] = -1.0
            RM[o + i, o + i + 32] = 1.0
    c["RMM"] = RM.astype(ml_dtypes.bfloat16)
    mk = np.zeros((128, 128), dtype=np.float32)
    mk[np.triu_indices(128, 1)] = NEGINF
    c["MASKC"] = mk
    t = np.arange(S, dtype=np.float64)
    w1 = 10000.0 ** (-(np.arange(64, dtype=np.float64)) / 64.0)
    ang1 = np.outer(np.concatenate([w1, w1]), t)
    c["COS1"] = np.cos(ang1).astype(np.float32)
    c["SIN1"] = np.sin(ang1).astype(np.float32)
    wM = 10000.0 ** (-(np.arange(32, dtype=np.float64)) / 32.0)
    angM = np.outer(np.concatenate([wM] * 4), t)
    c["COSM"] = np.cos(angM).astype(np.float32)
    c["SINM"] = np.sin(angM).astype(np.float32)
    sel = np.zeros((8, 8 * 128), dtype=np.float32)
    for e in range(8):
        sel[e, e * 128:(e + 1) * 128] = 0.125
    c["SEL8"] = sel
    c["IDNF"] = np.eye(128, dtype=np.float32)
    return c


def build_nc(taps=(), phases="ABCDEFG"):
    nc = bass.Bass("TRN2", target_bir_lowering=False)
    dram = {}

    def din(name, shape, dt=F32):
        dram[name] = nc.dram_tensor(name, shape, dt, kind="ExternalInput").ap()
        return dram[name]

    def dint(name, shape, dt):
        dram[name] = nc.dram_tensor(name, shape, dt, kind="Internal").ap()
        return dram[name]

    din("x0", [S, D])
    din("w_qkv", [D, 3 * D])
    din("w_o1", [D, D])
    din("ln1_w", [1, D])
    din("ln2_w", [1, D])
    din("kv_d_w", [D, LAT])
    din("q_d_w", [D, LAT])
    din("k_u_w", [LAT, 1024])
    din("v_u_w", [LAT, D])
    din("q_u_w", [LAT, 1024])
    din("rope_k_w", [D, 1024])
    din("rope_q_w", [LAT, 1024])
    din("o_w", [D, D])
    din("sh_gate", [D, MOE_I])
    din("sh_up", [D, MOE_I])
    din("sh_down", [MOE_I, D])
    din("r_gate", [NE, D, MOE_I])
    din("r_up", [NE, D, MOE_I])
    din("r_down", [NE, MOE_I, D])
    din("router_w", [D, NE])
    din("rb_pad", [1, 8])
    din("IDN", [128, 128], BF16)
    din("R1M", [128, 128], BF16)
    din("RMM", [128, 128], BF16)
    din("MASKC", [128, 128])
    din("COS1", [128, S])
    din("SIN1", [128, S])
    din("COSM", [128, S])
    din("SINM", [128, S])
    din("SEL8", [8, 8 * 128])
    din("IDNF", [128, 128])
    dint("sQ1", [NH, 128, S], BF16)
    dint("sK1", [NH, 128, S], BF16)
    dint("sV1", [NH, 128, NT, 128], BF16)
    dint("sQM", [NH, 128, S], BF16)
    dint("sKM", [NH, 128, S], BF16)
    dint("sVM", [NH, 128, NT, 128], BF16)
    dint("sX1", [NT, 128, D], F32)
    dint("sY", [NT, 128, D], F32)
    dint("sH", [NT, 128, D], F32)
    dint("sAT1", [128, NT, S], BF16)
    dint("sMF", [NG, 128, NT, 512], BF16)
    dint("sR1F", [128, NT, S], BF16)
    dint("sKVF", [128, 4, S], BF16)
    out = nc.dram_tensor("out", [S, D], F32, kind="ExternalOutput").ap()
    dram["out"] = out
    dram["chk"] = nc.dram_tensor("zzchk", [128, D], F32, kind="ExternalOutput").ap()

    tap_aps = {}
    for tp in taps:
        if tp in ("R1F", "KVF"):
            srcd = dram["s" + tp]
            tap_aps[tp] = nc.dram_tensor("tap_" + tp, list(srcd.shape), srcd.dtype,
                                         kind="ExternalOutput").ap()
            continue
        if tp == "XB":
            tap_aps[tp] = nc.dram_tensor("tap_XB", [128, NT, S], BF16,
                                         kind="ExternalOutput").ap()
        else:
            srcd = dram["s" + tp]
            tap_aps[tp] = nc.dram_tensor("tap_" + tp, list(srcd.shape), srcd.dtype,
                                         kind="ExternalOutput").ap()

    prog = Prog()
    P = prog.emit

    with ExitStack() as ctx:
        def sbuf(name, shape, dt):
            return ctx.enter_context(nc.sbuf_tensor(name, shape, dt))
        def psum(name, shape, dt):
            return ctx.enter_context(nc.psum_tensor(name, shape, dt))

        idn = sbuf("idn", [128, 128], BF16)
        idnf = sbuf("idnf", [128, 128], F32)
        r1m = sbuf("r1m", [128, 128], BF16)
        rmm = sbuf("rmm", [128, 128], BF16)
        maskc = sbuf("maskc", [128, 128], F32)
        sel8 = sbuf("sel8", [8, 8 * 128], F32)
        rbp = sbuf("rbp", [1, 8], F32)
        stt = sbuf("stt", [128, 64], F32)
        sumsM = sbuf("sumsM", [128, NT], F32)
        recM = sbuf("recM", [128, NT], F32)
        wT = sbuf("wT", [8, S], F32)
        probs8 = sbuf("probs8", [128, 8], F32)
        scr8 = sbuf("scr8", [128, 8], F32)
        scr8b = sbuf("scr8b", [128, 8], F32)
        cosb = sbuf("cosb", [128, S], F32)
        sinb = sbuf("sinb", [128, S], F32)

        ARENA_BYTES = 176 * KB
        arena = sbuf("arena", [128, ARENA_BYTES], U8)

        def carve(off_kb, shape, dt):
            nbytes = int(np.prod(shape[1:])) * (2 if dt == BF16 else 4)
            assert off_kb * KB + nbytes <= ARENA_BYTES, (off_kb, nbytes)
            ap = arena[:, off_kb * KB: off_kb * KB + nbytes].bitcast(dt)
            if len(shape) == 3:
                ap = ap.rearrange("p (a b) -> p a b", a=shape[1])
            elif len(shape) == 4:
                ap = ap.rearrange("p (a b c) -> p a b c", a=shape[1], b=shape[2])
            return ap

        ps_mm = psum("ps_mm", [128, 2048], F32)
        ps_mm2 = psum("ps_mm2", [128, 2048], F32)
        ptb = ps_mm2[:, 1024:2048].bitcast(BF16)

        sems = {e: ctx.enter_context(nc.semaphore("sem_" + e)) for e in ENGINES}
        dsems = {q: [ctx.enter_context(nc.semaphore(f"dsem_{q}{i}")) for i in range(NPOOL)]
                 for q in ("sync", "gpsimd")}

        def finish():
            prog.link()
            counts = prog.assign()
            prog.prepare()
            with nc.Block() as block:
                @block.tensor
                def _(t):
                    prog.mat_engine("tensor", t, sems, dsems)
                @block.vector
                def _(v):
                    prog.mat_engine("vector", v, sems, dsems)
                @block.scalar
                def _(s):
                    prog.mat_engine("scalar", s, sems, dsems)
                @block.sync
                def _(sy):
                    prog.mat_engine("sync", sy, sems, dsems)
                @block.gpsimd
                def _(g):
                    prog.mat_engine("gpsimd", g, sems, dsems)
            return counts

        def dma(queue, dst, src, reads, writes, tag=""):
            P(queue, lambda q, dst=dst, src=src: q.dma_start(out=dst, in_=src),
              reads=reads, writes=writes, dma=True, tag=tag)

        def load_weight_cast(dst_ap2, w_dram, row0, nrows, col0, ncols, key, tag, queue="gpsimd"):
            nk = nrows // 128
            for k in range(nk):
                src = w_dram[row0 + k * 128: row0 + (k + 1) * 128, col0: col0 + ncols]
                dst = dst_ap2[:, k * ncols: (k + 1) * ncols]
                dma(queue, dst, src, (), (key + (k,),), tag + f"k{k}")

        def ln_stats(src_ap, src_keys, sq_dump, sq_key, col, n, tag=""):
            sq_keys = sq_key if isinstance(sq_key, tuple) and sq_key and isinstance(sq_key[0], tuple) else (sq_key,)
            ssum = stt[:, col + 2: col + 3]
            ssq = stt[:, col + 3: col + 4]
            P("vector", lambda v: v.reduce_sum(ssum, src_ap, axis=AX.X),
              reads=src_keys, writes=(("stt", col, "sum"),), tag=tag + "sum")
            P("scalar", lambda s: s.activation(sq_dump, src_ap, AF.Square, bias=0.0,
                                               scale=1.0, accum_out=ssq),
              reads=src_keys + (("stt", col, "sum"),),
              writes=sq_keys + (("stt", col, "sq"),), tag=tag + "sq")
            P("vector", lambda v: v.tensor_scalar(stt[:, col:col + 1], ssum, 1.0 / n, None,
                                                  op0=ALU.mult),
              reads=(("stt", col, "sum"),), writes=(("stt", col, "mean"),), tag=tag + "mean")
            P("vector", lambda v: v.tensor_scalar(ssum, stt[:, col:col + 1],
                                                  stt[:, col:col + 1], None, op0=ALU.mult),
              reads=(("stt", col, "mean"),), writes=(("stt", col, "m2"),), tag=tag + "m2")
            P("vector", lambda v: v.tensor_scalar(ssq, ssq, 1.0 / n, None, op0=ALU.mult),
              reads=(("stt", col, "sq"),), writes=(("stt", col, "sqn"),), tag=tag + "sqn")
            P("vector", lambda v: v.tensor_tensor(ssq, ssq, ssum, op=ALU.subtract),
              reads=(("stt", col, "sqn"), ("stt", col, "m2")), writes=(("stt", col, "var"),),
              tag=tag + "var")
            P("vector", lambda v: v.tensor_scalar(ssq, ssq, EPS, None, op0=ALU.add),
              reads=(("stt", col, "var"),), writes=(("stt", col, "ve"),), tag=tag + "ve")
            P("scalar", lambda s: s.activation(ssum, ssq, AF.Sqrt, bias=0.0, scale=1.0),
              reads=(("stt", col, "ve"),), writes=(("stt", col, "sd"),), tag=tag + "sd")
            P("vector", lambda v: v.reciprocal(stt[:, col + 1:col + 2], ssum),
              reads=(("stt", col, "sd"),), writes=(("stt", col, "rstd"),), tag=tag + "rstd")

        def rms_stats(src_ap, src_keys, sq_dump, sq_key, col, n, tag=""):
            sq_keys = sq_key if isinstance(sq_key, tuple) and sq_key and isinstance(sq_key[0], tuple) else (sq_key,)
            ssum = stt[:, col + 2: col + 3]
            ssq = stt[:, col + 3: col + 4]
            P("scalar", lambda s: s.activation(sq_dump, src_ap, AF.Square, bias=0.0,
                                               scale=1.0, accum_out=ssq),
              reads=src_keys, writes=sq_keys + (("stt", col, "sq"),), tag=tag + "sq")
            P("vector", lambda v: v.tensor_scalar(ssq, ssq, 1.0 / n, EPS, op0=ALU.mult,
                                                  op1=ALU.add),
              reads=(("stt", col, "sq"),), writes=(("stt", col, "ve"),), tag=tag + "ve")
            P("scalar", lambda s: s.activation(ssum, ssq, AF.Sqrt, bias=0.0, scale=1.0),
              reads=(("stt", col, "ve"),), writes=(("stt", col, "sd"),), tag=tag + "sd")
            P("vector", lambda v: v.reciprocal(stt[:, col + 1:col + 2], ssum),
              reads=(("stt", col, "sd"),), writes=(("stt", col, "rstd"),), tag=tag + "rstd")

        class Chain(list):
            def add(self, eng, fn, reads=(), writes=(), tag="", dma=False):
                self.append((eng, fn, reads, writes, tag, dma))

        def emit_zip(*chains):
            idx = [0] * len(chains)
            while True:
                alive = False
                for ci, c in enumerate(chains):
                    if idx[ci] < len(c):
                        eng, fn, reads, writes, tag, dma_ = c[idx[ci]]
                        P(eng, fn, reads=reads, writes=writes, tag=tag, dma=dma_)
                        idx[ci] += 1
                        alive = True
                if not alive:
                    break

        dma("sync", idn[:], dram["IDN"], (), (("idn",),), "c")
        dma("sync", idnf[:], dram["IDNF"], (), (("idnf",),), "c")
        dma("sync", r1m[:], dram["R1M"], (), (("rot1",),), "c")
        dma("sync", rmm[:], dram["RMM"], (), (("rotM",),), "c")
        dma("sync", maskc[:], dram["MASKC"], (), (("maskc",),), "c")
        dma("sync", sel8[:], dram["SEL8"], (), (("sel8",),), "c")
        dma("sync", rbp[:], dram["rb_pad"], (), (("rbp",),), "c")
        dma("sync", cosb[:], dram["COS1"], (), (("cosb",),), "c")
        dma("sync", sinb[:], dram["SIN1"], (), (("sinb",),), "c")

        x0r = dram["x0"].rearrange("(t p) d -> t p d", p=128)

        ln0_fm = carve(0, [128, NT, S], BF16)
        sqsc = carve(64, [128, D], F32)
        x0t_a = [carve(72 + 8 * i, [128, D], F32) for i in range(2)]
        ln0bf = [carve(88 + 4 * i, [128, D], BF16) for i in range(4)]
        wblkQ = [carve(72 + 32 * i, [128, 16, 1024], BF16) for i in range(2)]
        qraw = [carve(136 + 1 * i, [128, 512], BF16) for i in range(2)]
        tmp1 = [carve(138 + 2 * i, [128, 512], F32) for i in range(2)]
        tmp2 = [carve(142 + 2 * i, [128, 512], F32) for i in range(2)]
        qout = [carve(146 + 1 * i, [128, 512], BF16) for i in range(2)]
        vout = [carve(148 + 1 * i, [128, 512], BF16) for i in range(2)]

        for g in range(NG):
            for ti in range(4):
                t = g * 4 + ti
                sl = t % 2
                xk = ("x0t", sl)
                dma("sync", x0t_a[sl], x0r[t], (), (xk,), f"A0x{t}")
                col = sl * 8
                ln_stats(x0t_a[sl], (xk,), sqsc, ("sqsc",), col, D, tag=f"A0s{t}")
                P("vector", lambda v, sl=sl, col=col, ti=ti: v.tensor_scalar(
                    ln0bf[ti], x0t_a[sl], stt[:, col:col + 1], stt[:, col + 1:col + 2],
                    op0=ALU.subtract, op1=ALU.mult),
                  reads=(xk, ("stt", col, "mean"), ("stt", col, "rstd")),
                  writes=(("ln0bf", ti),), tag=f"A0a{t}")
            for c in range(NT):
                bank = c % 2
                pk = ("ptA", bank)
                for ti in range(4):
                    P("tensor", lambda tn, c=c, ti=ti, bank=bank: tn.transpose(
                        ptb[:, bank * 1024 + ti * 128: bank * 1024 + (ti + 1) * 128],
                        ln0bf[ti][:, c * 128:(c + 1) * 128], idn[:]),
                      reads=(("ln0bf", ti), ("idn",)), writes=(pk,), tag=f"A0t{g}c{c}")
                P("vector", lambda v, c=c, g=g, bank=bank: v.tensor_copy(
                    ln0_fm[:, c, g * 512:(g + 1) * 512],
                    ptb[:, bank * 1024: bank * 1024 + 512]),
                  reads=(pk,), writes=(("ln0_fm", c, g),), tag=f"A0e{g}c{c}")

        ln0_all = tuple(("ln0_fm", c, g) for c in range(NT) for g in range(NG))
        prog.barrier()

        def rope_evac(uid, g, bank, rot_ap, rot_key, scr_name, head, tag):
            sl = uid % 2
            pk = ("psA", bank)
            qk = ("qraw", sl)
            P("scalar", lambda s, sl=sl, bank=bank: s.activation(
                qraw[sl], ps_mm[:, bank * 512:(bank + 1) * 512], AF.Identity,
                bias=0.0, scale=1.0),
              reads=(pk,), writes=(qk,), tag=tag + "ev")
            rk = ("psrot", sl)
            P("tensor", lambda tn, sl=sl, rot_ap=rot_ap: tn.matmul(
                ps_mm2[:, sl * 512:(sl + 1) * 512], rot_ap, qraw[sl], start=True, stop=True),
              reads=(qk, (rot_key,)), writes=(rk,), tag=tag + "rot")
            P("vector", lambda v, sl=sl, g=g: v.tensor_tensor(
                tmp1[sl], qraw[sl], cosb[:, g * 512:(g + 1) * 512], op=ALU.mult),
              reads=(qk, ("cosb",)), writes=(("tmp1", sl),), tag=tag + "m1")
            P("vector", lambda v, sl=sl, g=g: v.tensor_tensor(
                tmp2[sl], ps_mm2[:, sl * 512:(sl + 1) * 512],
                sinb[:, g * 512:(g + 1) * 512], op=ALU.mult),
              reads=(rk, ("sinb",)), writes=(("tmp2", sl),), tag=tag + "m2")
            P("vector", lambda v, sl=sl: v.tensor_tensor(
                qout[sl], tmp1[sl], tmp2[sl], op=ALU.add),
              reads=(("tmp1", sl), ("tmp2", sl)), writes=(("qout", sl),), tag=tag + "ad")
            dma("sync", dram[scr_name][head][:, g * 512:(g + 1) * 512], qout[sl],
                (("qout", sl),), ((scr_name, head, g),), tag + "st")

        uid = 0
        for j in range(4):
            sl = j % 2
            wk = ("wblkQ", sl)
            load_weight_cast(wblkQ[sl].rearrange("p a b -> p (a b)"), dram["w_qkv"],
                             0, D, j * 1024, 1024, wk, f"A1w{j}")
            for oc in range(8):
                col_glob = j * 1024 + oc * 128
                is_k = col_glob >= 2048
                head = (col_glob % 2048) // 128
                scr_name = "sK1" if is_k else "sQ1"
                for g in range(NG):
                    bank = uid % 4
                    pk = ("psA", bank)
                    for k in range(NT):
                        P("tensor", lambda tn, sl=sl, k=k, oc=oc, g=g, bank=bank,
                          first=(k == 0), last=(k == NT - 1): tn.matmul(
                            ps_mm[:, bank * 512:(bank + 1) * 512],
                            wblkQ[sl][:, k, oc * 128:(oc + 1) * 128],
                            ln0_fm[:, k, g * 512:(g + 1) * 512],
                            start=first, stop=last),
                          reads=(wk + (k,),) + (ln0_all if k == 0 else ()),
                          writes=(pk,), tag=f"A1m{j}o{oc}g{g}k{k}")
                    rope_evac(uid, g, bank, r1m[:], "rot1", scr_name, head,
                              f"A1r{j}o{oc}g{g}")
                    uid += 1

        for j in range(2):
            sl = j % 2
            wk = ("wblkQ", sl)
            load_weight_cast(wblkQ[sl].rearrange("p a b -> p (a b)"), dram["w_qkv"],
                             0, D, 4096 + j * 1024, 1024, wk, f"A2w{j}")
            for cg in range(2):
                for t in range(NT):
                    bank = (cg * NT + t) % 4
                    pk = ("psA", bank)
                    for k in range(NT):
                        P("tensor", lambda tn, sl=sl, k=k, t=t, cg=cg, bank=bank,
                          first=(k == 0), last=(k == NT - 1): tn.matmul(
                            ps_mm[:, bank * 512:(bank + 1) * 512],
                            ln0_fm[:, k, t * 128:(t + 1) * 128],
                            wblkQ[sl][:, k, cg * 512:(cg + 1) * 512],
                            start=first, stop=last),
                          reads=(wk + (k,),) + (ln0_all if k == 0 else ()),
                          writes=(pk,), tag=f"A2m{j}c{cg}t{t}k{k}")
                    vsl = (t + cg * NT) % 2
                    P("scalar", lambda s, bank=bank, vsl=vsl: s.activation(
                        vout[vsl], ps_mm[:, bank * 512:(bank + 1) * 512],
                        AF.Identity, bias=0.0, scale=1.0),
                      reads=(pk,), writes=(("vout", vsl),), tag=f"A2e{j}c{cg}t{t}")
                    h0 = (j * 1024 + cg * 512) // 128
                    dst = dram["sV1"][h0:h0 + 4, :, t, :].rearrange("h p f -> p h f")
                    src = vout[vsl].rearrange("p (h f) -> p h f", h=4)
                    dma("sync", dst, src, (("vout", vsl),), (("sV1", t, h0),),
                        f"A2s{j}c{cg}t{t}")

        prog.barrier()

        def attention_phase(scrQ, scrK, scrV, attn_fm, akey, with_sums, ph):
            Qh = [carve(64 + 4 * i, [128, S], BF16) for i in range(4)]
            Kh = [carve(80 + 4 * i, [128, S], BF16) for i in range(4)]
            Vh = [carve(96 + 4 * i, [128, NT, 128], BF16) for i in range(4)]
            Psb = [carve(112 + 4 * i, [128, S], BF16) for i in range(4)]
            PTg = [carve(128 + 16 * i, [128, 16, 512], BF16) for i in range(2)]
            tc = [0]

            def load_head(h):
                sl = h % 4
                dma("sync", Qh[sl][:, :], dram[scrQ][h], (), (("Qh", sl),), f"{ph}lq{h}")
                dma("sync", Kh[sl][:, :], dram[scrK][h], (), (("Kh", sl),), f"{ph}lk{h}")
                dma("sync", Vh[sl], dram[scrV][h], (), (("Vh", sl),), f"{ph}lv{h}")

            def sm_chain(h, qt):
                c = Chain()
                sl = h % 4
                hp2 = h % 2
                ctx = hp2 * 2 + (qt % 2)
                base = ctx * 16
                qsl = ctx
                ext = (qt + 1) * 128
                np_ = (ext + 511) // 512
                for j in range(np_):
                    w = min(512, ext - j * 512)
                    b = hp2 * 2 + (j % 2)
                    is_diag = (j == np_ - 1)
                    c.add("tensor", lambda tn, sl=sl, qt=qt, j=j, w=w, b=b: tn.matmul(
                        ps_mm[:, b * 512: b * 512 + w],
                        Qh[sl][:, qt * 128:(qt + 1) * 128],
                        Kh[sl][:, j * 512: j * 512 + w],
                        start=True, stop=True),
                      reads=(("Qh", sl), ("Kh", sl)), writes=(("psS", b),),
                      tag=f"{ph}s{h}q{qt}p{j}")
                    if is_diag:
                        c.add("vector", lambda v, b=b, w=w: v.tensor_tensor(
                            ps_mm[:, b * 512 + w - 128: b * 512 + w],
                            ps_mm[:, b * 512 + w - 128: b * 512 + w], maskc[:], op=ALU.add),
                          reads=(("psS", b), ("maskc",)), writes=(("psS", b),),
                          tag=f"{ph}mk{h}q{qt}")
                    c.add("scalar", lambda s, b=b, j=j, w=w, base=base, qsl=qsl:
                      s.activation(Psb[qsl][:, j * 512: j * 512 + w],
                                   ps_mm[:, b * 512: b * 512 + w], AF.Exp,
                                   bias=0.0, scale=SCALE,
                                   accum_out=stt[:, base + 8 + j: base + 9 + j]),
                      reads=(("psS", b),),
                      writes=(("Psb", qsl), ("att", base, "ps", j)),
                      tag=f"{ph}ex{h}q{qt}p{j}")
                psk = tuple(("att", base, "ps", j) for j in range(np_))
                if np_ == 1:
                    c.add("vector", lambda v, base=base: v.reciprocal(
                        stt[:, base + 14: base + 15], stt[:, base + 8: base + 9]),
                      reads=psk, writes=(("att", base, "rc"),), tag=f"{ph}rc{h}q{qt}")
                else:
                    c.add("vector", lambda v, base=base, np_=np_: v.reduce_sum(
                        stt[:, base + 13: base + 14], stt[:, base + 8: base + 8 + np_],
                        axis=AX.X),
                      reads=psk, writes=(("att", base, "tt"),), tag=f"{ph}ts{h}q{qt}")
                    c.add("vector", lambda v, base=base: v.reciprocal(
                        stt[:, base + 14: base + 15], stt[:, base + 13: base + 14]),
                      reads=(("att", base, "tt"),), writes=(("att", base, "rc"),),
                      tag=f"{ph}rc{h}q{qt}")
                for j in range(np_):
                    w = min(512, ext - j * 512)
                    c.add("vector", lambda v, qsl=qsl, j=j, w=w, base=base: v.tensor_scalar(
                        Psb[qsl][:, j * 512: j * 512 + w], Psb[qsl][:, j * 512: j * 512 + w],
                        stt[:, base + 14: base + 15], None, op0=ALU.mult),
                      reads=(("Psb", qsl), ("att", base, "rc")), writes=(("Psb", qsl),),
                      tag=f"{ph}pn{h}q{qt}p{j}")
                return c

            def transposes(h, qt):
                qsl = (h % 2) * 2 + (qt % 2)
                hp2 = h % 2
                for j0 in range(0, qt + 1, 8):
                    nb = min(8, qt + 1 - j0)
                    tb = tc[0] % 2
                    tc[0] += 1
                    tkey = ("ptT", tb)
                    for jj in range(nb):
                        kt = j0 + jj
                        P("tensor", lambda tn, qsl=qsl, kt=kt, jj=jj, tb=tb: tn.transpose(
                            ptb[:, tb * 1024 + jj * 128: tb * 1024 + (jj + 1) * 128],
                            Psb[qsl][:, kt * 128:(kt + 1) * 128], idn[:]),
                          reads=(("Psb", qsl), ("idn",)), writes=(tkey,),
                          tag=f"{ph}t{h}q{qt}j{kt}")
                    P("scalar", lambda s, hp2=hp2, qt=qt, j0=j0, nb=nb, tb=tb:
                      s.copy(
                        PTg[hp2][:, j0:j0 + nb, (qt % 4) * 128:(qt % 4) * 128 + 128],
                        ptb[:, tb * 1024: tb * 1024 + nb * 128].rearrange(
                            "p (a b) -> p a b", a=nb)),
                      reads=(tkey, ("PTgz", hp2)), writes=(("PTg", hp2, qt, j0),),
                      tag=f"{ph}pe{h}q{qt}j{j0}")

            for h in range(4):
                if h < NH:
                    load_head(h)
            for hp in range(0, NH, 2):
                if hp + 3 < NH:
                    load_head(hp + 2)
                    load_head(hp + 3)
                pair = (hp, hp + 1)
                for g in range(NG):
                    nkt = 4 * g + 4
                    for h in pair:
                        hp2 = h % 2
                        P("vector", lambda v, hp2=hp2, g=g: v.memset(
                            PTg[hp2].rearrange("p a b -> p (a b)")[:, 4 * g * 512:(4 * g + 4) * 512], 0.0),
                          reads=(), writes=(("PTgz", hp2),), tag=f"{ph}z{h}g{g}")
                    for qt in range(4 * g, 4 * g + 4):
                        emit_zip(sm_chain(pair[0], qt), sm_chain(pair[1], qt))
                        if qt > 4 * g:
                            for h in pair:
                                transposes(h, qt - 1)
                    for h in pair:
                        transposes(h, 4 * g + 3)
                    for h in pair:
                        hp2 = h % 2
                        sl = h % 4
                        okey = ("psO", hp2)
                        ptreads = tuple(("PTg", hp2, qt, j0) for qt in range(4 * g, 4 * g + 4)
                                        for j0 in range(0, qt + 1, 8)) + (("PTgz", hp2),)
                        for kt in range(nkt):
                            P("tensor", lambda tn, sl=sl, kt=kt, hp2=hp2,
                              first=(kt == 0), last=(kt == nkt - 1): tn.matmul(
                                ps_mm2[:, hp2 * 512:(hp2 + 1) * 512],
                                Vh[sl][:, kt, :], PTg[hp2][:, kt, :],
                                start=first, stop=last),
                              reads=(("Vh", sl),) + (ptreads if (kt == 0 or kt == nkt - 1) else ()),
                              writes=(okey,), tag=f"{ph}pv{h}g{g}k{kt}")
                        P("vector", lambda v, h=h, g=g, hp2=hp2: v.tensor_copy(
                            attn_fm[:, h, g * 512:(g + 1) * 512],
                            ps_mm2[:, hp2 * 512:(hp2 + 1) * 512]),
                          reads=(okey,), writes=((akey, h, g),), tag=f"{ph}oe{h}g{g}")

        attn1_fm = carve(0, [128, NT, S], BF16)
        if "B" in phases:
            attention_phase("sQ1", "sK1", "sV1", attn1_fm, "at1", False, "B")

        if "C" in phases:
            wo_hi = carve(160, [128, 4, D], BF16)
            load_weight_cast(wo_hi.rearrange("p a b -> p (a b)"), dram["w_o1"], 12 * 128, 512, 0, D,
                             ("wo", 1), "Cw1")

        prog.barrier()

        if "B" in phases and "XB" in taps:
            dma("sync", dram["sAT1"], attn1_fm, tuple(("at1", h, g) for h in range(NH) for g in range(NG)),
                (("sAT1w",),), "tapB")
            dma("sync", tap_aps["XB"], dram["sAT1"], (("sAT1w",),), (("XB_tap",),), "tapB2")

        if "C" in phases:
            wo_lo = carve(64, [128, 12, D], BF16)
            load_weight_cast(wo_lo.rearrange("p a b -> p (a b)"), dram["w_o1"], 0, 12 * 128, 0, D,
                             ("wo", 0), "Cw0")

        if "C" not in phases:
            prog.barrier()
            for tp in taps:
                if tp == "XB":
                    continue
                dma("sync", tap_aps[tp], dram["s" + tp], (), ((tp + "_tap",),), "tap" + tp)
            prog.barrier()
            return nc, finish(), len(prog.ops)
        x0t_c = [carve(112 + 8 * i, [128, D], F32) for i in range(2)]
        x1t = [carve(128 + 8 * i, [128, D], F32) for i in range(2)]
        yt = carve(144, [128, D], F32)
        sqc = carve(152, [128, D], F32)
        at1_all = tuple(("at1", h, g) for h in range(NH) for g in range(NG))

        for t in range(NT):
            par = t % 2
            xk = ("x0c", par)
            dma("sync", x0t_c[par], x0r[t], (), (xk,), f"Cx{t}")
            pk = ("psC", par)
            pst = ps_mm if par == 0 else ps_mm2
            korder = [12, 13, 14, 15] + list(range(12))
            for cg in range(4):
                for ki, k in enumerate(korder):
                    half = 1 if k >= 12 else 0
                    wbuf = wo_hi if half == 1 else wo_lo
                    P("tensor", lambda tn, t=t, cg=cg, k=k, pst=pst, wbuf=wbuf,
                      first=(ki == 0), last=(ki == NT - 1), ki=ki: tn.matmul(
                        pst[:, cg * 512:(cg + 1) * 512],
                        attn1_fm[:, k, t * 128:(t + 1) * 128],
                        wbuf[:, k % 12 if k < 12 else k - 12, cg * 512:(cg + 1) * 512],
                        start=first, stop=last),
                      reads=(("wo", half, (k % 12) if k < 12 else k - 12),) + (at1_all if (cg == 0 and ki == 0) else ()),
                      writes=(pk,), tag=f"Cm{t}c{cg}k{k}")
            col = 40 + par * 8
            ln_stats(pst[:, 0:D], (pk,), sqc, ("sqc",), col, D, tag=f"Cs{t}")
            P("vector", lambda v, pst=pst, col=col, par=par: v.tensor_scalar(
                x1t[par], pst[:, 0:D], stt[:, col:col + 1], stt[:, col + 1:col + 2],
                op0=ALU.subtract, op1=ALU.mult),
              reads=(pk, ("stt", col, "mean"), ("stt", col, "rstd")),
              writes=(("x1t", par),), tag=f"Ca{t}")
            P("vector", lambda v, par=par: v.tensor_tensor(
                x1t[par], x1t[par], x0t_c[par], op=ALU.add),
              reads=(("x1t", par), xk), writes=(("x1t", par),), tag=f"Cr{t}")
            dma("sync", dram["sX1"][t], x1t[par], (("x1t", par),), (("sX1", t),), f"Cox{t}")
            col2 = 56 + par * 4
            ln_stats(x1t[par], (("x1t", par),), sqc, ("sqc",), col2, D, tag=f"Cy{t}")
            P("vector", lambda v, par=par, col2=col2: v.tensor_scalar(
                yt, x1t[par], stt[:, col2:col2 + 1], stt[:, col2 + 1:col2 + 2],
                op0=ALU.subtract, op1=ALU.mult),
              reads=(("x1t", par), ("stt", col2, "mean"), ("stt", col2, "rstd")),
              writes=(("yt",),), tag=f"Cyap{t}")
            dma("sync", dram["sY"][t], yt, (("yt",),), (("sY", t),), f"Coy{t}")

        prog.barrier()

        if "D" in phases:
            dma("sync", cosb[:], dram["COSM"], (("cosb",),), (("cosb",),), "Dc")
            dma("sync", sinb[:], dram["SINM"], (("sinb",),), (("sinb",),), "Ds")
            r1_fm = carve(0, [128, NT, S], BF16)
            lnw1_bc = carve(64, [128, D], F32)
            ytD = [carve(72 + 8 * i, [128, D], F32) for i in range(2)]
            r1bf = [carve(88 + 4 * i, [128, D], BF16) for i in range(4)]
            sqD = carve(104, [128, D], F32)
            wkvd = carve(112, [128, NT, 512], BF16)
            wqd = carve(128, [128, NT, 512], BF16)
            kv_fm = carve(144, [128, 4, S], BF16)
            qd_fm = carve(160, [128, 4, S], BF16)

            dma("sync", lnw1_bc, dram["ln1_w"][0:1, :].partition_broadcast(128).squeeze(1),
                (), (("lnw1",),), "Dlnw")
            load_weight_cast(wkvd.rearrange("p a b -> p (a b)"), dram["kv_d_w"], 0, D, 0, 512,
                             ("wkvd",), "Dwk")
            load_weight_cast(wqd.rearrange("p a b -> p (a b)"), dram["q_d_w"], 0, D, 0, 512,
                             ("wqd",), "Dwq")

            for g in range(NG):
                for ti in range(4):
                    t = g * 4 + ti
                    sl = t % 2
                    yk = ("ytD", sl)
                    dma("sync", ytD[sl], dram["sY"][t], (), (yk,), f"Dy{t}")
                    col = sl * 8
                    rms_stats(ytD[sl], (yk,), sqD, ("sqD",), col, D, tag=f"Dr{t}")
                    P("vector", lambda v, sl=sl, col=col: v.tensor_scalar(
                        sqD, ytD[sl], stt[:, col + 1:col + 2], None, op0=ALU.mult),
                      reads=(yk, ("stt", col, "rstd")), writes=(("sqD",),), tag=f"Dm1{t}")
                    P("vector", lambda v, ti=ti: v.tensor_tensor(
                        r1bf[ti], sqD, lnw1_bc, op=ALU.mult),
                      reads=(("sqD",), ("lnw1",)), writes=(("r1bf", ti),), tag=f"Dm2{t}")
                for c in range(NT):
                    bank = c % 2
                    pk = ("ptA", bank)
                    for ti in range(4):
                        P("tensor", lambda tn, c=c, ti=ti, bank=bank: tn.transpose(
                            ptb[:, bank * 1024 + ti * 128: bank * 1024 + (ti + 1) * 128],
                            r1bf[ti][:, c * 128:(c + 1) * 128], idn[:]),
                          reads=(("r1bf", ti), ("idn",)), writes=(pk,), tag=f"Dt{g}c{c}")
                    P("vector", lambda v, c=c, g=g, bank=bank: v.tensor_copy(
                        r1_fm[:, c, g * 512:(g + 1) * 512],
                        ptb[:, bank * 1024: bank * 1024 + 512]),
                      reads=(pk,), writes=(("r1_fm", c, g),), tag=f"De{g}c{c}")

            r1_all = tuple(("r1_fm", c, g) for c in range(NT) for g in range(NG))

            uidD = 0
            for which, wbuf, wkey, dstb, dkey in (
                    (0, wkvd, ("wkvd",), kv_fm, "kv_fm"), (1, wqd, ("wqd",), qd_fm, "qd_fm")):
                for oc in range(4):
                    for g in range(NG):
                        bank = uidD % 4
                        pk = ("psA", bank)
                        for k in range(NT):
                            P("tensor", lambda tn, wbuf=wbuf, k=k, oc=oc, g=g, bank=bank,
                              first=(k == 0), last=(k == NT - 1): tn.matmul(
                                ps_mm[:, bank * 512:(bank + 1) * 512],
                                wbuf[:, k, oc * 128:(oc + 1) * 128],
                                r1_fm[:, k, g * 512:(g + 1) * 512],
                                start=first, stop=last),
                              reads=(wkey + (k,),) + (r1_all if k == 0 else ()),
                              writes=(pk,), tag=f"Dp{which}o{oc}g{g}k{k}")
                        P("scalar", lambda s, dstb=dstb, oc=oc, g=g, bank=bank: s.activation(
                            dstb[:, oc, g * 512:(g + 1) * 512],
                            ps_mm[:, bank * 512:(bank + 1) * 512],
                            AF.Identity, bias=0.0, scale=1.0),
                          reads=(pk,), writes=((dkey, oc, g),), tag=f"Dpe{which}o{oc}g{g}")
                        uidD += 1

            if "R1F" in taps:
                dma("sync", dram["sR1F"], r1_fm, r1_all, (("sR1Fw",),), "tapR1F")
                dma("sync", tap_aps["R1F"], dram["sR1F"], (("sR1Fw",),), (("R1F_tap",),), "tapR1F2")
            if "KVF" in taps:
                dma("sync", dram["sKVF"], kv_fm, kv_all0 if False else tuple(("kv_fm", oc, g) for oc in range(4) for g in range(NG)), (("sKVFw",),), "tapKVF")
                dma("sync", tap_aps["KVF"], dram["sKVF"], (("sKVFw",),), (("KVF_tap",),), "tapKVF2")

            prog.barrier()

            wrk = carve(64, [128, NT, 1024], BF16)
            qraw2 = [carve(104 + 1 * i, [128, 512], BF16) for i in range(2)]
            tmp1b = [carve(106 + 2 * i, [128, 512], F32) for i in range(2)]
            tmp2b = [carve(110 + 2 * i, [128, 512], F32) for i in range(2)]
            qout2 = [carve(114 + 1 * i, [128, 512], BF16) for i in range(2)]
            vout2 = [carve(116 + 1 * i, [128, 512], BF16) for i in range(2)]
            load_weight_cast(wrk.rearrange("p a b -> p (a b)"), dram["rope_k_w"], 0, D, 0, 1024,
                             ("wrk",), "D2wr")

            kv_all = tuple(("kv_fm", oc, g) for oc in range(4) for g in range(NG))
            qd_all = tuple(("qd_fm", oc, g) for oc in range(4) for g in range(NG))

            def mla_unit(uid, nk, mm_fn, wkeys, rhs_all, g, oc, do_rope, scr_name, row0, tag):
                c = Chain()
                sl = uid % 2
                bank = uid % 4
                pk = ("psA", bank)
                for k in range(nk):
                    c.add("tensor", lambda tn, k=k, bank=bank, first=(k == 0),
                          last=(k == nk - 1): mm_fn(tn, k, bank, first, last),
                          reads=(wkeys[k],) + (rhs_all if k == 0 else ()),
                          writes=(pk,), tag=tag + f"k{k}")
                c.add("scalar", lambda s, sl=sl, bank=bank: s.activation(
                    qraw2[sl], ps_mm[:, bank * 512:(bank + 1) * 512], AF.Identity,
                    bias=0.0, scale=1.0),
                  reads=(pk,), writes=(("qr2", sl),), tag=tag + "ev")
                qk = ("qr2", sl)
                if do_rope:
                    rk = ("psrot", sl)
                    c.add("tensor", lambda tn, sl=sl: tn.matmul(
                        ps_mm2[:, sl * 512:(sl + 1) * 512], rmm[:], qraw2[sl],
                        start=True, stop=True),
                      reads=(qk, ("rotM",)), writes=(rk,), tag=tag + "rot")
                    c.add("vector", lambda v, sl=sl, g=g: v.tensor_tensor(
                        tmp1b[sl], qraw2[sl], cosb[:, g * 512:(g + 1) * 512], op=ALU.mult),
                      reads=(qk, ("cosb",)), writes=(("t1b", sl),), tag=tag + "m1")
                    c.add("vector", lambda v, sl=sl, g=g: v.tensor_tensor(
                        tmp2b[sl], ps_mm2[:, sl * 512:(sl + 1) * 512],
                        sinb[:, g * 512:(g + 1) * 512], op=ALU.mult),
                      reads=(rk, ("sinb",)), writes=(("t2b", sl),), tag=tag + "m2")
                    c.add("vector", lambda v, sl=sl: v.tensor_tensor(
                        qout2[sl], tmp1b[sl], tmp2b[sl], op=ALU.add),
                      reads=(("t1b", sl), ("t2b", sl)), writes=(("qo2", sl),), tag=tag + "ad")
                    srcb, srckey = qout2[sl], ("qo2", sl)
                else:
                    srcb, srckey = qraw2[sl], qk
                for half in range(2):
                    h = 2 * oc + half
                    dstp = dram[scr_name][h][row0:row0 + 64, g * 512:(g + 1) * 512]
                    c.add("sync", lambda q, dstp=dstp, sb=srcb, half=half: q.dma_start(
                        out=dstp, in_=sb[half * 64:(half + 1) * 64, :]),
                      reads=(srckey,), writes=((scr_name, h, row0, g),),
                      tag=tag + f"st{half}", dma=True)
                return c

            def zip_pairs(units):
                for i in range(0, len(units) - 1, 2):
                    emit_zip(units[i], units[i + 1])
                if len(units) % 2:
                    emit_zip(units[-1])

            uid2 = 0
            units = []
            for oc in range(8):
                for g in range(NG):
                    def mk_mm(oc=oc, g=g):
                        def f(tn, k, bank, first, last):
                            return tn.matmul(
                                ps_mm[:, bank * 512:(bank + 1) * 512],
                                wrk[:, k, oc * 128:(oc + 1) * 128],
                                r1_fm[:, k, g * 512:(g + 1) * 512],
                                start=first, stop=last)
                        return f
                    units.append(mla_unit(uid2, NT, mk_mm(), [("wrk", k) for k in range(NT)],
                                          r1_all, g, oc, True, "sKM", 64, f"D2kr{oc}g{g}"))
                    uid2 += 1
            zip_pairs(units)

            prog.barrier()

            wku = carve(64, [128, 4, 1024], BF16)
            wqu = carve(72, [128, 4, 1024], BF16)
            wrq = carve(80, [128, 4, 1024], BF16)
            wvu = carve(88, [128, 4, S], BF16)
            load_weight_cast(wku.rearrange("p a b -> p (a b)"), dram["k_u_w"], 0, LAT, 0, 1024,
                             ("wku",), "D2wku")
            load_weight_cast(wqu.rearrange("p a b -> p (a b)"), dram["q_u_w"], 0, LAT, 0, 1024,
                             ("wqu",), "D2wqu")
            load_weight_cast(wrq.rearrange("p a b -> p (a b)"), dram["rope_q_w"], 0, LAT, 0, 1024,
                             ("wrq",), "D2wrq")
            load_weight_cast(wvu.rearrange("p a b -> p (a b)"), dram["v_u_w"], 0, LAT, 0, D,
                             ("wvu",), "D2wvu")

            units = []
            for which, wbuf, wkeyn, rhsb, rhsall, rope_f, scrn, row0 in (
                    ("kp", wku, "wku", kv_fm, kv_all, False, "sKM", 0),
                    ("qp", wqu, "wqu", qd_fm, qd_all, False, "sQM", 0),
                    ("qr", wrq, "wrq", qd_fm, qd_all, True, "sQM", 64)):
                for oc in range(8):
                    for g in range(NG):
                        def mk_mm(wbuf=wbuf, rhsb=rhsb, oc=oc, g=g):
                            def f(tn, k, bank, first, last):
                                return tn.matmul(
                                    ps_mm[:, bank * 512:(bank + 1) * 512],
                                    wbuf[:, k, oc * 128:(oc + 1) * 128],
                                    rhsb[:, k, g * 512:(g + 1) * 512],
                                    start=first, stop=last)
                            return f
                        units.append(mla_unit(uid2, 4, mk_mm(),
                                              [(wkeyn, k) for k in range(4)], rhsall,
                                              g, oc, rope_f, scrn, row0,
                                              f"D2{which}{oc}g{g}"))
                        uid2 += 1
            def v_unit(uid, cg, t):
                c = Chain()
                bank = uid % 4
                pk = ("psA", bank)
                for k in range(4):
                    c.add("tensor", lambda tn, k=k, t=t, cg=cg, bank=bank,
                          first=(k == 0), last=(k == 3): tn.matmul(
                        ps_mm[:, bank * 512:(bank + 1) * 512],
                        kv_fm[:, k, t * 128:(t + 1) * 128],
                        wvu[:, k, cg * 512:(cg + 1) * 512],
                        start=first, stop=last),
                      reads=(("wvu", k),) + (kv_all if k == 0 else ()),
                      writes=(pk,), tag=f"D2v{cg}t{t}k{k}")
                vsl = uid % 2
                c.add("scalar", lambda s, bank=bank, vsl=vsl: s.activation(
                    vout2[vsl], ps_mm[:, bank * 512:(bank + 1) * 512],
                    AF.Identity, bias=0.0, scale=1.0),
                  reads=(pk,), writes=(("vo2", vsl),), tag=f"D2ve{cg}t{t}")
                h0 = cg * 4
                dstp = dram["sVM"][h0:h0 + 4, :, t, :].rearrange("h p f -> p h f")
                srcp = vout2[vsl].rearrange("p (h f) -> p h f", h=4)
                c.add("sync", lambda q, dstp=dstp, srcp=srcp: q.dma_start(out=dstp, in_=srcp),
                      reads=(("vo2", vsl),), writes=(("sVM", t, h0),),
                      tag=f"D2vs{cg}t{t}", dma=True)
                return c

            for cg in range(4):
                for t in range(NT):
                    units.append(v_unit(uid2, cg, t))
                    uid2 += 1
            zip_pairs(units)

            prog.barrier()

        if "E" in phases:
            attn2_fm = carve(0, [128, NT, S], BF16)
            attention_phase("sQM", "sKM", "sVM", attn2_fm, "at2", False, "E")
            ow_hi = carve(160, [128, 4, D], BF16)
            load_weight_cast(ow_hi.rearrange("p a b -> p (a b)"), dram["o_w"], 12 * 128, 512,
                             0, D, ("ow", 1), "Fw1")
            prog.barrier()

        if "F" in phases:
            ow_lo = carve(64, [128, 12, D], BF16)
            load_weight_cast(ow_lo.rearrange("p a b -> p (a b)"), dram["o_w"], 0, 12 * 128,
                             0, D, ("ow", 0), "Fw0")
            ytF = carve(112, [128, D], F32)
            htF = carve(120, [128, D], F32)
            m_bf = [carve(128 + 4 * i, [128, D], BF16) for i in range(2)]
            mstage = [carve(136 + 2 * i, [128, 512], BF16) for i in range(2)]
            sqF = carve(140, [128, D], F32)
            lnw2_bc = carve(148, [128, D], F32)
            rw_sb = carve(156, [128, NT, 8], BF16)
            rbbc = carve(157, [128, 8], F32)
            at2_all = tuple(("at2", h, g) for h in range(NH) for g in range(NG))

            dma("sync", lnw2_bc, dram["ln2_w"][0:1, :].partition_broadcast(128).squeeze(1),
                (), (("lnw2",),), "Flnw")
            dma("sync", rbbc, dram["rb_pad"][0:1, :].partition_broadcast(128).squeeze(1),
                (), (("rbbc",),), "Frb")
            P("vector", lambda v: v.memset(rw_sb.rearrange("p a b -> p (a b)"), 0.0),
              reads=(), writes=(("rwz",),), tag="Frwz")
            for k in range(NT):
                dma("gpsimd", rw_sb[:, k, 0:7], dram["router_w"][k * 128:(k + 1) * 128, :],
                    (("rwz",),), (("rw", k),), f"Frw{k}")

            for t in range(NT):
                g, tt4 = t // 4, t % 4
                korder = [12, 13, 14, 15] + list(range(12))
                for cg in range(4):
                    for ki, k in enumerate(korder):
                        half = 1 if k >= 12 else 0
                        wbuf = ow_hi if half == 1 else ow_lo
                        P("tensor", lambda tn, t=t, cg=cg, k=k, wbuf=wbuf,
                          first=(ki == 0), last=(ki == NT - 1), ki=ki: tn.matmul(
                            ps_mm[:, cg * 512:(cg + 1) * 512],
                            attn2_fm[:, k, t * 128:(t + 1) * 128],
                            wbuf[:, k if k < 12 else k - 12, cg * 512:(cg + 1) * 512],
                            start=first, stop=last),
                          reads=(("ow", half, k if k < 12 else k - 12),) + (at2_all if (cg == 0 and ki == 0) else ()),
                          writes=(("psF", cg),), tag=f"Fm{t}c{cg}k{k}")
                dma("sync", ytF, dram["sY"][t], (), (("ytF",),), f"Fy{t}")
                for cg in range(4):
                    P("vector", lambda v, cg=cg: v.tensor_tensor(
                        htF[:, cg * 512:(cg + 1) * 512], ps_mm[:, cg * 512:(cg + 1) * 512],
                        ytF[:, cg * 512:(cg + 1) * 512], op=ALU.add),
                      reads=(("psF", cg), ("ytF",)), writes=(("htF", cg),), tag=f"Fh{t}c{cg}")
                hkeys = tuple(("htF", cg) for cg in range(4))
                dma("sync", dram["sH"][t], htF, hkeys, (("sHw", t),), f"Fho{t}")
                col = 16 + (t % 2) * 8
                rms_stats(htF, hkeys, sqF, ("sqF",), col, D, tag=f"Fr{t}")
                P("vector", lambda v, col=col: v.tensor_scalar(
                    sqF, htF, stt[:, col + 1:col + 2], None, op0=ALU.mult),
                  reads=hkeys + (("stt", col, "rstd"),), writes=(("sqF",),), tag=f"Fm1{t}")
                mpar = t % 2
                P("vector", lambda v, mpar=mpar: v.tensor_tensor(
                    m_bf[mpar], sqF, lnw2_bc, op=ALU.mult),
                  reads=(("sqF",), ("lnw2",)), writes=(("m_bf", mpar),), tag=f"Fm2{t}")
                for c4 in range(4):
                    tb = c4 % 2
                    tkey = ("ptT", tb)
                    for cc in range(4):
                        c = c4 * 4 + cc
                        P("tensor", lambda tn, mpar=mpar, c=c, cc=cc, tb=tb: tn.transpose(
                            ptb[:, tb * 1024 + cc * 128: tb * 1024 + (cc + 1) * 128],
                            m_bf[mpar][:, c * 128:(c + 1) * 128], idn[:]),
                          reads=(("m_bf", mpar), ("idn",)), writes=(tkey,), tag=f"Ft{t}c{c}")
                    msl = c4 % 2
                    P("vector", lambda v, msl=msl, tb=tb: v.tensor_copy(
                        mstage[msl], ptb[:, tb * 1024: tb * 1024 + 512]),
                      reads=(tkey,), writes=(("mst", msl),), tag=f"Fme{t}c{c4}")
                    dst = dram["sMF"][g][:, c4 * 4:(c4 + 1) * 4, tt4 * 128:(tt4 + 1) * 128]
                    src = mstage[msl].rearrange("p (a b) -> p a b", a=4)
                    dma("sync", dst, src, (("mst", msl),), (("sMF", t, c4),), f"Fms{t}c{c4}")
                    for cc in range(4):
                        c = c4 * 4 + cc
                        P("tensor", lambda tn, msl=msl, cc=cc, c=c,
                          first=(c == 0), last=(c == NT - 1): tn.matmul(
                            ps_mm2[:, 0:8], mstage[msl][:, cc * 128:(cc + 1) * 128],
                            rw_sb[:, c, :], start=first, stop=last),
                          reads=(("mst", msl), ("rw", c)), writes=(("psR",),), tag=f"Flg{t}c{c}")
                P("vector", lambda v: v.tensor_tensor(ps_mm2[:, 0:8], ps_mm2[:, 0:8],
                                                      rbbc, op=ALU.add),
                  reads=(("psR",), ("rbbc",)), writes=(("psR",),), tag=f"Fba{t}")
                P("scalar", lambda s: s.activation(probs8[:], ps_mm2[:, 0:8], AF.Sigmoid,
                                                   bias=0.0, scale=1.0),
                  reads=(("psR",),), writes=(("pr8",),), tag=f"Fsg{t}")
                P("vector", lambda v: v.reduce_max(stt[:, 48:49], probs8[:], axis=AX.X),
                  reads=(("pr8",),), writes=(("rm1",),), tag=f"Fx1{t}")
                P("vector", lambda v: v.tensor_scalar(scr8[:], probs8[:], stt[:, 48:49], None,
                                                      op0=ALU.is_equal),
                  reads=(("pr8",), ("rm1",)), writes=(("eq1",),), tag=f"Fe1{t}")
                P("vector", lambda v: v.tensor_scalar(scr8b[:], scr8[:], -1e9, None, op0=ALU.mult),
                  reads=(("eq1",),), writes=(("eq1b",),), tag=f"Fe2{t}")
                P("vector", lambda v: v.tensor_tensor(scr8b[:], probs8[:], scr8b[:], op=ALU.add),
                  reads=(("pr8",), ("eq1b",)), writes=(("msk",),), tag=f"Fe3{t}")
                P("vector", lambda v: v.reduce_max(stt[:, 49:50], scr8b[:], axis=AX.X),
                  reads=(("msk",),), writes=(("rm2",),), tag=f"Fx2{t}")
                P("vector", lambda v: v.tensor_scalar(scr8b[:], scr8b[:], stt[:, 49:50], None,
                                                      op0=ALU.is_equal),
                  reads=(("msk",), ("rm2",)), writes=(("eq2",),), tag=f"Fe4{t}")
                P("vector", lambda v: v.tensor_tensor(scr8[:], scr8[:], scr8b[:], op=ALU.add),
                  reads=(("eq1",), ("eq2",)), writes=(("sel2",),), tag=f"Fe5{t}")
                P("vector", lambda v: v.tensor_tensor(scr8[:], scr8[:], probs8[:], op=ALU.mult),
                  reads=(("sel2",), ("pr8",)), writes=(("wsel",),), tag=f"Fe6{t}")
                P("vector", lambda v: v.tensor_scalar(stt[:, 50:51], stt[:, 48:49],
                                                      stt[:, 49:50], None, op0=ALU.add),
                  reads=(("rm1",), ("rm2",)), writes=(("den",),), tag=f"Fe7{t}")
                P("vector", lambda v: v.reciprocal(stt[:, 51:52], stt[:, 50:51]),
                  reads=(("den",),), writes=(("rden",),), tag=f"Fe8{t}")
                P("vector", lambda v: v.tensor_scalar(scr8[:], scr8[:], stt[:, 51:52], None,
                                                      op0=ALU.mult),
                  reads=(("wsel",), ("rden",)), writes=(("wfin",),), tag=f"Fe9{t}")
                P("tensor", lambda tn: tn.transpose(ps_mm2[0:8, 512:640], scr8[:], idnf[:]),
                  reads=(("wfin",), ("idnf",)), writes=(("psW",),), tag=f"Fwt{t}")
                P("vector", lambda v, t=t: v.tensor_copy(wT[0:8, t * 128:(t + 1) * 128],
                                                         ps_mm2[0:8, 512:640]),
                  reads=(("psW",),), writes=(("wT", t),), tag=f"Fwc{t}")

            prog.barrier()

        if "G" in phases:
            m_g = [carve(0 + 16 * i, [128, NT, 512], BF16) for i in range(2)]
            wg_b = carve(32, [128, NT, 1024], BF16)
            wu_b = carve(64, [128, NT, 1024], BF16)
            wd_b = carve(96, [128, 8, D], BF16)
            tE = carve(128, [128, 8, 512], BF16)
            tEf = carve(128, [128, D], F32)
            f_sb = carve(136, [128, 4, D], F32)
            gsb = [carve(168 + 2 * i, [128, 512], F32) for i in range(2)]
            wbc = [carve(172 + 2 * i, [128, 512], F32) for i in range(2)]

            def g_load_expert(e, tag):
                if e == 0:
                    wg_d, wu_d, wd_d = dram["sh_gate"], dram["sh_up"], dram["sh_down"]
                else:
                    wg_d, wu_d, wd_d = (dram["r_gate"][e - 1], dram["r_up"][e - 1],
                                        dram["r_down"][e - 1])
                load_weight_cast(wg_b.rearrange("p a b -> p (a b)"), wg_d, 0, D, 0, MOE_I,
                                 ("wgB",), tag + "g")
                load_weight_cast(wu_b.rearrange("p a b -> p (a b)"), wu_d, 0, D, 0, MOE_I,
                                 ("wuB",), tag + "u")
                load_weight_cast(wd_b.rearrange("p a b -> p (a b)"), wd_d, 0, MOE_I, 0, D,
                                 ("wdB",), tag + "d")

            dma("sync", m_g[0].rearrange("p a b -> p (a b)"),
                dram["sMF"][0].rearrange("p a b -> p (a b)"), (), (("m_g", 0),), "Gm0")
            for g in range(NG):
                gp = g % 2
                mk = ("m_g", gp)
                for ti in range(4):
                    t = g * 4 + ti
                    dma("sync", f_sb[:, ti, :], dram["sH"][t], (("fS", ti),),
                        (("fS", ti),), f"Gh{g}t{ti}")
                order = list(range(8)) if g % 2 == 0 else list(range(7, -1, -1))
                for idx, e in enumerate(order):
                    tg = f"Ge{g}e{e}"
                    if not (g > 0 and idx == 0):
                        g_load_expert(e, tg)
                    if idx == 7 and g + 1 < NG:
                        gp2 = (g + 1) % 2
                        dma("sync", m_g[gp2].rearrange("p a b -> p (a b)"),
                            dram["sMF"][g + 1].rearrange("p a b -> p (a b)"), (),
                            (("m_g", gp2),), f"Gm{g + 1}")
                    if e > 0:
                        P("tensor", lambda tn, e=e, g=g: tn.matmul(
                            ps_mm[:, 0:512], sel8[:, (e - 1) * 128: e * 128],
                            wT[:, g * 512:(g + 1) * 512], start=True, stop=True),
                          reads=(("sel8",), ("wT", 4 * g), ("wT", 4 * g + 1),
                                 ("wT", 4 * g + 2), ("wT", 4 * g + 3)),
                          writes=(("psGg", 0),), tag=tg + "wb")
                        P("vector", lambda v, gp=gp: v.tensor_copy(wbc[gp], ps_mm[:, 0:512]),
                          reads=(("psGg", 0),), writes=(("wbc", gp),), tag=tg + "wc")
                    for ic in range(8):
                        bg = ic % 2
                        bu = 2 + ic % 2
                        for k in range(NT):
                            P("tensor", lambda tn, k=k, ic=ic, gp=gp, bg=bg,
                              first=(k == 0), last=(k == NT - 1): tn.matmul(
                                ps_mm[:, bg * 512:(bg + 1) * 512],
                                wg_b[:, k, ic * 128:(ic + 1) * 128],
                                m_g[gp][:, k, :], start=first, stop=last),
                              reads=(("wgB", k), mk), writes=(("psGg", bg),), tag=tg + f"g{ic}k{k}")
                        for k in range(NT):
                            P("tensor", lambda tn, k=k, ic=ic, gp=gp, bu=bu,
                              first=(k == 0), last=(k == NT - 1): tn.matmul(
                                ps_mm[:, bu * 512:(bu + 1) * 512],
                                wu_b[:, k, ic * 128:(ic + 1) * 128],
                                m_g[gp][:, k, :], start=first, stop=last),
                              reads=(("wuB", k), mk), writes=(("psGu", bu),), tag=tg + f"u{ic}k{k}")
                        gp2 = ic % 2
                        P("scalar", lambda s, gp2=gp2, bg=bg: s.activation(
                            gsb[gp2], ps_mm[:, bg * 512:(bg + 1) * 512], AF.Silu,
                            bias=0.0, scale=1.0),
                          reads=(("psGg", bg),), writes=(("gsb", gp2),), tag=tg + f"si{ic}")
                        P("vector", lambda v, gp2=gp2, bu=bu: v.tensor_tensor(
                            gsb[gp2], gsb[gp2], ps_mm[:, bu * 512:(bu + 1) * 512],
                            op=ALU.mult),
                          reads=(("gsb", gp2), ("psGu", bu)), writes=(("gsb", gp2),),
                          tag=tg + f"gu{ic}")
                        if e > 0:
                            P("vector", lambda v, gp2=gp2, ic=ic, gp=gp: v.tensor_tensor(
                                tE[:, ic, :], gsb[gp2], wbc[gp], op=ALU.mult),
                              reads=(("gsb", gp2), ("wbc", gp)), writes=(("tE", ic),),
                              tag=tg + f"tw{ic}")
                        else:
                            P("vector", lambda v, gp2=gp2, ic=ic: v.tensor_scalar(
                                tE[:, ic, :], gsb[gp2], 0.125, None, op0=ALU.mult),
                              reads=(("gsb", gp2),), writes=(("tE", ic),), tag=tg + f"ts{ic}")
                    tE_all = tuple(("tE", ic) for ic in range(8))
                    for ti in range(4):
                        for cg in range(4):
                            for ic in range(8):
                                P("tensor", lambda tn, ti=ti, cg=cg, ic=ic,
                                  first=(ic == 0), last=(ic == 7): tn.matmul(
                                    ps_mm2[:, cg * 512:(cg + 1) * 512],
                                    tE[:, ic, ti * 128:(ti + 1) * 128],
                                    wd_b[:, ic, cg * 512:(cg + 1) * 512],
                                    start=first, stop=last),
                                  reads=(("wdB", ic), ("tE", ic)),
                                  writes=(("psGd", cg),), tag=tg + f"d{ti}c{cg}i{ic}")
                            P("vector", lambda v, ti=ti, cg=cg: v.tensor_tensor(
                                f_sb[:, ti, cg * 512:(cg + 1) * 512],
                                f_sb[:, ti, cg * 512:(cg + 1) * 512],
                                ps_mm2[:, cg * 512:(cg + 1) * 512], op=ALU.add),
                              reads=(("psGd", cg), ("fS", ti)), writes=(("fS", ti),),
                              tag=tg + f"fa{ti}c{cg}")
                for ti in range(4):
                    t = g * 4 + ti
                    col = 32 + (ti % 2) * 8
                    ln_stats(f_sb[:, ti, :], (("fS", ti),), tEf,
                             tuple(("tE", ic) for ic in range(8)), col, D, tag=f"Gl{t}")
                    P("vector", lambda v, ti=ti, col=col: v.tensor_scalar(
                        tEf, f_sb[:, ti, :], stt[:, col:col + 1], stt[:, col + 1:col + 2],
                        op0=ALU.subtract, op1=ALU.mult),
                      reads=(("fS", ti), ("stt", col, "mean"), ("stt", col, "rstd")),
                      writes=tuple(("tE", ic) for ic in range(8)), tag=f"Ga{t}")
                    dma("sync", f_sb[:, ti, :], dram["sX1"][t],
                        (("fS", ti),), (("fS", ti),), f"Gx1{t}")
                    P("vector", lambda v, ti=ti: v.tensor_tensor(
                        tEf, tEf, f_sb[:, ti, :], op=ALU.add),
                      reads=(("tE", 0), ("fS", ti)), writes=tuple(("tE", ic) for ic in range(8)),
                      tag=f"Gx2{t}")
                    dma("sync", dram["out"][t * 128:(t + 1) * 128, :], tEf,
                        (("tE", 0),), (("outw", t),), f"Go{t}")

        prog.barrier()

        dma("sync", dram["chk"], dram["sX1"][0], (), (("chk",),), "chk")

        for tp in taps:
            if tp == "XB":
                continue
            dma("sync", tap_aps[tp], dram["s" + tp], (), ((tp + "_tap",),), "tap" + tp)

        prog.barrier()

        counts = finish()

    return nc, counts, len(prog.ops)


def make_inmaps(inputs):
    c = host_consts()
    x = np.ascontiguousarray(np.asarray(inputs["L0_attn_x"]), dtype=np.float32)
    base = {
        "w_qkv": np.ascontiguousarray(np.asarray(inputs["L0_attn_w_qkv"], np.float32)),
        "w_o1": np.ascontiguousarray(np.asarray(inputs["L0_attn_w_o"], np.float32)),
        "ln1_w": np.ascontiguousarray(np.asarray(inputs["L0_ffn_ln1_w"], np.float32)).reshape(1, D),
        "ln2_w": np.ascontiguousarray(np.asarray(inputs["L0_ffn_ln2_w"], np.float32)).reshape(1, D),
        "kv_d_w": np.ascontiguousarray(np.asarray(inputs["L0_ffn_kv_d_w"], np.float32)),
        "q_d_w": np.ascontiguousarray(np.asarray(inputs["L0_ffn_q_d_w"], np.float32)),
        "k_u_w": np.ascontiguousarray(np.asarray(inputs["L0_ffn_k_u_w"], np.float32)),
        "v_u_w": np.ascontiguousarray(np.asarray(inputs["L0_ffn_v_u_w"], np.float32)),
        "q_u_w": np.ascontiguousarray(np.asarray(inputs["L0_ffn_q_u_w"], np.float32)),
        "rope_k_w": np.ascontiguousarray(np.asarray(inputs["L0_ffn_rope_k_w"], np.float32)),
        "rope_q_w": np.ascontiguousarray(np.asarray(inputs["L0_ffn_rope_q_w"], np.float32)),
        "o_w": np.ascontiguousarray(np.asarray(inputs["L0_ffn_o_w"], np.float32)),
        "sh_gate": np.ascontiguousarray(np.asarray(inputs["L0_ffn_sh_gate"], np.float32)),
        "sh_up": np.ascontiguousarray(np.asarray(inputs["L0_ffn_sh_up"], np.float32)),
        "sh_down": np.ascontiguousarray(np.asarray(inputs["L0_ffn_sh_down"], np.float32)),
        "r_gate": np.ascontiguousarray(np.asarray(inputs["L0_ffn_r_gate"], np.float32)),
        "r_up": np.ascontiguousarray(np.asarray(inputs["L0_ffn_r_up"], np.float32)),
        "r_down": np.ascontiguousarray(np.asarray(inputs["L0_ffn_r_down"], np.float32)),
        "router_w": np.ascontiguousarray(np.asarray(inputs["L0_ffn_router_w"], np.float32)),
        "rb_pad": np.concatenate([np.asarray(inputs["L0_ffn_router_b"], np.float32).ravel(),
                                  np.float32([-1e9])]).reshape(1, 8),
    }
    base.update(c)
    in_maps = []
    for core in range(8):
        m = dict(base)
        m["x0"] = np.ascontiguousarray(x[core])
        in_maps.append(m)
    return in_maps


def kernel(**inputs):
    nc, _, _ = build_nc(taps=())
    in_maps = make_inmaps(inputs)
    res = run_bass_kernel_spmd(nc, in_maps, core_ids=list(range(8)))
    out = np.stack([np.asarray(res.results[i]["out"], np.float32) for i in range(8)], axis=0)
    return out
